# version 02
# baseline (speedup 1.0000x reference)
import numpy as np
import concourse.bass as bass
import concourse.mybir as mybir
from concourse.bass_utils import run_bass_kernel_spmd

F32 = mybir.dt.float32
BF16 = mybir.dt.bfloat16
AF = mybir.ActivationFunctionType
ALU = mybir.AluOpType
AX = mybir.AxisListType

DEBUG_X1 = False
DEBUG_XN = False
DEBUG_Y = False
C = 1024
NCH = 8
FF = 4096
NFF = 32
T = 512
NS = 4
U = 256
NU = 8
TLOC = 2048
EPS = 1e-5


class Sem:
    def __init__(self, h):
        self.h = h
        self.n = 0


class Arena:
    def __init__(self, nc, size):
        self.nc = nc
        self.size = size
        self.ctx = nc.sbuf_tensor("arena", [128, size], mybir.dt.uint8)
        self.big = self.ctx.__enter__()
        self.base = nc.lookup_mloc(self.big).addr
        self.cur = 0

    def alloc(self, name, shape, dtype, at=None):
        per = int(np.prod(shape[1:])) * mybir.dt.size(dtype)
        off = self.cur if at is None else at
        off = (off + 63) & ~63
        assert off + per <= self.size, (name, off, per, self.size)
        h = self.nc.virtual_sbuf_tensor(name, shape, dtype)
        self.nc._tensor(h.name, h.shape, h.dtype, type="SB")
        m = self.nc.lookup_mloc(h)
        m.allocated = True
        m.addr = self.base + off
        if at is None:
            self.cur = off + per
        return h, off


class Build:
    ENGS = ("sync", "gpsimd", "scalar", "vector", "tensor")

    def __init__(self):
        nc = bass.Bass("TRN2", target_bir_lowering=False, debug=False)
        self.nc = nc
        self.prog = {e: [] for e in self.ENGS}
        d = nc.dram_tensor
        self.xm = d("xm", [C, TLOC + 1], F32, kind="ExternalInput").ap()
        self.xp = d("xp", [C, TLOC + 1], F32, kind="ExternalInput").ap()
        self.msk = d("msk", [1, 1], F32, kind="ExternalInput").ap()
        self.wk = d("wk", [C, C], F32, kind="ExternalInput").ap()
        self.wv = d("wv", [C, C], F32, kind="ExternalInput").ap()
        self.wr = d("wr", [C, C], F32, kind="ExternalInput").ap()
        self.wo = d("wo", [C, C], F32, kind="ExternalInput").ap()
        self.w1 = d("w1", [C, FF], F32, kind="ExternalInput").ap()
        self.w2 = d("w2", [FF, C], F32, kind="ExternalInput").ap()
        self.tmkd = d("tmkd", [128, NCH], F32, kind="ExternalInput").ap()
        self.tmvd = d("tmvd", [128, NCH], F32, kind="ExternalInput").ap()
        self.tmrd = d("tmrd", [128, NCH], F32, kind="ExternalInput").ap()
        self.tdd = d("tdd", [128, NCH], F32, kind="ExternalInput").ap()
        self.tfd = d("tfd", [128, NCH], F32, kind="ExternalInput").ap()
        self.b1d = d("b1d", [128, NFF], F32, kind="ExternalInput").ap()
        self.b2d = d("b2d", [128, NCH], F32, kind="ExternalInput").ap()
        self.outT = d("outT", [C, TLOC], F32, kind="ExternalOutput").ap()
        self.x1d = d("x1d", [C, TLOC], F32,
                     kind="ExternalOutput" if DEBUG_X1 else "Internal").ap()
        self.xnd = (d("xnd", [C, T + 1], F32, kind="ExternalOutput").ap()
                    if DEBUG_XN else None)
        self.ydbg = (d("ydbg", [C, T], F32, kind="ExternalOutput").ap()
                     if DEBUG_Y else None)
        self.kdbg = (d("kdbg", [C, T], F32, kind="ExternalOutput").ap()
                     if DEBUG_Y else None)
        self.rkdbg = (d("rkdbg", [C, T], F32, kind="ExternalOutput").ap()
                      if DEBUG_Y else None)
        self.sgdbg = (d("sgdbg", [C, T], F32, kind="ExternalOutput").ap()
                      if DEBUG_Y else None)
        self.w1td = d("w1td", [C, FF], BF16, kind="Internal").ap()
        self.w2td = d("w2td", [FF, C], BF16, kind="Internal").ap()

        self.qs = Sem(nc.alloc_semaphore("qs"))
        self.qg = Sem(nc.alloc_semaphore("qg"))
        self.pe = Sem(nc.alloc_semaphore("pe"))
        self.act = Sem(nc.alloc_semaphore("act"))
        self.dve = Sem(nc.alloc_semaphore("dve"))
        self.pl = Sem(nc.alloc_semaphore("pl"))

        self._alloc_sbuf()
        p = nc.alloc_psum_tensor
        self.psA = p("psA", [128, T], F32)
        self.psB = p("psB", [128, T], F32)
        self.psC = p("psC", [128, T], F32)
        self.psD = p("psD", [128, T], F32)
        self.psE = p("psE", [128, T], F32)
        self.psF = p("psF", [128, T], F32)
        self.psG = p("psG", [128, T], F32)
        self.psH = p("psH", [128, T], F32)

        self.rot3 = [None, None, None]
        self.gidx = 0
        self.stage_guard = [None, None]
        self.w1c_guard = [None, None]
        self.tern_slot = 0
        self.xsg = {0: [], 1: []}
        self.x1wr_slot = {}
        self.gh_guard = None
        self.xn_guard = []
        self.mix_guard = None
        self.xnlast_src = None
        self.rb_guard = None

    def _alloc_sbuf(self):
        a = Arena(self.nc, 200 * 1024)
        self.ar = a

        def al(name, shape, dtype, at=None):
            h, off = a.alloc(name, shape, dtype, at)
            setattr(self, name, h)
            return off
        al("tmk", [128, NCH], F32); al("tmv", [128, NCH], F32)
        al("tmr", [128, NCH], F32); al("tdr", [128, NCH], F32)
        al("tfr", [128, NCH], F32); al("alpha", [128, NCH], F32)
        al("eu", [128, NCH], F32); al("b1v", [128, NFF], F32)
        al("b2v", [128, NCH], F32); al("onesb", [128, 1], BF16)
        al("onesf", [1, 128], F32); al("onescol", [128, 1], F32)
        al("msksb", [1, 1], F32); al("mkcol", [128, 1], F32)
        al("mcol", [128, 1], F32); al("rcol", [128, 1], F32)
        al("thr1", [128, 1], F32); al("nthr1", [128, 1], F32)
        al("thr2", [128, 1], F32); al("nthr2", [128, 1], F32)
        al("trow", [1, 1], F32); al("cN", [128, NCH], F32)
        al("cD", [128, NCH], F32)
        al("xpcP", [128, NCH, 1], F32); al("xpcM", [128, NCH, 1], F32)
        al("xpcB", [128, NCH, 1], BF16)
        al("colsA", [128, 64], F32); al("colsB", [128, 64], F32)
        al("colred", [128, 1], F32); al("tiny", [1, 8], F32)
        al("xnlast", [128, NCH, 1], F32)
        al("rows", [1, 4, T], F32); al("mrrow", [1, T], F32)
        al("Rb", [128, T], F32); al("MRb", [128, T], F32)
        al("stg", [128, 2, T], F32); al("mn", [128, 2, T], F32)
        al("w1c", [128, 2, T], BF16)
        wbase = a.cur
        al("wkb", [128, NCH, C], BF16); al("wvb", [128, NCH, C], BF16)
        al("wrb", [128, NCH, C], BF16); al("wob", [128, NCH, C], BF16)
        al("w1tb", [128, NCH, FF], BF16, at=wbase)
        sbase = a.cur
        al("xs", [128, NCH, 2, T], F32)
        xnoff = al("xn", [128, NCH, T + 1], F32)
        xyoff = al("xkb", [128, NCH, T], BF16)
        al("xvb", [128, NCH, T], BF16)
        al("xrb", [128, NCH, T], BF16)
        al("arot", [128, 2, T], F32); al("avrot", [128, 2, T], F32)
        al("N1", [128, T + 1], F32); al("D1", [128, T + 1], F32)
        al("xbr", [128, 4, T], BF16); al("xsqr", [128, 4, T], BF16)
        al("dd", [128, T], F32); al("numt", [128, T], F32)
        al("dent", [128, T], F32)
        al("sg", [128, 2, T], F32); al("x1c", [128, 2, T], F32)
        a_end = a.cur
        al("yt", [128, NCH, T], F32, at=xnoff)
        al("rwkvb", [128, NCH, T], BF16, at=xyoff)
        al("w2tb", [128, NFF, C], BF16, at=sbase)
        o = sbase + NFF * C * 2
        o = al("x1s", [128, NCH, 2, U], F32, at=o) + NCH * 2 * U * 4
        o = al("xn2b", [128, NCH, U], BF16, at=o) + NCH * U * 2
        o = al("hgb", [128, NFF, U], BF16, at=o) + NFF * U * 2
        o = al("xb2", [128, 2, U], BF16, at=o) + 2 * U * 2
        o = al("xsq2", [128, 2, U], BF16, at=o) + 2 * U * 2
        o = al("xn2f", [128, U], F32, at=o) + U * 4
        o = al("outc", [128, 2, U], F32, at=o) + 2 * U * 4
        assert max(a_end, o) <= 200 * 1024, (a_end, o)

    def emit(self, e, fn):
        self.prog[e].append(fn)

    def wait(self, e, s, val=None):
        v = s.n if val is None else val
        h = s.h
        if v > 0:
            self.emit(e, lambda eng, h=h, v=v: eng.wait_ge(h, v))

    def waitg(self, e, g):
        if g:
            self.wait(e, g[0], g[1])

    def inc(self, e, fn, s, k=1):
        s.n += k
        h = s.h
        self.emit(e, lambda eng, fn=fn, h=h, k=k: fn(eng).then_inc(h, k))
        return s.n

    def op(self, e, fn):
        self.emit(e, lambda eng, fn=fn: fn(eng))

    def dma(self, e, out, in_, s, slow=False):
        if slow:
            return self.inc(e, lambda eng, o=out, i=in_: eng.dma_start(
                out=o, in_=i, allow_slow_non_contiguous=True), s, 16)
        return self.inc(e, lambda eng, o=out, i=in_: eng.dma_start(out=o, in_=i), s, 16)

    def mmgroup(self, ps_ap, pairs):
        n = len(pairs)
        for i, (lhsT, rhs) in enumerate(pairs):
            st, sp = (i == 0), (i == n - 1)
            if sp:
                self.inc("tensor", lambda eng, o=ps_ap, l=lhsT, r=rhs, st=st, sp=sp:
                         eng.matmul(o, l, r, start=st, stop=sp), self.pe)
            else:
                self.op("tensor", lambda eng, o=ps_ap, l=lhsT, r=rhs, st=st, sp=sp:
                        eng.matmul(o, l, r, start=st, stop=sp))
        return self.pe.n

    def init_phase(self):
        qs, qg = self.qs, self.qg
        for dst, src in ((self.tmk, self.tmkd), (self.tmv, self.tmvd),
                         (self.tmr, self.tmrd), (self.tdr, self.tdd),
                         (self.tfr, self.tfd), (self.b1v, self.b1d),
                         (self.b2v, self.b2d), (self.msksb, self.msk)):
            self.dma("sync", dst[:, :], src, qs)
        self.dma("sync", self.xpcP[:, :, :],
                 self.xp[:, 0:1].rearrange("(g p) t -> p g t", p=128), qs, slow=True)
        self.dma("sync", self.xpcM[:, :, :],
                 self.xm[:, 0:1].rearrange("(g p) t -> p g t", p=128), qs, slow=True)
        self.vec_loads = qs.n
        for wdst, wsrc in ((self.wkb, self.wk), (self.wvb, self.wv),
                           (self.wrb, self.wr), (self.wob, self.wo)):
            for ci in range(NCH):
                self.dma("gpsimd", wdst[:, ci, :], wsrc[ci * 128:(ci + 1) * 128, :], qg)
        self.w_loaded = qg.n
        self.op("vector", lambda e: e.memset(self.onesb[:, :], 1.0))
        self.op("vector", lambda e: e.memset(self.onesf[:, :], 1.0))
        self.op("vector", lambda e: e.memset(self.onescol[:, :], 1.0))
        self.op("vector", lambda e: e.memset(self.cN[:, :], 0.0))
        self.inc("vector", lambda e: e.memset(self.cD[:, :], 0.0), self.dve)
        self.memset_done = self.dve.n
        self.wait("scalar", qs, self.vec_loads)
        self.op("scalar", lambda e: e.activation(out=self.alpha[:, :], in_=self.tdr[:, :], func=AF.Exp))
        self.op("scalar", lambda e: e.activation(out=self.alpha[:, :], in_=self.alpha[:, :], func=AF.Exp, scale=-1.0))
        self.inc("scalar", lambda e: e.activation(out=self.eu[:, :], in_=self.tfr[:, :], func=AF.Exp), self.act)
        self.wait("tensor", self.dve, self.memset_done)
        self.wait("tensor", qs, self.vec_loads)
        mk_mm = self.mmgroup(self.psE[:, 0:1], [(self.onesf[:, :], self.msksb[:, :])])
        self.wait("scalar", self.pe, mk_mm)
        self.inc("scalar", lambda e: e.activation(out=self.mkcol[:, :], in_=self.psE[:, 0:1], func=AF.Copy), self.act)
        self.mk_done = self.act.n

    def tiny_stats(self, xpc, wait_q=None):
        if wait_q is not None:
            self.wait("scalar", self.qs, wait_q)
        self.inc("scalar", lambda e: e.activation(out=self.xpcB[:, :, :], in_=xpc[:, :, :], func=AF.Copy), self.act)
        self.wait("tensor", self.act, self.act.n)
        mmx = self.mmgroup(self.psF[0:1, 0:1],
                           [(self.onesb[:, :], self.xpcB[:, j, :]) for j in range(NCH)])
        self.wait("scalar", self.pe, mmx)
        self.inc("scalar", lambda e: e.activation(out=self.xpcB[:, :, :], in_=xpc[:, :, :], func=AF.Square), self.act)
        self.wait("tensor", self.act, self.act.n)
        self.mmgroup(self.psD[0:1, 0:1],
                     [(self.onesb[:, :], self.xpcB[:, j, :]) for j in range(NCH)])
        self.wait("vector", self.pe, self.pe.n)
        ti = self.tiny
        self.op("vector", lambda e: e.tensor_single_scalar(out=ti[:, 0:1], in_=self.psF[0:1, 0:1], scalar=1.0 / C, op=ALU.mult))
        self.op("vector", lambda e: e.tensor_single_scalar(out=ti[:, 1:2], in_=self.psD[0:1, 0:1], scalar=1.0 / C, op=ALU.mult))
        self.op("vector", lambda e: e.tensor_tensor(out=ti[:, 2:3], in0=ti[:, 0:1], in1=ti[:, 0:1], op=ALU.mult))
        self.op("vector", lambda e: e.tensor_tensor(out=ti[:, 1:2], in0=ti[:, 1:2], in1=ti[:, 2:3], op=ALU.subtract))
        self.inc("vector", lambda e: e.tensor_single_scalar(out=ti[:, 1:2], in_=ti[:, 1:2], scalar=EPS, op=ALU.add), self.dve)
        self.wait("scalar", self.dve, self.dve.n)
        self.op("scalar", lambda e: e.activation(out=ti[:, 2:3], in_=ti[:, 1:2], func=AF.Ln))
        self.inc("scalar", lambda e: e.activation(out=ti[:, 3:4], in_=ti[:, 2:3], func=AF.Exp, scale=-0.5), self.act)
        self.wait("tensor", self.act, self.act.n)
        self.mmgroup(self.psF[:, 0:1], [(self.onesf[:, :], ti[:, 0:1])])
        mm3 = self.mmgroup(self.psD[:, 0:1], [(self.onesf[:, :], ti[:, 3:4])])
        self.wait("scalar", self.pe, mm3)
        self.op("scalar", lambda e: e.activation(out=self.mcol[:, :], in_=self.psF[:, 0:1], func=AF.Copy))
        self.inc("scalar", lambda e: e.activation(out=self.rcol[:, :], in_=self.psD[:, 0:1], func=AF.Copy), self.act)
        self.wait("vector", self.act, self.act.n)
        for j in range(NCH):
            fn = lambda e, j=j: e.scalar_tensor_tensor(
                out=self.xn[:, j, 0:1], in0=xpc[:, j, :], scalar=self.mcol[:, 0:1],
                in1=self.rcol[:, 0:1], op0=ALU.subtract, op1=ALU.mult)
            if j == NCH - 1:
                self.inc("vector", fn, self.dve)
            else:
                self.op("vector", fn)

    def strip(self, s, src, prefix):
        buf = s % 2
        c0 = 1 + s * T
        qs = self.qs
        for g in self.xsg[buf]:
            self.waitg("sync", g)
        xin = src.rearrange("(g p) t -> p g t", p=128)[:, :, c0:c0 + T]
        xl = self.dma("sync", self.xs[:, :, buf, :], xin, qs)
        self.wait("scalar", qs, xl)
        stats_mm = []
        for j in range(NCH):
            if j >= 4:
                self.wait("scalar", self.pe, stats_mm[j - 4])
            self.op("scalar", lambda e, j=j, buf=buf: e.activation(
                out=self.xbr[:, j % 4, :], in_=self.xs[:, j, buf, :], func=AF.Copy))
            self.inc("scalar", lambda e, j=j, buf=buf: e.activation(
                out=self.xsqr[:, j % 4, :], in_=self.xs[:, j, buf, :], func=AF.Square), self.act)
            self.wait("tensor", self.act, self.act.n)
            if j == 0:
                self.waitg("tensor", self.gh_guard)
            stG, spG = (j == 0), (j == NCH - 1)
            self.op("tensor", lambda e, j=j, st=stG, sp=spG: e.matmul(
                self.psG[0:1, :], self.onesb[:, :], self.xbr[:, j % 4, :], start=st, stop=sp))
            self.inc("tensor", lambda e, j=j, st=stG, sp=spG: e.matmul(
                self.psH[0:1, :], self.onesb[:, :], self.xsqr[:, j % 4, :], start=st, stop=sp), self.pe)
            stats_mm.append(self.pe.n)
        stats_done = self.pe.n
        self.wait("vector", self.pe, stats_done)
        rw = self.rows
        self.op("vector", lambda e: e.tensor_single_scalar(out=rw[:, 0, :], in_=self.psG[0:1, :], scalar=1.0 / C, op=ALU.mult))
        self.op("vector", lambda e: e.tensor_single_scalar(out=rw[:, 1, :], in_=self.psH[0:1, :], scalar=1.0 / C, op=ALU.mult))
        self.op("vector", lambda e: e.tensor_tensor(out=rw[:, 2, :], in0=rw[:, 0, :], in1=rw[:, 0, :], op=ALU.mult))
        self.op("vector", lambda e: e.tensor_tensor(out=rw[:, 1, :], in0=rw[:, 1, :], in1=rw[:, 2, :], op=ALU.subtract))
        self.inc("vector", lambda e: e.tensor_single_scalar(out=rw[:, 1, :], in_=rw[:, 1, :], scalar=EPS, op=ALU.add), self.dve)
        self.wait("scalar", self.dve, self.dve.n)
        self.op("scalar", lambda e: e.activation(out=rw[:, 2, :], in_=rw[:, 1, :], func=AF.Ln))
        self.inc("scalar", lambda e: e.activation(out=rw[:, 3, :], in_=rw[:, 2, :], func=AF.Exp, scale=-0.5), self.act)
        self.wait("vector", self.act, self.act.n)
        self.inc("vector", lambda e: e.tensor_tensor(out=self.mrrow[:, :], in0=rw[:, 0, :], in1=rw[:, 3, :], op=ALU.mult), self.dve)
        self.wait("tensor", self.dve, self.dve.n)
        self.mmgroup(self.psG[:, :], [(self.onesf[:, :], rw[:, 3, :])])
        bc_done = self.mmgroup(self.psH[:, :], [(self.onesf[:, :], self.mrrow[:, :])])
        self.wait("scalar", self.pe, bc_done)
        if self.rb_guard is not None:
            self.waitg("scalar", self.rb_guard)
        self.op("scalar", lambda e: e.activation(out=self.Rb[:, :], in_=self.psG[:, :], func=AF.Copy))
        self.inc("scalar", lambda e: e.activation(out=self.MRb[:, :], in_=self.psH[:, :], func=AF.Copy), self.act)
        self.gh_guard = (self.act, self.act.n)
        self.wait("gpsimd", self.act, self.act.n)
        for g in self.xn_guard:
            self.waitg("gpsimd", g)
        if s > 0:
            self.waitg("gpsimd", self.xnlast_src)
            self.op("gpsimd", lambda e: e.tensor_copy(out=self.xn[:, :, 0:1], in_=self.xnlast[:, :, :]))
        xn_chunk = []
        for j in range(NCH):
            self.op("gpsimd", lambda e, j=j, buf=buf: e.tensor_tensor(
                out=self.xn[:, j, 1:T + 1], in0=self.xs[:, j, buf, :], in1=self.Rb[:, :], op=ALU.mult))
            self.inc("gpsimd", lambda e, j=j: e.tensor_tensor(
                out=self.xn[:, j, 1:T + 1], in0=self.xn[:, j, 1:T + 1], in1=self.MRb[:, :], op=ALU.subtract), self.pl)
            xn_chunk.append(self.pl.n)
        xn_done = self.pl.n
        self.rb_guard = (self.pl, xn_done)
        self.waitg("vector", self.mix_guard)
        for j in range(NCH):
            self.wait("vector", self.pl, xn_chunk[j])
            self.op("vector", lambda e, j=j: e.tensor_tensor(
                out=self.dd[:, :], in0=self.xn[:, j, 1:T + 1], in1=self.xn[:, j, 0:T], op=ALU.subtract))
            self.op("vector", lambda e, j=j: e.scalar_tensor_tensor(
                out=self.xkb[:, j, :], in0=self.dd[:, :], scalar=self.tmk[:, j:j + 1],
                in1=self.xn[:, j, 0:T], op0=ALU.mult, op1=ALU.add))
            fnv = lambda e, j=j: e.scalar_tensor_tensor(
                out=self.xvb[:, j, :], in0=self.dd[:, :], scalar=self.tmv[:, j:j + 1],
                in1=self.xn[:, j, 0:T], op0=ALU.mult, op1=ALU.add)
            if prefix:
                self.inc("vector", fnv, self.dve)
            else:
                self.op("vector", fnv)
                self.inc("vector", lambda e, j=j: e.scalar_tensor_tensor(
                    out=self.xrb[:, j, :], in0=self.dd[:, :], scalar=self.tmr[:, j:j + 1],
                    in1=self.xn[:, j, 0:T], op0=ALU.mult, op1=ALU.add), self.dve)
        self.inc("vector", lambda e: e.tensor_copy(
            out=self.xnlast[:, :, :], in_=self.xn[:, :, T:T + 1]), self.dve)
        mixes_done = self.dve.n
        if DEBUG_XN and s == 0 and not prefix:
            self.wait("sync", self.dve, mixes_done)
            dbgwr = self.dma("sync", self.xnd.rearrange("(g p) t -> p g t", p=128),
                             self.xn[:, :, :], self.qs)
            self.wait("vector", self.qs, dbgwr)
        self.xnlast_src = (self.dve, mixes_done)
        self.wait("tensor", self.dve, mixes_done)
        if s == 0 and prefix:
            self.wait("tensor", self.qg, self.w_loaded)
        rotps = [self.psA, self.psB, self.psC]
        dn_cnt = {}
        for dj in range(NCH):
            bank = self.gidx % 3
            self.waitg("tensor", self.rot3[bank])
            kmm = self.mmgroup(rotps[bank][:, :],
                               [(self.wkb[:, ci, dj * 128:(dj + 1) * 128], self.xkb[:, ci, :])
                                for ci in range(NCH)])
            self.wait("scalar", self.pe, kmm)
            if dj >= 2:
                self.wait("scalar", self.dve, dn_cnt[dj - 2])
            self.inc("scalar", lambda e, b=bank, dj=dj: e.activation(
                out=self.arot[:, dj % 2, :], in_=rotps[b][:, :], func=AF.Exp), self.act)
            a_ev = self.act.n
            self.rot3[bank] = (self.act, a_ev)
            self.gidx += 1
            if DEBUG_Y and s == 0 and not prefix:
                self.wait("sync", self.act, a_ev)
                kw_ = self.dma("sync", self.kdbg[dj * 128:(dj + 1) * 128, :],
                               self.arot[:, dj % 2, :], self.qs)
                self.wait("scalar", self.qs, kw_)
            bank = self.gidx % 3
            self.waitg("tensor", self.rot3[bank])
            vmm = self.mmgroup(rotps[bank][:, :],
                               [(self.wvb[:, ci, dj * 128:(dj + 1) * 128], self.xvb[:, ci, :])
                                for ci in range(NCH)])
            self.wait("vector", self.pe, vmm)
            self.wait("vector", self.act, a_ev)
            self.inc("vector", lambda e, b=bank, dj=dj: e.tensor_tensor(
                out=self.avrot[:, dj % 2, :], in0=rotps[b][:, :], in1=self.arot[:, dj % 2, :], op=ALU.mult), self.dve)
            self.rot3[bank] = (self.dve, self.dve.n)
            self.gidx += 1
            self.op("vector", lambda e, dj=dj: e.tensor_copy(out=self.N1[:, 0:1], in_=self.cN[:, dj:dj + 1]))
            self.op("vector", lambda e, dj=dj: e.tensor_copy(out=self.D1[:, 0:1], in_=self.cD[:, dj:dj + 1]))
            self.op("vector", lambda e, dj=dj: e.tensor_tensor_scan(
                out=self.N1[:, 1:T + 1], data0=self.alpha[:, dj:dj + 1].broadcast_to([128, T]),
                data1=self.avrot[:, dj % 2, :], initial=self.cN[:, dj:dj + 1], op0=ALU.mult, op1=ALU.add))
            self.op("vector", lambda e, dj=dj: e.tensor_tensor_scan(
                out=self.D1[:, 1:T + 1], data0=self.alpha[:, dj:dj + 1].broadcast_to([128, T]),
                data1=self.arot[:, dj % 2, :], initial=self.cD[:, dj:dj + 1], op0=ALU.mult, op1=ALU.add))
            self.op("vector", lambda e, dj=dj: e.tensor_copy(out=self.cN[:, dj:dj + 1], in_=self.N1[:, T:T + 1]))
            self.op("vector", lambda e, dj=dj: e.tensor_copy(out=self.cD[:, dj:dj + 1], in_=self.D1[:, T:T + 1]))
            if not prefix:
                self.op("vector", lambda e, dj=dj: e.scalar_tensor_tensor(
                    out=self.numt[:, :], in0=self.avrot[:, dj % 2, :], scalar=self.eu[:, dj:dj + 1],
                    in1=self.N1[:, 0:T], op0=ALU.mult, op1=ALU.add))
                self.op("vector", lambda e, dj=dj: e.scalar_tensor_tensor(
                    out=self.dent[:, :], in0=self.arot[:, dj % 2, :], scalar=self.eu[:, dj:dj + 1],
                    in1=self.D1[:, 0:T], op0=ALU.mult, op1=ALU.add))
                self.op("vector", lambda e: e.reciprocal(out=self.dent[:, :], in_=self.dent[:, :]))
                self.inc("vector", lambda e, dj=dj: e.tensor_tensor(
                    out=self.yt[:, dj, :], in0=self.numt[:, :], in1=self.dent[:, :], op=ALU.mult), self.dve)
            else:
                self.inc("vector", lambda e: e.engine_nop(), self.dve)
            dn_cnt[dj] = self.dve.n
        if DEBUG_Y and s == 0 and not prefix:
            self.wait("sync", self.dve, dn_cnt[NCH - 1])
            yw = self.dma("sync", self.ydbg.rearrange("(g p) t -> p g t", p=128),
                          self.yt[:, :, :], self.qs)
            self.wait("gpsimd", self.qs, yw)
        if prefix:
            self.xsg[buf] = [(self.pl, xn_done), (self.act, self.gh_guard[1])]
            self.xn_guard = [(self.dve, dn_cnt[NCH - 1])]
            return
        rwkv_cnt = {}
        for dj in range(NCH):
            bank = self.gidx % 3
            self.waitg("tensor", self.rot3[bank])
            rmm = self.mmgroup(rotps[bank][:, :],
                               [(self.wrb[:, ci, dj * 128:(dj + 1) * 128], self.xrb[:, ci, :])
                                for ci in range(NCH)])
            self.wait("scalar", self.pe, rmm)
            if dj >= 2:
                self.wait("scalar", self.pl, rwkv_cnt[dj - 2])
            self.inc("scalar", lambda e, b=bank, dj=dj: e.activation(
                out=self.sg[:, dj % 2, :], in_=rotps[b][:, :], func=AF.Sigmoid), self.act)
            self.rot3[bank] = (self.act, self.act.n)
            self.gidx += 1
            if DEBUG_Y and s == 0 and not prefix:
                self.wait("sync", self.act, self.act.n)
                sw_ = self.dma("sync", self.sgdbg[dj * 128:(dj + 1) * 128, :],
                               self.sg[:, dj % 2, :], self.qs)
                self.wait("scalar", self.qs, sw_)
            self.wait("gpsimd", self.act, self.act.n)
            self.wait("gpsimd", self.dve, dn_cnt[dj])
            self.inc("gpsimd", lambda e, dj=dj: e.tensor_tensor(
                out=self.rwkvb[:, dj, :], in0=self.sg[:, dj % 2, :], in1=self.yt[:, dj, :], op=ALU.mult), self.pl)
            rwkv_cnt[dj] = self.pl.n
        if DEBUG_Y and s == 0 and not prefix:
            self.wait("gpsimd", self.pl, self.pl.n)
            rw_ = self.dma("gpsimd", self.rkdbg.rearrange("(g p) t -> p g t", p=128),
                           self.rwkvb[:, :, :], self.qg)
            self.wait("vector", self.qg, rw_)
        x1_evac_last = 0
        for dj in range(NCH):
            bank = self.gidx % 3
            self.waitg("tensor", self.rot3[bank])
            if dj == 0:
                self.wait("tensor", self.pl, rwkv_cnt[NCH - 1])
            womm = self.mmgroup(rotps[bank][:, :],
                                [(self.wob[:, ci, dj * 128:(dj + 1) * 128], self.rwkvb[:, ci, :])
                                 for ci in range(NCH)])
            self.wait("vector", self.pe, womm)
            slot = dj % 2
            if slot in self.x1wr_slot:
                self.wait("vector", qs, self.x1wr_slot[slot])
            self.inc("vector", lambda e, b=bank, dj=dj, buf=buf: e.tensor_tensor(
                out=self.x1c[:, dj % 2, :], in0=rotps[b][:, :], in1=self.xs[:, dj, buf, :], op=ALU.add), self.dve)
            x1_evac_last = self.dve.n
            self.rot3[bank] = (self.dve, self.dve.n)
            self.gidx += 1
            self.wait("sync", self.dve, x1_evac_last)
            self.x1wr_slot[slot] = self.dma(
                "sync", self.x1d[dj * 128:(dj + 1) * 128, s * T:(s + 1) * T],
                self.x1c[:, dj % 2, :], qs)
        self.xsg[buf] = [(self.dve, x1_evac_last)]
        self.xn_guard = [(self.pe, self.pe.n), (self.dve, dn_cnt[NCH - 1])]
        self.mix_guard = (self.pe, self.pe.n)

    def tern_pass1(self, which, chunks):
        cols = self.colsA if which == 1 else self.colsB
        for src, col in chunks:
            slot = self.tern_slot
            self.tern_slot = (self.tern_slot + 1) % 2
            self.waitg("sync", self.stage_guard[slot])
            ld = self.dma("sync", self.stg[:, slot, :], src, self.qs)
            self.wait("vector", self.qs, ld)
            self.inc("vector", lambda e, slot=slot, cols=cols, col=col: e.tensor_reduce(
                out=cols[:, col:col + 1], in_=self.stg[:, slot, :], axis=AX.X,
                op=ALU.add, apply_absolute_value=True), self.dve)
            self.stage_guard[slot] = (self.dve, self.dve.n)

    def tern_scale(self, which, nelem):
        cols = self.colsA if which == 1 else self.colsB
        thr = self.thr1 if which == 1 else self.thr2
        nthr = self.nthr1 if which == 1 else self.nthr2
        self.inc("vector", lambda e, cols=cols: e.tensor_reduce(
            out=self.colred[:, :], in_=cols[:, :], axis=AX.X, op=ALU.add), self.dve)
        self.wait("tensor", self.dve, self.dve.n)
        self.wait("tensor", self.act, self.act.n)
        tot = self.mmgroup(self.psE[0:1, 0:1], [(self.colred[:, :], self.onescol[:, :])])
        self.wait("scalar", self.pe, tot)
        self.inc("scalar", lambda e, n=nelem: e.activation(
            out=self.trow[:, :], in_=self.psE[0:1, 0:1], func=AF.Copy, scale=1.0 / n), self.act)
        self.wait("vector", self.act, self.act.n)
        self.inc("vector", lambda e: e.tensor_single_scalar(
            out=self.trow[:, :], in_=self.trow[:, :], scalar=1e-8, op=ALU.max), self.dve)
        self.wait("tensor", self.dve, self.dve.n)
        bc = self.mmgroup(self.psE[:, 0:1], [(self.onesf[:, :], self.trow[:, :])])
        self.wait("scalar", self.pe, bc)
        self.op("scalar", lambda e, thr=thr: e.activation(out=thr[:, :], in_=self.psE[:, 0:1], func=AF.Copy, scale=0.5))
        self.inc("scalar", lambda e, nthr=nthr: e.activation(out=nthr[:, :], in_=self.psE[:, 0:1], func=AF.Copy, scale=-0.5), self.act)
        if which == 1:
            self.thr1_done = self.act.n
        else:
            self.thr2_done = self.act.n

    def tern_pass2(self, which, chunks):
        thr = self.thr1 if which == 1 else self.thr2
        nthr = self.nthr1 if which == 1 else self.nthr2
        thr_done = self.thr1_done if which == 1 else self.thr2_done
        for src, dst in chunks:
            slot = self.tern_slot
            self.tern_slot = (self.tern_slot + 1) % 2
            self.waitg("sync", self.stage_guard[slot])
            ld = self.dma("sync", self.stg[:, slot, :], src, self.qs)
            self.wait("gpsimd", self.qs, ld)
            self.wait("gpsimd", self.act, thr_done)
            self.inc("gpsimd", lambda e, slot=slot, nthr=nthr: e.tensor_single_scalar(
                out=self.mn[:, slot, :], in_=self.stg[:, slot, :], scalar=nthr[:, 0:1], op=ALU.is_lt), self.pl)
            self.wait("vector", self.pl, self.pl.n)
            if self.w1c_guard[slot] is not None:
                self.wait("vector", self.qg, self.w1c_guard[slot])
            self.inc("vector", lambda e, slot=slot, thr=thr: e.scalar_tensor_tensor(
                out=self.w1c[:, slot, :], in0=self.stg[:, slot, :], scalar=thr[:, 0:1],
                in1=self.mn[:, slot, :], op0=ALU.is_gt, op1=ALU.subtract), self.dve)
            self.stage_guard[slot] = (self.dve, self.dve.n)
            self.wait("gpsimd", self.dve, self.dve.n)
            self.w1c_guard[slot] = self.dma("gpsimd", dst, self.w1c[:, slot, :], self.qg)

    def phase_b(self):
        qs = self.qs
        for s in (self.pe, self.act, self.dve, self.pl, self.qg, self.qs):
            self.wait("sync", s, s.n)
        for ci in range(NCH):
            self.dma("sync", self.w1tb[:, ci, :], self.w1td[ci * 128:(ci + 1) * 128, :], qs)
        for fi in range(NFF):
            self.dma("sync", self.w2tb[:, fi, :], self.w2td[fi * 128:(fi + 1) * 128, :], qs)
        bw = qs.n
        for e in ("scalar", "tensor", "vector", "gpsimd"):
            self.wait(e, qs, bw)
        gh_guard = None
        rot2 = [None, None]
        outbank = [None] * 4
        outwr_slot = {}
        x1_guard = {}
        hg_guard = None
        for u in range(NU):
            buf = u % 2
            if buf in x1_guard:
                self.waitg("sync", x1_guard[buf])
            x1in = self.x1d.rearrange("(g p) t -> p g t", p=128)[:, :, u * U:(u + 1) * U]
            xl = self.dma("sync", self.x1s[:, :, buf, :], x1in, qs)
            self.wait("scalar", qs, xl)
            stats_mm = []
            for j in range(NCH):
                if j >= 2:
                    self.wait("scalar", self.pe, stats_mm[j - 2])
                self.op("scalar", lambda e, j=j, buf=buf: e.activation(
                    out=self.xb2[:, j % 2, :], in_=self.x1s[:, j, buf, :], func=AF.Copy))
                self.inc("scalar", lambda e, j=j, buf=buf: e.activation(
                    out=self.xsq2[:, j % 2, :], in_=self.x1s[:, j, buf, :], func=AF.Square), self.act)
                self.wait("tensor", self.act, self.act.n)
                if j == 0 and gh_guard is not None:
                    self.waitg("tensor", gh_guard)
                stG, spG = (j == 0), (j == NCH - 1)
                self.op("tensor", lambda e, j=j, st=stG, sp=spG: e.matmul(
                    self.psG[0:1, 0:U], self.onesb[:, :], self.xb2[:, j % 2, :], start=st, stop=sp))
                self.inc("tensor", lambda e, j=j, st=stG, sp=spG: e.matmul(
                    self.psH[0:1, 0:U], self.onesb[:, :], self.xsq2[:, j % 2, :], start=st, stop=sp), self.pe)
                stats_mm.append(self.pe.n)
            self.wait("vector", self.pe, self.pe.n)
            rw = self.rows
            self.op("vector", lambda e: e.tensor_single_scalar(out=rw[:, 0, 0:U], in_=self.psG[0:1, 0:U], scalar=1.0 / C, op=ALU.mult))
            self.op("vector", lambda e: e.tensor_single_scalar(out=rw[:, 1, 0:U], in_=self.psH[0:1, 0:U], scalar=1.0 / C, op=ALU.mult))
            self.op("vector", lambda e: e.tensor_tensor(out=rw[:, 2, 0:U], in0=rw[:, 0, 0:U], in1=rw[:, 0, 0:U], op=ALU.mult))
            self.op("vector", lambda e: e.tensor_tensor(out=rw[:, 1, 0:U], in0=rw[:, 1, 0:U], in1=rw[:, 2, 0:U], op=ALU.subtract))
            self.inc("vector", lambda e: e.tensor_single_scalar(out=rw[:, 1, 0:U], in_=rw[:, 1, 0:U], scalar=EPS, op=ALU.add), self.dve)
            self.wait("scalar", self.dve, self.dve.n)
            self.op("scalar", lambda e: e.activation(out=rw[:, 2, 0:U], in_=rw[:, 1, 0:U], func=AF.Ln))
            self.inc("scalar", lambda e: e.activation(out=rw[:, 3, 0:U], in_=rw[:, 2, 0:U], func=AF.Exp, scale=-0.5), self.act)
            self.wait("vector", self.act, self.act.n)
            self.inc("vector", lambda e: e.tensor_tensor(out=self.mrrow[:, 0:U], in0=rw[:, 0, 0:U], in1=rw[:, 3, 0:U], op=ALU.mult), self.dve)
            self.wait("tensor", self.dve, self.dve.n)
            self.mmgroup(self.psG[:, 0:U], [(self.onesf[:, :], rw[:, 3, 0:U])])
            bc_done = self.mmgroup(self.psH[:, 0:U], [(self.onesf[:, :], self.mrrow[:, 0:U])])
            self.wait("scalar", self.pe, bc_done)
            if self.rb_guard is not None:
                self.waitg("scalar", self.rb_guard)
            self.op("scalar", lambda e: e.activation(out=self.Rb[:, 0:U], in_=self.psG[:, 0:U], func=AF.Copy))
            self.inc("scalar", lambda e: e.activation(out=self.MRb[:, 0:U], in_=self.psH[:, 0:U], func=AF.Copy), self.act)
            gh_guard = (self.act, self.act.n)
            self.wait("gpsimd", self.act, self.act.n)
            if hg_guard is not None:
                self.waitg("gpsimd", hg_guard)
            for j in range(NCH):
                self.op("gpsimd", lambda e, j=j, buf=buf: e.tensor_tensor(
                    out=self.xn2f[:, :], in0=self.x1s[:, j, buf, :], in1=self.Rb[:, 0:U], op=ALU.mult))
                self.inc("gpsimd", lambda e, j=j: e.tensor_tensor(
                    out=self.xn2b[:, j, :], in0=self.xn2f[:, :], in1=self.MRb[:, 0:U], op=ALU.subtract), self.pl)
            xn2_done = self.pl.n
            self.rb_guard = (self.pl, xn2_done)
            self.wait("tensor", self.pl, xn2_done)
            rotps = [self.psA, self.psB]
            for fj in range(NFF):
                bank = fj % 2
                if rot2[bank] is not None:
                    self.waitg("tensor", rot2[bank])
                hmm = self.mmgroup(rotps[bank][:, 0:U],
                                   [(self.w1tb[:, ci, fj * 128:(fj + 1) * 128], self.xn2b[:, ci, :])
                                    for ci in range(NCH)])
                self.wait("scalar", self.pe, hmm)
                if fj == 0 and hg_guard is not None:
                    self.waitg("scalar", hg_guard)
                self.inc("scalar", lambda e, b=bank, fj=fj: e.activation(
                    out=self.hgb[:, fj, :], in_=rotps[b][:, 0:U], func=AF.Gelu,
                    bias=self.b1v[:, fj:fj + 1]), self.act)
                rot2[bank] = (self.act, self.act.n)
            gelu_all = self.act.n
            outps = [self.psC, self.psD, self.psE, self.psF]
            self.wait("tensor", self.act, gelu_all)
            for w in range(2):
                for q in range(4):
                    dj = 4 * w + q
                    if outbank[q] is not None:
                        self.waitg("tensor", outbank[q])
                    omm = self.mmgroup(outps[q][:, 0:U],
                                       [(self.w2tb[:, fi, dj * 128:(dj + 1) * 128], self.hgb[:, fi, :])
                                        for fi in range(NFF)])
                    self.wait("vector", self.pe, omm)
                    slot = dj % 2
                    if slot in outwr_slot:
                        self.wait("vector", qs, outwr_slot[slot])
                    self.inc("vector", lambda e, q=q, dj=dj, buf=buf: e.scalar_tensor_tensor(
                        out=self.outc[:, dj % 2, :], in0=outps[q][:, 0:U], scalar=self.b2v[:, dj:dj + 1],
                        in1=self.x1s[:, dj, buf, :], op0=ALU.add, op1=ALU.add), self.dve)
                    outbank[q] = (self.dve, self.dve.n)
                    self.wait("sync", self.dve, self.dve.n)
                    outwr_slot[slot] = self.dma(
                        "sync", self.outT[dj * 128:(dj + 1) * 128, u * U:(u + 1) * U],
                        self.outc[:, dj % 2, :], qs)
            hg_guard = (self.pe, self.pe.n)
            x1_guard[buf] = (self.dve, self.dve.n)
        self.wait("sync", qs, qs.n)
        self.wait("sync", self.qg, self.qg.n)

    def build(self):
        self.init_phase()
        w1p1 = [(self.w1[ci * 128:(ci + 1) * 128, fo * T:(fo + 1) * T], ci * (FF // T) + fo)
                for ci in range(NCH) for fo in range(FF // T)]
        w2p1 = [(self.w2[fi * 128:(fi + 1) * 128, do * T:(do + 1) * T], fi * (C // T) + do)
                for fi in range(NFF) for do in range(C // T)]
        w1p2 = [(self.w1[ci * 128:(ci + 1) * 128, fo * T:(fo + 1) * T],
                 self.w1td[ci * 128:(ci + 1) * 128, fo * T:(fo + 1) * T])
                for ci in range(NCH) for fo in range(FF // T)]
        w2p2 = [(self.w2[fi * 128:(fi + 1) * 128, do * T:(do + 1) * T],
                 self.w2td[fi * 128:(fi + 1) * 128, do * T:(do + 1) * T])
                for fi in range(NFF) for do in range(C // T)]

        self.tiny_stats(self.xpcP, wait_q=self.vec_loads)
        for s in range(NS):
            self.strip(s, self.xp, True)
            self.tern_pass1(1, w1p1[s * 8:(s + 1) * 8])
            self.tern_pass1(2, w2p1[s * 8:(s + 1) * 8])
        self.tern_pass1(1, w1p1[32:])
        self.tern_pass1(2, w2p1[32:])
        self.tern_scale(1, C * FF)
        self.tern_scale(2, C * FF)
        self.wait("vector", self.act, self.mk_done)
        self.op("vector", lambda e: e.tensor_single_scalar(
            out=self.cN[:, :], in_=self.cN[:, :], scalar=self.mkcol[:, 0:1], op=ALU.mult))
        self.inc("vector", lambda e: e.tensor_single_scalar(
            out=self.cD[:, :], in_=self.cD[:, :], scalar=self.mkcol[:, 0:1], op=ALU.mult), self.dve)
        self.xn_guard = [(self.dve, self.dve.n)]
        self.mix_guard = (self.pe, self.pe.n)
        self.tiny_stats(self.xpcM)
        for s in range(NS):
            self.strip(s, self.xm, False)
            self.tern_pass2(1, w1p2[s * 16:(s + 1) * 16])
            self.tern_pass2(2, w2p2[s * 16:(s + 1) * 16])
        self.phase_b()

        nc = self.nc
        with nc.Block() as blk:
            for name, meth in (("sync", blk.sync), ("gpsimd", blk.gpsimd),
                               ("scalar", blk.scalar), ("vector", blk.vector),
                               ("tensor", blk.tensor)):
                thunks = self.prog[name]
                meth(lambda eng, thunks=thunks: [f(eng) for f in thunks])
        return nc


_NC_CACHE = {}


def _get_nc():
    if "nc" not in _NC_CACHE:
        _NC_CACHE["nc"] = Build().build()
    return _NC_CACHE["nc"]


def kernel(L0_attn_x, L0_attn_time_decay, L0_attn_time_first, L0_attn_time_mix_k,
           L0_attn_time_mix_v, L0_attn_time_mix_r, L0_attn_Wk, L0_attn_Wv,
           L0_attn_Wr, L0_attn_Wo, L0_ffn_W1, L0_ffn_b1, L0_ffn_W2, L0_ffn_b2):
    x = np.ascontiguousarray(np.asarray(L0_attn_x, dtype=np.float32))
    assert x.shape == (4, 4096, 1024)
    ac = np.ascontiguousarray

    def rv(v, n):
        return ac(np.asarray(v, np.float32).reshape(n, 128).T)

    shared = dict(
        wk=ac(np.asarray(L0_attn_Wk, np.float32).T),
        wv=ac(np.asarray(L0_attn_Wv, np.float32).T),
        wr=ac(np.asarray(L0_attn_Wr, np.float32).T),
        wo=ac(np.asarray(L0_attn_Wo, np.float32).T),
        w1=ac(np.asarray(L0_ffn_W1, np.float32).T),
        w2=ac(np.asarray(L0_ffn_W2, np.float32).T),
        tmkd=rv(L0_attn_time_mix_k, 8), tmvd=rv(L0_attn_time_mix_v, 8),
        tmrd=rv(L0_attn_time_mix_r, 8), tdd=rv(L0_attn_time_decay, 8),
        tfd=rv(L0_attn_time_first, 8), b1d=rv(L0_ffn_b1, 32), b2d=rv(L0_ffn_b2, 8))
    in_maps = []
    for c in range(8):
        b, h = c // 2, c % 2
        xmT = np.empty((1024, TLOC + 1), np.float32)
        xmT[:, 0] = 0.0 if h == 0 else x[b, TLOC - 1, :]
        xmT[:, 1:] = x[b, h * TLOC:(h + 1) * TLOC, :].T
        xpT = np.empty((1024, TLOC + 1), np.float32)
        xpT[:, 0] = 0.0
        xpT[:, 1:] = x[b, 0:TLOC, :].T
        m = dict(shared)
        m["xm"] = ac(xmT)
        m["xp"] = ac(xpT)
        m["msk"] = np.full((1, 1), float(h), np.float32)
        in_maps.append(m)

    nc = _get_nc()
    res = run_bass_kernel_spmd(nc, in_maps, core_ids=list(range(8)))
    out = np.empty((4, 4096, 1024), np.float32)
    for c in range(8):
        b, h = c // 2, c % 2
        out[b, h * TLOC:(h + 1) * TLOC, :] = np.asarray(res.results[c]["outT"]).T
    return out


# version 04
# speedup vs baseline: 1.2491x; 1.2491x over previous
import numpy as np
import concourse.bass as bass
import concourse.mybir as mybir
from concourse.bass_utils import run_bass_kernel_spmd

F32 = mybir.dt.float32
BF16 = mybir.dt.bfloat16
AF = mybir.ActivationFunctionType
ALU = mybir.AluOpType
AX = mybir.AxisListType

DEBUG_X1 = False
DEBUG_XN = False
DEBUG_Y = False
C = 1024
NCH = 8
FF = 4096
NFF = 32
T = 512
NS = 4
U = 256
NU = 8
TLOC = 2048
EPS = 1e-5


class Sem:
    def __init__(self, h):
        self.h = h
        self.n = 0


class Arena:
    def __init__(self, nc, size):
        self.nc = nc
        self.size = size
        self.ctx = nc.sbuf_tensor("arena", [128, size], mybir.dt.uint8)
        self.big = self.ctx.__enter__()
        self.base = nc.lookup_mloc(self.big).addr
        self.cur = 0

    def alloc(self, name, shape, dtype, at=None):
        per = int(np.prod(shape[1:])) * mybir.dt.size(dtype)
        off = self.cur if at is None else at
        off = (off + 63) & ~63
        assert off + per <= self.size, (name, off, per, self.size)
        h = self.nc.virtual_sbuf_tensor(name, shape, dtype)
        self.nc._tensor(h.name, h.shape, h.dtype, type="SB")
        m = self.nc.lookup_mloc(h)
        m.allocated = True
        m.addr = self.base + off
        if at is None:
            self.cur = off + per
        return h, off


class Build:
    ENGS = ("sync", "gpsimd", "scalar", "vector", "tensor")

    def __init__(self):
        nc = bass.Bass("TRN2", target_bir_lowering=False, debug=False)
        self.nc = nc
        self.prog = {e: [] for e in self.ENGS}
        d = nc.dram_tensor
        self.xm = d("xm", [C, TLOC + 1], F32, kind="ExternalInput").ap()
        self.xp = d("xp", [C, TLOC + 1], F32, kind="ExternalInput").ap()
        self.msk = d("msk", [1, 1], F32, kind="ExternalInput").ap()
        self.wk = d("wk", [C, C], F32, kind="ExternalInput").ap()
        self.wv = d("wv", [C, C], F32, kind="ExternalInput").ap()
        self.wr = d("wr", [C, C], F32, kind="ExternalInput").ap()
        self.wo = d("wo", [C, C], F32, kind="ExternalInput").ap()
        self.w1 = d("w1", [C, FF], F32, kind="ExternalInput").ap()
        self.w2 = d("w2", [FF, C], F32, kind="ExternalInput").ap()
        self.tmkd = d("tmkd", [128, NCH], F32, kind="ExternalInput").ap()
        self.tmvd = d("tmvd", [128, NCH], F32, kind="ExternalInput").ap()
        self.tmrd = d("tmrd", [128, NCH], F32, kind="ExternalInput").ap()
        self.tdd = d("tdd", [128, NCH], F32, kind="ExternalInput").ap()
        self.tfd = d("tfd", [128, NCH], F32, kind="ExternalInput").ap()
        self.b1d = d("b1d", [128, NFF], F32, kind="ExternalInput").ap()
        self.b2d = d("b2d", [128, NCH], F32, kind="ExternalInput").ap()
        self.outT = d("outT", [C, TLOC], F32, kind="ExternalOutput").ap()
        self.x1d = d("x1d", [C, TLOC], F32,
                     kind="ExternalOutput" if DEBUG_X1 else "Internal").ap()
        self.xnd = (d("xnd", [C, T + 1], F32, kind="ExternalOutput").ap()
                    if DEBUG_XN else None)
        self.ydbg = (d("ydbg", [C, T], F32, kind="ExternalOutput").ap()
                     if DEBUG_Y else None)
        self.kdbg = (d("kdbg", [C, T], F32, kind="ExternalOutput").ap()
                     if DEBUG_Y else None)
        self.rkdbg = (d("rkdbg", [C, T], F32, kind="ExternalOutput").ap()
                      if DEBUG_Y else None)
        self.sgdbg = (d("sgdbg", [C, T], F32, kind="ExternalOutput").ap()
                      if DEBUG_Y else None)
        self.w1td = d("w1td", [C, FF], BF16, kind="Internal").ap()
        self.w2td = d("w2td", [FF, C], BF16, kind="Internal").ap()

        self.qs = Sem(nc.alloc_semaphore("qs"))
        self.qg = Sem(nc.alloc_semaphore("qg"))
        self.pe = Sem(nc.alloc_semaphore("pe"))
        self.act = Sem(nc.alloc_semaphore("act"))
        self.dve = Sem(nc.alloc_semaphore("dve"))
        self.pl = Sem(nc.alloc_semaphore("pl"))

        self._alloc_sbuf()
        p = nc.alloc_psum_tensor
        self.psA = p("psA", [128, T], F32)
        self.psB = p("psB", [128, T], F32)
        self.psC = p("psC", [128, T], F32)
        self.psD = p("psD", [128, T], F32)
        self.psE = p("psE", [128, T], F32)
        self.psF = p("psF", [128, T], F32)
        self.psG = p("psG", [128, T], F32)
        self.psH = p("psH", [128, T], F32)

        self.rot3 = [None, None, None]
        self.gidx = 0
        self.stage_guard = [None, None]
        self.w1c_guard = [None, None]
        self.tern_slot = 0
        self.xsg = {0: [], 1: []}
        self.x1wr_slot = {}
        self.gh_guard = None
        self.xn_guard = []
        self.mix_guard = None
        self.xnlast_src = None
        self.rb_guard = None

    def _alloc_sbuf(self):
        a = Arena(self.nc, 200 * 1024)
        self.ar = a

        def al(name, shape, dtype, at=None):
            h, off = a.alloc(name, shape, dtype, at)
            setattr(self, name, h)
            return off
        al("tmk", [128, NCH], F32); al("tmv", [128, NCH], F32)
        al("tmr", [128, NCH], F32); al("tdr", [128, NCH], F32)
        al("tfr", [128, NCH], F32); al("alpha", [128, NCH], F32)
        al("eu", [128, NCH], F32); al("b1v", [128, NFF], F32)
        al("b2v", [128, NCH], F32); al("onesb", [128, 1], BF16)
        al("onesf", [1, 128], F32); al("onescol", [128, 1], F32)
        al("msksb", [1, 1], F32); al("mkcol", [128, 1], F32)
        al("mcol", [128, 1], F32); al("rcol", [128, 1], F32)
        al("thr1", [128, 1], F32); al("nthr1", [128, 1], F32)
        al("thr2", [128, 1], F32); al("nthr2", [128, 1], F32)
        al("trow", [1, 1], F32); al("cN", [128, NCH], F32)
        al("cD", [128, NCH], F32)
        al("xpcP", [128, NCH, 1], F32); al("xpcM", [128, NCH, 1], F32)
        al("xpcB", [128, NCH, 1], BF16)
        al("colsA", [128, 64], F32); al("colsB", [128, 64], F32)
        al("colred", [128, 1], F32); al("tiny", [1, 8], F32)
        al("xnlast", [128, NCH, 1], F32)
        al("rows", [1, 4, T], F32); al("mrrow", [1, T], F32)
        al("Rb", [128, T], F32); al("MRb", [128, T], F32)
        al("stg", [128, 2, T], F32); al("mn", [128, 2, T], F32)
        al("w1c", [128, 2, T], BF16)
        wbase = a.cur
        al("wkb", [128, NCH, C], BF16); al("wvb", [128, NCH, C], BF16)
        al("wrb", [128, NCH, C], BF16); al("wob", [128, NCH, C], BF16)
        al("w1tb", [128, NCH, FF], BF16, at=wbase)
        sbase = a.cur
        al("xs", [128, NCH, 2, T], F32)
        xnoff = al("xn", [128, NCH, T + 1], F32)
        xyoff = al("xkb", [128, NCH, T], BF16)
        al("xvb", [128, NCH, T], BF16)
        al("xrb", [128, NCH, T], BF16)
        al("arot", [128, 2, T], F32); al("avrot", [128, 2, T], F32)
        al("N1", [128, T + 1], F32); al("D1", [128, T + 1], F32)
        al("xbr", [128, 4, T], BF16); al("xsqr", [128, 4, T], BF16)
        al("dd", [128, T], F32); al("numt", [128, T], F32)
        al("dent", [128, T], F32)
        al("sg", [128, 2, T], F32); al("x1c", [128, 2, T], F32)
        a_end = a.cur
        al("yt", [128, NCH, T], F32, at=xnoff)
        al("rwkvb", [128, NCH, T], BF16, at=xyoff)
        al("w2tb", [128, NFF, C], BF16, at=sbase)
        o = sbase + NFF * C * 2
        o = al("x1s", [128, NCH, 2, U], F32, at=o) + NCH * 2 * U * 4
        o = al("xn2b", [128, NCH, U], BF16, at=o) + NCH * U * 2
        o = al("hgb", [128, NFF, U], BF16, at=o) + NFF * U * 2
        o = al("xb2", [128, 2, U], BF16, at=o) + 2 * U * 2
        o = al("xsq2", [128, 2, U], BF16, at=o) + 2 * U * 2
        o = al("xn2f", [128, U], F32, at=o) + U * 4
        o = al("outc", [128, 2, U], F32, at=o) + 2 * U * 4
        assert max(a_end, o) <= 200 * 1024, (a_end, o)

    def emit(self, e, fn):
        self.prog[e].append(fn)

    def wait(self, e, s, val=None):
        v = s.n if val is None else val
        h = s.h
        if v > 0:
            self.emit(e, lambda eng, h=h, v=v: eng.wait_ge(h, v))

    def waitg(self, e, g):
        if g:
            self.wait(e, g[0], g[1])

    def inc(self, e, fn, s, k=1):
        s.n += k
        h = s.h
        self.emit(e, lambda eng, fn=fn, h=h, k=k: fn(eng).then_inc(h, k))
        return s.n

    def op(self, e, fn):
        self.emit(e, lambda eng, fn=fn: fn(eng))

    def dma(self, e, out, in_, s, slow=False):
        if slow:
            return self.inc(e, lambda eng, o=out, i=in_: eng.dma_start(
                out=o, in_=i, allow_slow_non_contiguous=True), s, 16)
        return self.inc(e, lambda eng, o=out, i=in_: eng.dma_start(out=o, in_=i), s, 16)

    def mmgroup(self, ps_ap, pairs):
        n = len(pairs)
        for i, (lhsT, rhs) in enumerate(pairs):
            st, sp = (i == 0), (i == n - 1)
            if sp:
                self.inc("tensor", lambda eng, o=ps_ap, l=lhsT, r=rhs, st=st, sp=sp:
                         eng.matmul(o, l, r, start=st, stop=sp), self.pe)
            else:
                self.op("tensor", lambda eng, o=ps_ap, l=lhsT, r=rhs, st=st, sp=sp:
                        eng.matmul(o, l, r, start=st, stop=sp))
        return self.pe.n

    def init_phase(self):
        qs, qg = self.qs, self.qg
        for dst, src in ((self.tmk, self.tmkd), (self.tmv, self.tmvd),
                         (self.tmr, self.tmrd), (self.tdr, self.tdd),
                         (self.tfr, self.tfd), (self.b1v, self.b1d),
                         (self.b2v, self.b2d), (self.msksb, self.msk)):
            self.dma("sync", dst[:, :], src, qs)
        self.dma("sync", self.xpcP[:, :, :],
                 self.xp[:, 0:1].rearrange("(g p) t -> p g t", p=128), qs, slow=True)
        self.dma("sync", self.xpcM[:, :, :],
                 self.xm[:, 0:1].rearrange("(g p) t -> p g t", p=128), qs, slow=True)
        self.vec_loads = qs.n
        for wdst, wsrc in ((self.wkb, self.wk), (self.wvb, self.wv),
                           (self.wrb, self.wr), (self.wob, self.wo)):
            for ci in range(NCH):
                self.dma("gpsimd", wdst[:, ci, :], wsrc[ci * 128:(ci + 1) * 128, :], qg)
        self.w_loaded = qg.n
        self.op("vector", lambda e: e.memset(self.onesb[:, :], 1.0))
        self.op("vector", lambda e: e.memset(self.onesf[:, :], 1.0))
        self.op("vector", lambda e: e.memset(self.onescol[:, :], 1.0))
        self.op("vector", lambda e: e.memset(self.cN[:, :], 0.0))
        self.inc("vector", lambda e: e.memset(self.cD[:, :], 0.0), self.dve)
        self.memset_done = self.dve.n
        self.wait("scalar", qs, self.vec_loads)
        self.op("scalar", lambda e: e.activation(out=self.alpha[:, :], in_=self.tdr[:, :], func=AF.Exp))
        self.op("scalar", lambda e: e.activation(out=self.alpha[:, :], in_=self.alpha[:, :], func=AF.Exp, scale=-1.0))
        self.inc("scalar", lambda e: e.activation(out=self.eu[:, :], in_=self.tfr[:, :], func=AF.Exp), self.act)
        self.wait("tensor", self.dve, self.memset_done)
        self.wait("tensor", qs, self.vec_loads)
        mk_mm = self.mmgroup(self.psE[:, 0:1], [(self.onesf[:, :], self.msksb[:, :])])
        self.wait("scalar", self.pe, mk_mm)
        self.inc("scalar", lambda e: e.activation(out=self.mkcol[:, :], in_=self.psE[:, 0:1], func=AF.Copy), self.act)
        self.mk_done = self.act.n

    def tiny_stats(self, xpc, wait_q=None):
        if wait_q is not None:
            self.wait("scalar", self.qs, wait_q)
        self.inc("scalar", lambda e: e.activation(out=self.xpcB[:, :, :], in_=xpc[:, :, :], func=AF.Copy), self.act)
        self.wait("tensor", self.act, self.act.n)
        mmx = self.mmgroup(self.psF[0:1, 0:1],
                           [(self.onesb[:, :], self.xpcB[:, j, :]) for j in range(NCH)])
        self.wait("scalar", self.pe, mmx)
        self.inc("scalar", lambda e: e.activation(out=self.xpcB[:, :, :], in_=xpc[:, :, :], func=AF.Square), self.act)
        self.wait("tensor", self.act, self.act.n)
        self.mmgroup(self.psD[0:1, 0:1],
                     [(self.onesb[:, :], self.xpcB[:, j, :]) for j in range(NCH)])
        self.wait("vector", self.pe, self.pe.n)
        ti = self.tiny
        self.op("vector", lambda e: e.tensor_single_scalar(out=ti[:, 0:1], in_=self.psF[0:1, 0:1], scalar=1.0 / C, op=ALU.mult))
        self.op("vector", lambda e: e.tensor_single_scalar(out=ti[:, 1:2], in_=self.psD[0:1, 0:1], scalar=1.0 / C, op=ALU.mult))
        self.op("vector", lambda e: e.tensor_tensor(out=ti[:, 2:3], in0=ti[:, 0:1], in1=ti[:, 0:1], op=ALU.mult))
        self.op("vector", lambda e: e.tensor_tensor(out=ti[:, 1:2], in0=ti[:, 1:2], in1=ti[:, 2:3], op=ALU.subtract))
        self.inc("vector", lambda e: e.tensor_single_scalar(out=ti[:, 1:2], in_=ti[:, 1:2], scalar=EPS, op=ALU.add), self.dve)
        self.wait("scalar", self.dve, self.dve.n)
        self.op("scalar", lambda e: e.activation(out=ti[:, 2:3], in_=ti[:, 1:2], func=AF.Ln))
        self.inc("scalar", lambda e: e.activation(out=ti[:, 3:4], in_=ti[:, 2:3], func=AF.Exp, scale=-0.5), self.act)
        self.wait("tensor", self.act, self.act.n)
        self.mmgroup(self.psF[:, 0:1], [(self.onesf[:, :], ti[:, 0:1])])
        mm3 = self.mmgroup(self.psD[:, 0:1], [(self.onesf[:, :], ti[:, 3:4])])
        self.wait("scalar", self.pe, mm3)
        self.op("scalar", lambda e: e.activation(out=self.mcol[:, :], in_=self.psF[:, 0:1], func=AF.Copy))
        self.inc("scalar", lambda e: e.activation(out=self.rcol[:, :], in_=self.psD[:, 0:1], func=AF.Copy), self.act)
        self.wait("vector", self.act, self.act.n)
        for j in range(NCH):
            fn = lambda e, j=j: e.scalar_tensor_tensor(
                out=self.xn[:, j, 0:1], in0=xpc[:, j, :], scalar=self.mcol[:, 0:1],
                in1=self.rcol[:, 0:1], op0=ALU.subtract, op1=ALU.mult)
            if j == NCH - 1:
                self.inc("vector", fn, self.dve)
            else:
                self.op("vector", fn)

    def strip(self, s, src, prefix):
        buf = s % 2
        c0 = 1 + s * T
        qs = self.qs
        for g in self.xsg[buf]:
            self.waitg("sync", g)
        xin = src.rearrange("(g p) t -> p g t", p=128)[:, :, c0:c0 + T]
        xl = self.dma("sync", self.xs[:, :, buf, :], xin, qs)
        self.wait("scalar", qs, xl)
        stats_mm = []
        for j in range(NCH):
            if j >= 4:
                self.wait("scalar", self.pe, stats_mm[j - 4])
            self.op("scalar", lambda e, j=j, buf=buf: e.activation(
                out=self.xbr[:, j % 4, :], in_=self.xs[:, j, buf, :], func=AF.Copy))
            self.inc("scalar", lambda e, j=j, buf=buf: e.activation(
                out=self.xsqr[:, j % 4, :], in_=self.xs[:, j, buf, :], func=AF.Square), self.act)
            self.wait("tensor", self.act, self.act.n)
            if j == 0:
                self.waitg("tensor", self.gh_guard)
            stG, spG = (j == 0), (j == NCH - 1)
            self.op("tensor", lambda e, j=j, st=stG, sp=spG: e.matmul(
                self.psG[0:1, :], self.onesb[:, :], self.xbr[:, j % 4, :], start=st, stop=sp))
            self.inc("tensor", lambda e, j=j, st=stG, sp=spG: e.matmul(
                self.psH[0:1, :], self.onesb[:, :], self.xsqr[:, j % 4, :], start=st, stop=sp), self.pe)
            stats_mm.append(self.pe.n)
        stats_done = self.pe.n
        self.wait("vector", self.pe, stats_done)
        rw = self.rows
        self.op("vector", lambda e: e.tensor_single_scalar(out=rw[:, 0, :], in_=self.psG[0:1, :], scalar=1.0 / C, op=ALU.mult))
        self.op("vector", lambda e: e.tensor_single_scalar(out=rw[:, 1, :], in_=self.psH[0:1, :], scalar=1.0 / C, op=ALU.mult))
        self.op("vector", lambda e: e.tensor_tensor(out=rw[:, 2, :], in0=rw[:, 0, :], in1=rw[:, 0, :], op=ALU.mult))
        self.op("vector", lambda e: e.tensor_tensor(out=rw[:, 1, :], in0=rw[:, 1, :], in1=rw[:, 2, :], op=ALU.subtract))
        self.inc("vector", lambda e: e.tensor_single_scalar(out=rw[:, 1, :], in_=rw[:, 1, :], scalar=EPS, op=ALU.add), self.dve)
        self.wait("scalar", self.dve, self.dve.n)
        self.op("scalar", lambda e: e.activation(out=rw[:, 2, :], in_=rw[:, 1, :], func=AF.Ln))
        self.inc("scalar", lambda e: e.activation(out=rw[:, 3, :], in_=rw[:, 2, :], func=AF.Exp, scale=-0.5), self.act)
        self.wait("vector", self.act, self.act.n)
        self.inc("vector", lambda e: e.tensor_tensor(out=self.mrrow[:, :], in0=rw[:, 0, :], in1=rw[:, 3, :], op=ALU.mult), self.dve)
        self.wait("tensor", self.dve, self.dve.n)
        self.mmgroup(self.psG[:, :], [(self.onesf[:, :], rw[:, 3, :])])
        bc_done = self.mmgroup(self.psH[:, :], [(self.onesf[:, :], self.mrrow[:, :])])
        self.wait("scalar", self.pe, bc_done)
        if self.rb_guard is not None:
            self.waitg("scalar", self.rb_guard)
        self.op("scalar", lambda e: e.activation(out=self.Rb[:, :], in_=self.psG[:, :], func=AF.Copy))
        self.inc("scalar", lambda e: e.activation(out=self.MRb[:, :], in_=self.psH[:, :], func=AF.Copy), self.act)
        self.gh_guard = (self.act, self.act.n)
        self.wait("vector", self.act, self.act.n)
        if s > 0:
            self.op("vector", lambda e: e.tensor_copy(out=self.xn[:, :, 0:1], in_=self.xnlast[:, :, :]))
        self.waitg("vector", self.mix_guard)
        for j in range(NCH):
            self.op("vector", lambda e, j=j, buf=buf: e.tensor_tensor(
                out=self.xn[:, j, 1:T + 1], in0=self.xs[:, j, buf, :], in1=self.Rb[:, :], op=ALU.mult))
            self.op("vector", lambda e, j=j: e.tensor_tensor(
                out=self.xn[:, j, 1:T + 1], in0=self.xn[:, j, 1:T + 1], in1=self.MRb[:, :], op=ALU.subtract))
        self.inc("vector", lambda e: e.engine_nop(), self.dve)
        xn_done = self.dve.n
        self.rb_guard = (self.dve, xn_done)
        for j in range(NCH):
            self.op("vector", lambda e, j=j: e.tensor_tensor(
                out=self.dd[:, :], in0=self.xn[:, j, 1:T + 1], in1=self.xn[:, j, 0:T], op=ALU.subtract))
            self.op("vector", lambda e, j=j: e.scalar_tensor_tensor(
                out=self.xkb[:, j, :], in0=self.dd[:, :], scalar=self.tmk[:, j:j + 1],
                in1=self.xn[:, j, 0:T], op0=ALU.mult, op1=ALU.add))
            fnv = lambda e, j=j: e.scalar_tensor_tensor(
                out=self.xvb[:, j, :], in0=self.dd[:, :], scalar=self.tmv[:, j:j + 1],
                in1=self.xn[:, j, 0:T], op0=ALU.mult, op1=ALU.add)
            if prefix:
                self.inc("vector", fnv, self.dve)
            else:
                self.op("vector", fnv)
                self.inc("vector", lambda e, j=j: e.scalar_tensor_tensor(
                    out=self.xrb[:, j, :], in0=self.dd[:, :], scalar=self.tmr[:, j:j + 1],
                    in1=self.xn[:, j, 0:T], op0=ALU.mult, op1=ALU.add), self.dve)
        self.inc("vector", lambda e: e.tensor_copy(
            out=self.xnlast[:, :, :], in_=self.xn[:, :, T:T + 1]), self.dve)
        mixes_done = self.dve.n
        if DEBUG_XN and s == 0 and not prefix:
            self.wait("sync", self.dve, mixes_done)
            dbgwr = self.dma("sync", self.xnd.rearrange("(g p) t -> p g t", p=128),
                             self.xn[:, :, :], self.qs)
            self.wait("vector", self.qs, dbgwr)
        self.xnlast_src = (self.dve, mixes_done)
        self.wait("tensor", self.dve, mixes_done)
        if s == 0 and prefix:
            self.wait("tensor", self.qg, self.w_loaded)
        rotps = [self.psA, self.psB, self.psC]
        dn_cnt = {}
        for dj in range(NCH):
            bank = self.gidx % 3
            self.waitg("tensor", self.rot3[bank])
            kmm = self.mmgroup(rotps[bank][:, :],
                               [(self.wkb[:, ci, dj * 128:(dj + 1) * 128], self.xkb[:, ci, :])
                                for ci in range(NCH)])
            self.wait("scalar", self.pe, kmm)
            if dj >= 2:
                self.wait("scalar", self.dve, dn_cnt[dj - 2])
            self.inc("scalar", lambda e, b=bank, dj=dj: e.activation(
                out=self.arot[:, dj % 2, :], in_=rotps[b][:, :], func=AF.Exp), self.act)
            a_ev = self.act.n
            self.rot3[bank] = (self.act, a_ev)
            self.gidx += 1
            if DEBUG_Y and s == 0 and not prefix:
                self.wait("sync", self.act, a_ev)
                kw_ = self.dma("sync", self.kdbg[dj * 128:(dj + 1) * 128, :],
                               self.arot[:, dj % 2, :], self.qs)
                self.wait("scalar", self.qs, kw_)
            bank = self.gidx % 3
            self.waitg("tensor", self.rot3[bank])
            vmm = self.mmgroup(rotps[bank][:, :],
                               [(self.wvb[:, ci, dj * 128:(dj + 1) * 128], self.xvb[:, ci, :])
                                for ci in range(NCH)])
            self.wait("vector", self.pe, vmm)
            self.wait("vector", self.act, a_ev)
            self.inc("vector", lambda e, b=bank, dj=dj: e.tensor_tensor(
                out=self.avrot[:, dj % 2, :], in0=rotps[b][:, :], in1=self.arot[:, dj % 2, :], op=ALU.mult), self.dve)
            self.rot3[bank] = (self.dve, self.dve.n)
            self.gidx += 1
            self.op("vector", lambda e, dj=dj: e.tensor_copy(out=self.N1[:, 0:1], in_=self.cN[:, dj:dj + 1]))
            self.op("vector", lambda e, dj=dj: e.tensor_copy(out=self.D1[:, 0:1], in_=self.cD[:, dj:dj + 1]))
            self.op("vector", lambda e, dj=dj: e.tensor_tensor_scan(
                out=self.N1[:, 1:T + 1], data0=self.alpha[:, dj:dj + 1].broadcast_to([128, T]),
                data1=self.avrot[:, dj % 2, :], initial=self.cN[:, dj:dj + 1], op0=ALU.mult, op1=ALU.add))
            self.op("vector", lambda e, dj=dj: e.tensor_tensor_scan(
                out=self.D1[:, 1:T + 1], data0=self.alpha[:, dj:dj + 1].broadcast_to([128, T]),
                data1=self.arot[:, dj % 2, :], initial=self.cD[:, dj:dj + 1], op0=ALU.mult, op1=ALU.add))
            self.op("vector", lambda e, dj=dj: e.tensor_copy(out=self.cN[:, dj:dj + 1], in_=self.N1[:, T:T + 1]))
            self.op("vector", lambda e, dj=dj: e.tensor_copy(out=self.cD[:, dj:dj + 1], in_=self.D1[:, T:T + 1]))
            if not prefix:
                self.op("vector", lambda e, dj=dj: e.scalar_tensor_tensor(
                    out=self.numt[:, :], in0=self.avrot[:, dj % 2, :], scalar=self.eu[:, dj:dj + 1],
                    in1=self.N1[:, 0:T], op0=ALU.mult, op1=ALU.add))
                self.op("vector", lambda e, dj=dj: e.scalar_tensor_tensor(
                    out=self.dent[:, :], in0=self.arot[:, dj % 2, :], scalar=self.eu[:, dj:dj + 1],
                    in1=self.D1[:, 0:T], op0=ALU.mult, op1=ALU.add))
                self.op("vector", lambda e: e.reciprocal(out=self.dent[:, :], in_=self.dent[:, :]))
                self.inc("vector", lambda e, dj=dj: e.tensor_tensor(
                    out=self.yt[:, dj, :], in0=self.numt[:, :], in1=self.dent[:, :], op=ALU.mult), self.dve)
            else:
                self.inc("vector", lambda e: e.engine_nop(), self.dve)
            dn_cnt[dj] = self.dve.n
        if DEBUG_Y and s == 0 and not prefix:
            self.wait("sync", self.dve, dn_cnt[NCH - 1])
            yw = self.dma("sync", self.ydbg.rearrange("(g p) t -> p g t", p=128),
                          self.yt[:, :, :], self.qs)
            self.wait("gpsimd", self.qs, yw)
        if prefix:
            self.xsg[buf] = [(self.dve, dn_cnt[NCH - 1])]
            return
        rwkv_cnt = {}
        for dj in range(NCH):
            bank = self.gidx % 3
            self.waitg("tensor", self.rot3[bank])
            rmm = self.mmgroup(rotps[bank][:, :],
                               [(self.wrb[:, ci, dj * 128:(dj + 1) * 128], self.xrb[:, ci, :])
                                for ci in range(NCH)])
            self.wait("scalar", self.pe, rmm)
            if dj >= 2:
                self.wait("scalar", self.dve, rwkv_cnt[dj - 2])
            self.inc("scalar", lambda e, b=bank, dj=dj: e.activation(
                out=self.sg[:, dj % 2, :], in_=rotps[b][:, :], func=AF.Sigmoid), self.act)
            self.rot3[bank] = (self.act, self.act.n)
            self.gidx += 1
            self.wait("vector", self.act, self.act.n)
            self.inc("vector", lambda e, dj=dj: e.tensor_tensor(
                out=self.rwkvb[:, dj, :], in0=self.sg[:, dj % 2, :], in1=self.yt[:, dj, :], op=ALU.mult), self.dve)
            rwkv_cnt[dj] = self.dve.n
        x1_evac_last = 0
        for dj in range(NCH):
            bank = self.gidx % 3
            self.waitg("tensor", self.rot3[bank])
            if dj == 0:
                self.wait("tensor", self.dve, rwkv_cnt[NCH - 1])
            womm = self.mmgroup(rotps[bank][:, :],
                                [(self.wob[:, ci, dj * 128:(dj + 1) * 128], self.rwkvb[:, ci, :])
                                 for ci in range(NCH)])
            self.wait("vector", self.pe, womm)
            slot = dj % 2
            if slot in self.x1wr_slot:
                self.wait("vector", qs, self.x1wr_slot[slot])
            self.inc("vector", lambda e, b=bank, dj=dj, buf=buf: e.tensor_tensor(
                out=self.x1c[:, dj % 2, :], in0=rotps[b][:, :], in1=self.xs[:, dj, buf, :], op=ALU.add), self.dve)
            x1_evac_last = self.dve.n
            self.rot3[bank] = (self.dve, self.dve.n)
            self.gidx += 1
            self.wait("sync", self.dve, x1_evac_last)
            self.x1wr_slot[slot] = self.dma(
                "sync", self.x1d[dj * 128:(dj + 1) * 128, s * T:(s + 1) * T],
                self.x1c[:, dj % 2, :], qs)
        self.xsg[buf] = [(self.dve, x1_evac_last)]
        self.mix_guard = (self.pe, self.pe.n)

    def tern_pass1(self, which, chunks):
        cols = self.colsA if which == 1 else self.colsB
        for src, col in chunks:
            slot = self.tern_slot
            self.tern_slot = (self.tern_slot + 1) % 2
            self.waitg("sync", self.stage_guard[slot])
            ld = self.dma("sync", self.stg[:, slot, :], src, self.qs)
            self.wait("vector", self.qs, ld)
            self.inc("vector", lambda e, slot=slot, cols=cols, col=col: e.tensor_reduce(
                out=cols[:, col:col + 1], in_=self.stg[:, slot, :], axis=AX.X,
                op=ALU.add, apply_absolute_value=True), self.dve)
            self.stage_guard[slot] = (self.dve, self.dve.n)

    def tern_scale(self, which, nelem):
        cols = self.colsA if which == 1 else self.colsB
        thr = self.thr1 if which == 1 else self.thr2
        nthr = self.nthr1 if which == 1 else self.nthr2
        self.inc("vector", lambda e, cols=cols: e.tensor_reduce(
            out=self.colred[:, :], in_=cols[:, :], axis=AX.X, op=ALU.add), self.dve)
        self.wait("tensor", self.dve, self.dve.n)
        self.wait("tensor", self.act, self.act.n)
        tot = self.mmgroup(self.psE[0:1, 0:1], [(self.colred[:, :], self.onescol[:, :])])
        self.wait("scalar", self.pe, tot)
        self.inc("scalar", lambda e, n=nelem: e.activation(
            out=self.trow[:, :], in_=self.psE[0:1, 0:1], func=AF.Copy, scale=1.0 / n), self.act)
        self.wait("vector", self.act, self.act.n)
        self.inc("vector", lambda e: e.tensor_single_scalar(
            out=self.trow[:, :], in_=self.trow[:, :], scalar=1e-8, op=ALU.max), self.dve)
        self.wait("tensor", self.dve, self.dve.n)
        bc = self.mmgroup(self.psE[:, 0:1], [(self.onesf[:, :], self.trow[:, :])])
        self.wait("scalar", self.pe, bc)
        self.op("scalar", lambda e, thr=thr: e.activation(out=thr[:, :], in_=self.psE[:, 0:1], func=AF.Copy, scale=0.5))
        self.inc("scalar", lambda e, nthr=nthr: e.activation(out=nthr[:, :], in_=self.psE[:, 0:1], func=AF.Copy, scale=-0.5), self.act)
        if which == 1:
            self.thr1_done = self.act.n
        else:
            self.thr2_done = self.act.n

    def tern_stream(self, which, chunks, on_pool=False):
        thr = self.thr1 if which == 1 else self.thr2
        nthr = self.nthr1 if which == 1 else self.nthr2
        thr_done = self.thr1_done if which == 1 else self.thr2_done
        done = []
        meng = "gpsimd" if on_pool else "vector"
        msem = self.pl if on_pool else self.dve
        for src, dst in chunks:
            slot = self.tern_slot
            self.tern_slot = (self.tern_slot + 1) % 2
            self.waitg("sync", self.stage_guard[slot])
            ld = self.dma("sync", self.stg[:, slot, :], src, self.qs)
            self.wait(meng, self.qs, ld)
            self.wait(meng, self.act, thr_done)
            self.inc(meng, lambda e, slot=slot, nthr=nthr: e.tensor_single_scalar(
                out=self.mn[:, slot, :], in_=self.stg[:, slot, :], scalar=nthr[:, 0:1], op=ALU.is_lt), msem)
            if on_pool:
                self.wait("vector", self.pl, self.pl.n)
            self.inc("vector", lambda e, slot=slot, thr=thr, dst=dst: e.scalar_tensor_tensor(
                out=dst, in0=self.stg[:, slot, :], scalar=thr[:, 0:1],
                in1=self.mn[:, slot, :], op0=ALU.is_gt, op1=ALU.subtract), self.dve)
            self.stage_guard[slot] = (self.dve, self.dve.n)
            done.append(self.dve.n)
        return done

    def phase_b(self, w1p2, w2p2):
        qs = self.qs
        for s in (self.pe, self.act, self.dve, self.pl, self.qg, self.qs):
            self.wait("sync", s, s.n)
        bw = qs.n
        for e in ("scalar", "tensor", "vector", "gpsimd"):
            self.wait(e, qs, bw)
        w1done = self.tern_stream(1, w1p2)
        w2done = self.tern_stream(2, w2p2, on_pool=True)
        w1_all = w1done[-1]
        w2_all = w2done[-1]
        gh_guard = None
        rot2 = [None, None]
        outbank = [None] * 4
        outwr_slot = {}
        x1_guard = {}
        hg_guard = None
        for u in range(NU):
            buf = u % 2
            if buf in x1_guard:
                self.waitg("sync", x1_guard[buf])
            x1in = self.x1d.rearrange("(g p) t -> p g t", p=128)[:, :, u * U:(u + 1) * U]
            xl = self.dma("sync", self.x1s[:, :, buf, :], x1in, qs)
            self.wait("scalar", qs, xl)
            stats_mm = []
            for j in range(NCH):
                if j >= 2:
                    self.wait("scalar", self.pe, stats_mm[j - 2])
                self.op("scalar", lambda e, j=j, buf=buf: e.activation(
                    out=self.xb2[:, j % 2, :], in_=self.x1s[:, j, buf, :], func=AF.Copy))
                self.inc("scalar", lambda e, j=j, buf=buf: e.activation(
                    out=self.xsq2[:, j % 2, :], in_=self.x1s[:, j, buf, :], func=AF.Square), self.act)
                self.wait("tensor", self.act, self.act.n)
                if j == 0 and gh_guard is not None:
                    self.waitg("tensor", gh_guard)
                stG, spG = (j == 0), (j == NCH - 1)
                self.op("tensor", lambda e, j=j, st=stG, sp=spG: e.matmul(
                    self.psG[0:1, 0:U], self.onesb[:, :], self.xb2[:, j % 2, :], start=st, stop=sp))
                self.inc("tensor", lambda e, j=j, st=stG, sp=spG: e.matmul(
                    self.psH[0:1, 0:U], self.onesb[:, :], self.xsq2[:, j % 2, :], start=st, stop=sp), self.pe)
                stats_mm.append(self.pe.n)
            self.wait("vector", self.pe, self.pe.n)
            rw = self.rows
            self.op("vector", lambda e: e.tensor_single_scalar(out=rw[:, 0, 0:U], in_=self.psG[0:1, 0:U], scalar=1.0 / C, op=ALU.mult))
            self.op("vector", lambda e: e.tensor_single_scalar(out=rw[:, 1, 0:U], in_=self.psH[0:1, 0:U], scalar=1.0 / C, op=ALU.mult))
            self.op("vector", lambda e: e.tensor_tensor(out=rw[:, 2, 0:U], in0=rw[:, 0, 0:U], in1=rw[:, 0, 0:U], op=ALU.mult))
            self.op("vector", lambda e: e.tensor_tensor(out=rw[:, 1, 0:U], in0=rw[:, 1, 0:U], in1=rw[:, 2, 0:U], op=ALU.subtract))
            self.inc("vector", lambda e: e.tensor_single_scalar(out=rw[:, 1, 0:U], in_=rw[:, 1, 0:U], scalar=EPS, op=ALU.add), self.dve)
            self.wait("scalar", self.dve, self.dve.n)
            self.op("scalar", lambda e: e.activation(out=rw[:, 2, 0:U], in_=rw[:, 1, 0:U], func=AF.Ln))
            self.inc("scalar", lambda e: e.activation(out=rw[:, 3, 0:U], in_=rw[:, 2, 0:U], func=AF.Exp, scale=-0.5), self.act)
            self.wait("vector", self.act, self.act.n)
            self.inc("vector", lambda e: e.tensor_tensor(out=self.mrrow[:, 0:U], in0=rw[:, 0, 0:U], in1=rw[:, 3, 0:U], op=ALU.mult), self.dve)
            self.wait("tensor", self.dve, self.dve.n)
            self.mmgroup(self.psG[:, 0:U], [(self.onesf[:, :], rw[:, 3, 0:U])])
            bc_done = self.mmgroup(self.psH[:, 0:U], [(self.onesf[:, :], self.mrrow[:, 0:U])])
            self.wait("scalar", self.pe, bc_done)
            if self.rb_guard is not None:
                self.waitg("scalar", self.rb_guard)
            self.op("scalar", lambda e: e.activation(out=self.Rb[:, 0:U], in_=self.psG[:, 0:U], func=AF.Copy))
            self.inc("scalar", lambda e: e.activation(out=self.MRb[:, 0:U], in_=self.psH[:, 0:U], func=AF.Copy), self.act)
            gh_guard = (self.act, self.act.n)
            self.wait("vector", self.act, self.act.n)
            if hg_guard is not None:
                self.waitg("vector", hg_guard)
            for j in range(NCH):
                self.op("vector", lambda e, j=j, buf=buf: e.tensor_tensor(
                    out=self.xn2f[:, :], in0=self.x1s[:, j, buf, :], in1=self.Rb[:, 0:U], op=ALU.mult))
                self.op("vector", lambda e, j=j: e.tensor_tensor(
                    out=self.xn2b[:, j, :], in0=self.xn2f[:, :], in1=self.MRb[:, 0:U], op=ALU.subtract))
            self.inc("vector", lambda e: e.engine_nop(), self.dve)
            xn2_done = self.dve.n
            self.rb_guard = (self.dve, xn2_done)
            self.wait("tensor", self.dve, xn2_done)
            if u == 0:
                self.wait("tensor", self.dve, w1_all)
            rotps = [self.psA, self.psB]
            for fj in range(NFF):
                bank = fj % 2
                if rot2[bank] is not None:
                    self.waitg("tensor", rot2[bank])
                hmm = self.mmgroup(rotps[bank][:, 0:U],
                                   [(self.w1tb[:, ci, fj * 128:(fj + 1) * 128], self.xn2b[:, ci, :])
                                    for ci in range(NCH)])
                self.wait("scalar", self.pe, hmm)
                if fj == 0 and hg_guard is not None:
                    self.waitg("scalar", hg_guard)
                self.inc("scalar", lambda e, b=bank, fj=fj: e.activation(
                    out=self.hgb[:, fj, :], in_=rotps[b][:, 0:U], func=AF.Gelu,
                    bias=self.b1v[:, fj:fj + 1]), self.act)
                rot2[bank] = (self.act, self.act.n)
            gelu_all = self.act.n
            outps = [self.psC, self.psD, self.psE, self.psF]
            self.wait("tensor", self.act, gelu_all)
            if u == 0:
                self.wait("tensor", self.dve, w2_all)
            for w in range(2):
                for q in range(4):
                    dj = 4 * w + q
                    if outbank[q] is not None:
                        self.waitg("tensor", outbank[q])
                    omm = self.mmgroup(outps[q][:, 0:U],
                                       [(self.w2tb[:, fi, dj * 128:(dj + 1) * 128], self.hgb[:, fi, :])
                                        for fi in range(NFF)])
                    self.wait("vector", self.pe, omm)
                    slot = dj % 2
                    if slot in outwr_slot:
                        self.wait("vector", qs, outwr_slot[slot])
                    self.inc("vector", lambda e, q=q, dj=dj, buf=buf: e.scalar_tensor_tensor(
                        out=self.outc[:, dj % 2, :], in0=outps[q][:, 0:U], scalar=self.b2v[:, dj:dj + 1],
                        in1=self.x1s[:, dj, buf, :], op0=ALU.add, op1=ALU.add), self.dve)
                    outbank[q] = (self.dve, self.dve.n)
                    self.wait("sync", self.dve, self.dve.n)
                    outwr_slot[slot] = self.dma(
                        "sync", self.outT[dj * 128:(dj + 1) * 128, u * U:(u + 1) * U],
                        self.outc[:, dj % 2, :], qs)
            hg_guard = (self.pe, self.pe.n)
            x1_guard[buf] = (self.dve, self.dve.n)
        self.wait("sync", qs, qs.n)
        self.wait("sync", self.qg, self.qg.n)

    def build(self):
        self.init_phase()
        w1p1 = [(self.w1[ci * 128:(ci + 1) * 128, fo * T:(fo + 1) * T], ci * (FF // T) + fo)
                for ci in range(NCH) for fo in range(FF // T)]
        w2p1 = [(self.w2[fi * 128:(fi + 1) * 128, do * T:(do + 1) * T], fi * (C // T) + do)
                for fi in range(NFF) for do in range(C // T)]
        w1p2 = [(self.w1[ci * 128:(ci + 1) * 128, fo * T:(fo + 1) * T],
                 self.w1tb[:, ci, fo * T:(fo + 1) * T])
                for ci in range(NCH) for fo in range(FF // T)]
        w2p2 = [(self.w2[fi * 128:(fi + 1) * 128, do * T:(do + 1) * T],
                 self.w2tb[:, fi, do * T:(do + 1) * T])
                for fi in range(NFF) for do in range(C // T)]

        self.tiny_stats(self.xpcP, wait_q=self.vec_loads)
        for s in range(NS):
            self.strip(s, self.xp, True)
            self.tern_pass1(1, w1p1[s * 8:(s + 1) * 8])
            self.tern_pass1(2, w2p1[s * 8:(s + 1) * 8])
        self.tern_pass1(1, w1p1[32:])
        self.tern_pass1(2, w2p1[32:])
        self.tern_scale(1, C * FF)
        self.tern_scale(2, C * FF)
        self.wait("vector", self.act, self.mk_done)
        self.op("vector", lambda e: e.tensor_single_scalar(
            out=self.cN[:, :], in_=self.cN[:, :], scalar=self.mkcol[:, 0:1], op=ALU.mult))
        self.inc("vector", lambda e: e.tensor_single_scalar(
            out=self.cD[:, :], in_=self.cD[:, :], scalar=self.mkcol[:, 0:1], op=ALU.mult), self.dve)
        self.mix_guard = (self.pe, self.pe.n)
        self.tiny_stats(self.xpcM)
        for s in range(NS):
            self.strip(s, self.xm, False)
        self.phase_b(w1p2, w2p2)

        nc = self.nc
        with nc.Block() as blk:
            for name, meth in (("sync", blk.sync), ("gpsimd", blk.gpsimd),
                               ("scalar", blk.scalar), ("vector", blk.vector),
                               ("tensor", blk.tensor)):
                thunks = self.prog[name]
                meth(lambda eng, thunks=thunks: [f(eng) for f in thunks])
        return nc


_NC_CACHE = {}


def _get_nc():
    if "nc" not in _NC_CACHE:
        _NC_CACHE["nc"] = Build().build()
    return _NC_CACHE["nc"]


def kernel(L0_attn_x, L0_attn_time_decay, L0_attn_time_first, L0_attn_time_mix_k,
           L0_attn_time_mix_v, L0_attn_time_mix_r, L0_attn_Wk, L0_attn_Wv,
           L0_attn_Wr, L0_attn_Wo, L0_ffn_W1, L0_ffn_b1, L0_ffn_W2, L0_ffn_b2):
    x = np.ascontiguousarray(np.asarray(L0_attn_x, dtype=np.float32))
    assert x.shape == (4, 4096, 1024)
    ac = np.ascontiguousarray

    def rv(v, n):
        return ac(np.asarray(v, np.float32).reshape(n, 128).T)

    shared = dict(
        wk=ac(np.asarray(L0_attn_Wk, np.float32).T),
        wv=ac(np.asarray(L0_attn_Wv, np.float32).T),
        wr=ac(np.asarray(L0_attn_Wr, np.float32).T),
        wo=ac(np.asarray(L0_attn_Wo, np.float32).T),
        w1=ac(np.asarray(L0_ffn_W1, np.float32).T),
        w2=ac(np.asarray(L0_ffn_W2, np.float32).T),
        tmkd=rv(L0_attn_time_mix_k, 8), tmvd=rv(L0_attn_time_mix_v, 8),
        tmrd=rv(L0_attn_time_mix_r, 8), tdd=rv(L0_attn_time_decay, 8),
        tfd=rv(L0_attn_time_first, 8), b1d=rv(L0_ffn_b1, 32), b2d=rv(L0_ffn_b2, 8))
    in_maps = []
    for c in range(8):
        b, h = c // 2, c % 2
        xmT = np.empty((1024, TLOC + 1), np.float32)
        xmT[:, 0] = 0.0 if h == 0 else x[b, TLOC - 1, :]
        xmT[:, 1:] = x[b, h * TLOC:(h + 1) * TLOC, :].T
        xpT = np.empty((1024, TLOC + 1), np.float32)
        xpT[:, 0] = 0.0
        xpT[:, 1:] = x[b, 0:TLOC, :].T
        m = dict(shared)
        m["xm"] = ac(xmT)
        m["xp"] = ac(xpT)
        m["msk"] = np.full((1, 1), float(h), np.float32)
        in_maps.append(m)

    nc = _get_nc()
    res = run_bass_kernel_spmd(nc, in_maps, core_ids=list(range(8)))
    out = np.empty((4, 4096, 1024), np.float32)
    for c in range(8):
        b, h = c // 2, c % 2
        out[b, h * TLOC:(h + 1) * TLOC, :] = np.asarray(res.results[c]["outT"]).T
    return out


# version 09
# speedup vs baseline: 1.5475x; 1.2389x over previous
import numpy as np
import concourse.bass as bass
import concourse.mybir as mybir
from concourse.bass_utils import run_bass_kernel_spmd

F32 = mybir.dt.float32
BF16 = mybir.dt.bfloat16
AF = mybir.ActivationFunctionType
ALU = mybir.AluOpType
AX = mybir.AxisListType

DEBUG_X1 = False
DEBUG_XN = False
DEBUG_Y = False
DEBUG_W = False
C = 1024
NCH = 8
FF = 4096
NFF = 32
T = 512
NS = 4
U = 256
NU = 8
TLOC = 2048
EPS = 1e-5


class Sem:
    def __init__(self, h):
        self.h = h
        self.n = 0


class Arena:
    def __init__(self, nc, size):
        self.nc = nc
        self.size = size
        self.ctx = nc.sbuf_tensor("arena", [128, size], mybir.dt.uint8)
        self.big = self.ctx.__enter__()
        self.base = nc.lookup_mloc(self.big).addr
        self.cur = 0

    def alloc(self, name, shape, dtype, at=None):
        per = int(np.prod(shape[1:])) * mybir.dt.size(dtype)
        off = self.cur if at is None else at
        off = (off + 63) & ~63
        assert off + per <= self.size, (name, off, per, self.size)
        h = self.nc.virtual_sbuf_tensor(name, shape, dtype)
        self.nc._tensor(h.name, h.shape, h.dtype, type="SB")
        m = self.nc.lookup_mloc(h)
        m.allocated = True
        m.addr = self.base + off
        if at is None:
            self.cur = off + per
        return h, off


class Build:
    ENGS = ("sync", "gpsimd", "scalar", "vector", "tensor")

    def __init__(self):
        nc = bass.Bass("TRN2", target_bir_lowering=False, debug=False)
        self.nc = nc
        self.prog = {e: [] for e in self.ENGS}
        d = nc.dram_tensor
        self.xm = d("xm", [C, TLOC + 1], F32, kind="ExternalInput").ap()
        self.xp = d("xp", [C, TLOC + 1], F32, kind="ExternalInput").ap()
        self.msk = d("msk", [1, 1], F32, kind="ExternalInput").ap()
        self.wk = d("wk", [C, C], F32, kind="ExternalInput").ap()
        self.wv = d("wv", [C, C], F32, kind="ExternalInput").ap()
        self.wr = d("wr", [C, C], F32, kind="ExternalInput").ap()
        self.wo = d("wo", [C, C], F32, kind="ExternalInput").ap()
        self.w1 = d("w1", [C, FF], F32, kind="ExternalInput").ap()
        self.w2 = d("w2", [FF, C], F32, kind="ExternalInput").ap()
        self.tmkd = d("tmkd", [128, NCH], F32, kind="ExternalInput").ap()
        self.tmvd = d("tmvd", [128, NCH], F32, kind="ExternalInput").ap()
        self.tmrd = d("tmrd", [128, NCH], F32, kind="ExternalInput").ap()
        self.tdd = d("tdd", [128, NCH], F32, kind="ExternalInput").ap()
        self.tfd = d("tfd", [128, NCH], F32, kind="ExternalInput").ap()
        self.b1d = d("b1d", [128, NFF], F32, kind="ExternalInput").ap()
        self.b2d = d("b2d", [128, NCH], F32, kind="ExternalInput").ap()
        self.outT = d("outT", [C, TLOC], F32, kind="ExternalOutput").ap()
        self.x1d = d("x1d", [C, TLOC], F32,
                     kind="ExternalOutput" if DEBUG_X1 else "Internal").ap()
        self.xnd = (d("xnd", [C, T + 1], F32, kind="ExternalOutput").ap()
                    if DEBUG_XN else None)
        self.ydbg = (d("ydbg", [C, T], F32, kind="ExternalOutput").ap()
                     if DEBUG_Y else None)
        self.kdbg = (d("kdbg", [C, T], F32, kind="ExternalOutput").ap()
                     if DEBUG_Y else None)
        self.rkdbg = (d("rkdbg", [C, T], F32, kind="ExternalOutput").ap()
                      if DEBUG_Y else None)
        self.sgdbg = (d("sgdbg", [C, T], F32, kind="ExternalOutput").ap()
                      if DEBUG_Y else None)
        self.w1dbg = (d("w1dbg", [C, FF], BF16, kind="ExternalOutput").ap()
                      if DEBUG_W else None)
        self.w2dbg = (d("w2dbg", [FF, C], BF16, kind="ExternalOutput").ap()
                      if DEBUG_W else None)

        self.qs = Sem(nc.alloc_semaphore("qs"))
        self.qg = Sem(nc.alloc_semaphore("qg"))
        self.qwc = Sem(nc.alloc_semaphore("qwc"))
        self.qx = [Sem(nc.alloc_semaphore("qx0")), Sem(nc.alloc_semaphore("qx1"))]
        self.qw = [Sem(nc.alloc_semaphore("qw0")), Sem(nc.alloc_semaphore("qw1"))]
        self.qt = [Sem(nc.alloc_semaphore("qt0")), Sem(nc.alloc_semaphore("qt1")),
                   Sem(nc.alloc_semaphore("qt2"))]
        self.pe = Sem(nc.alloc_semaphore("pe"))
        self.act = Sem(nc.alloc_semaphore("act"))
        self.dve = Sem(nc.alloc_semaphore("dve"))
        self.pl = Sem(nc.alloc_semaphore("pl"))

        self._alloc_sbuf()
        p = nc.alloc_psum_tensor
        self.psA = p("psA", [128, T], F32)
        self.psB = p("psB", [128, T], F32)
        self.psC = p("psC", [128, T], F32)
        self.psD = p("psD", [128, T], F32)
        self.psE = p("psE", [128, T], F32)
        self.psF = p("psF", [128, T], F32)
        self.psG = p("psG", [128, T], F32)
        self.psH = p("psH", [128, T], F32)

        self.rot3 = [None, None, None]
        self.gidx = 0
        self.stage_guard = [None, None, None]
        self.tern_slot = 0
        self.xsg = {0: [], 1: []}
        self.x1wr_slot = {}
        self.gh_guard = None
        self.xn_guard = []
        self.mix_guard = None
        self.xnlast_src = None
        self.rb_guard = None

    def _alloc_sbuf(self):
        a = Arena(self.nc, 200 * 1024)
        self.ar = a

        def al(name, shape, dtype, at=None):
            h, off = a.alloc(name, shape, dtype, at)
            setattr(self, name, h)
            return off
        al("tmk", [128, NCH], F32); al("tmv", [128, NCH], F32)
        al("tmr", [128, NCH], F32); al("tdr", [128, NCH], F32)
        al("tfr", [128, NCH], F32); al("alpha", [128, NCH], F32)
        al("eu", [128, NCH], F32); al("b1v", [128, NFF], F32)
        al("b2v", [128, NCH], F32); al("onesb", [128, 1], BF16)
        al("onesf", [1, 128], F32); al("onescol", [128, 1], F32)
        al("msksb", [1, 1], F32); al("mkcol", [128, 1], F32)
        al("mcol", [128, 1], F32); al("rcol", [128, 1], F32)
        al("thr1", [128, 1], F32); al("nthr1", [128, 1], F32)
        al("thr2", [128, 1], F32); al("nthr2", [128, 1], F32)
        al("trow", [1, 1], F32); al("cN", [128, NCH], F32)
        al("cD", [128, NCH], F32)
        al("xpcP", [128, NCH, 1], F32); al("xpcM", [128, NCH, 1], F32)
        al("xpcB", [128, NCH, 1], BF16)
        al("colsA", [128, 64], F32); al("colsB", [128, 64], F32)
        al("colred", [128, 1], F32); al("tiny", [1, 8], F32)
        al("xnlast", [128, NCH, 1], F32)
        al("rows", [1, 4, T], F32); al("mrrow", [1, T], F32)
        al("Rb", [128, T], F32); al("MRb", [128, T], F32)
        al("stg", [128, 3, T], F32); al("mn", [128, 3, T], F32)
        wbase = a.cur
        al("wkb", [128, NCH, C], BF16); al("wvb", [128, NCH, C], BF16)
        al("wrb", [128, NCH, C], BF16); al("wob", [128, NCH, C], BF16)
        al("w1tb", [128, NCH, FF], BF16, at=wbase)
        sbase = a.cur
        al("xs", [128, NCH, 2, T], F32)
        xnoff = al("xn", [128, NCH, T + 1], F32)
        xyoff = al("xkb", [128, NCH, T], BF16)
        al("xvb", [128, NCH, T], BF16)
        al("xrb", [128, NCH, T], BF16)
        al("arot", [128, 2, T], F32); al("avrot", [128, 2, T], F32)
        al("N1", [128, T + 1], F32); al("D1", [128, T + 1], F32)
        al("xbr", [128, 4, T], BF16); al("xsqr", [128, 4, T], BF16)
        al("dd", [128, T], F32); al("numt", [128, T], F32)
        al("dent", [128, T], F32)
        al("sg", [128, 2, T], F32); al("x1c", [128, 2, T], F32)
        a_end = a.cur
        al("yt", [128, NCH, T], F32, at=xnoff)
        al("rwkvb", [128, NCH, T], BF16, at=xyoff)
        al("w2tb", [128, NFF, C], BF16, at=sbase)
        o = sbase + NFF * C * 2
        o = al("x1s", [128, NCH, 2, U], F32, at=o) + NCH * 2 * U * 4
        o = al("xn2b", [128, NCH, U], BF16, at=o) + NCH * U * 2
        o = al("hgb", [128, NFF, U], BF16, at=o) + NFF * U * 2
        o = al("xb2", [128, 2, U], BF16, at=o) + 2 * U * 2
        o = al("xsq2", [128, 2, U], BF16, at=o) + 2 * U * 2
        o = al("xn2f", [128, U], F32, at=o) + U * 4
        o = al("outc", [128, 2, U], F32, at=o) + 2 * U * 4
        assert max(a_end, o) <= 200 * 1024, (a_end, o)

    def emit(self, e, fn):
        self.prog[e].append(fn)

    def wait(self, e, s, val=None):
        v = s.n if val is None else val
        h = s.h
        if v > 0:
            self.emit(e, lambda eng, h=h, v=v: eng.wait_ge(h, v))

    def waitg(self, e, g):
        if g:
            self.wait(e, g[0], g[1])

    def inc(self, e, fn, s, k=1):
        s.n += k
        h = s.h
        self.emit(e, lambda eng, fn=fn, h=h, k=k: fn(eng).then_inc(h, k))
        return s.n

    def op(self, e, fn):
        self.emit(e, lambda eng, fn=fn: fn(eng))

    def dma(self, e, out, in_, s, slow=False):
        if slow:
            return self.inc(e, lambda eng, o=out, i=in_: eng.dma_start(
                out=o, in_=i, allow_slow_non_contiguous=True), s, 16)
        return self.inc(e, lambda eng, o=out, i=in_: eng.dma_start(out=o, in_=i), s, 16)

    def mmgroup(self, ps_ap, pairs):
        n = len(pairs)
        for i, (lhsT, rhs) in enumerate(pairs):
            st, sp = (i == 0), (i == n - 1)
            if sp:
                self.inc("tensor", lambda eng, o=ps_ap, l=lhsT, r=rhs, st=st, sp=sp:
                         eng.matmul(o, l, r, start=st, stop=sp), self.pe)
            else:
                self.op("tensor", lambda eng, o=ps_ap, l=lhsT, r=rhs, st=st, sp=sp:
                        eng.matmul(o, l, r, start=st, stop=sp))
        return self.pe.n

    def init_phase(self):
        qs, qg = self.qs, self.qg
        for dst, src in ((self.tmk, self.tmkd), (self.tmv, self.tmvd),
                         (self.tmr, self.tmrd), (self.tdr, self.tdd),
                         (self.tfr, self.tfd), (self.b1v, self.b1d),
                         (self.b2v, self.b2d), (self.msksb, self.msk)):
            self.dma("sync", dst[:, :], src, qs)
        self.dma("sync", self.xpcP[:, :, :],
                 self.xp[:, 0:1].rearrange("(g p) t -> p g t", p=128), qs, slow=True)
        self.dma("sync", self.xpcM[:, :, :],
                 self.xm[:, 0:1].rearrange("(g p) t -> p g t", p=128), qs, slow=True)
        self.vec_loads = qs.n
        for wdst, wsrc in ((self.wkb, self.wk), (self.wvb, self.wv),
                           (self.wrb, self.wr), (self.wob, self.wo)):
            for ci in range(NCH):
                self.dma("gpsimd", wdst[:, ci, :], wsrc[ci * 128:(ci + 1) * 128, :], self.qwc)
        self.w_loaded = self.qwc.n
        self.op("vector", lambda e: e.memset(self.onesb[:, :], 1.0))
        self.op("vector", lambda e: e.memset(self.onesf[:, :], 1.0))
        self.op("vector", lambda e: e.memset(self.onescol[:, :], 1.0))
        self.op("vector", lambda e: e.memset(self.cN[:, :], 0.0))
        self.inc("vector", lambda e: e.memset(self.cD[:, :], 0.0), self.dve)
        self.memset_done = self.dve.n
        self.wait("scalar", qs, self.vec_loads)
        self.op("scalar", lambda e: e.activation(out=self.alpha[:, :], in_=self.tdr[:, :], func=AF.Exp))
        self.op("scalar", lambda e: e.activation(out=self.alpha[:, :], in_=self.alpha[:, :], func=AF.Exp, scale=-1.0))
        self.inc("scalar", lambda e: e.activation(out=self.eu[:, :], in_=self.tfr[:, :], func=AF.Exp), self.act)
        self.wait("tensor", self.dve, self.memset_done)
        self.wait("tensor", qs, self.vec_loads)
        mk_mm = self.mmgroup(self.psE[:, 0:1], [(self.onesf[:, :], self.msksb[:, :])])
        self.wait("scalar", self.pe, mk_mm)
        self.inc("scalar", lambda e: e.activation(out=self.mkcol[:, :], in_=self.psE[:, 0:1], func=AF.Copy), self.act)
        self.mk_done = self.act.n

    def tiny_stats(self, xpc, wait_q=None):
        if wait_q is not None:
            self.wait("scalar", self.qs, wait_q)
        self.inc("scalar", lambda e: e.activation(out=self.xpcB[:, :, :], in_=xpc[:, :, :], func=AF.Copy), self.act)
        self.wait("tensor", self.act, self.act.n)
        mmx = self.mmgroup(self.psF[0:1, 0:1],
                           [(self.onesb[:, :], self.xpcB[:, j, :]) for j in range(NCH)])
        self.wait("scalar", self.pe, mmx)
        self.inc("scalar", lambda e: e.activation(out=self.xpcB[:, :, :], in_=xpc[:, :, :], func=AF.Square), self.act)
        self.wait("tensor", self.act, self.act.n)
        self.mmgroup(self.psD[0:1, 0:1],
                     [(self.onesb[:, :], self.xpcB[:, j, :]) for j in range(NCH)])
        self.wait("vector", self.pe, self.pe.n)
        ti = self.tiny
        self.op("vector", lambda e: e.tensor_single_scalar(out=ti[:, 0:1], in_=self.psF[0:1, 0:1], scalar=1.0 / C, op=ALU.mult))
        self.op("vector", lambda e: e.tensor_single_scalar(out=ti[:, 1:2], in_=self.psD[0:1, 0:1], scalar=1.0 / C, op=ALU.mult))
        self.op("vector", lambda e: e.tensor_tensor(out=ti[:, 2:3], in0=ti[:, 0:1], in1=ti[:, 0:1], op=ALU.mult))
        self.op("vector", lambda e: e.tensor_tensor(out=ti[:, 1:2], in0=ti[:, 1:2], in1=ti[:, 2:3], op=ALU.subtract))
        self.inc("vector", lambda e: e.tensor_single_scalar(out=ti[:, 1:2], in_=ti[:, 1:2], scalar=EPS, op=ALU.add), self.dve)
        self.wait("scalar", self.dve, self.dve.n)
        self.op("scalar", lambda e: e.activation(out=ti[:, 2:3], in_=ti[:, 1:2], func=AF.Ln))
        self.inc("scalar", lambda e: e.activation(out=ti[:, 3:4], in_=ti[:, 2:3], func=AF.Exp, scale=-0.5), self.act)
        self.wait("tensor", self.act, self.act.n)
        self.mmgroup(self.psF[:, 0:1], [(self.onesf[:, :], ti[:, 0:1])])
        mm3 = self.mmgroup(self.psD[:, 0:1], [(self.onesf[:, :], ti[:, 3:4])])
        self.wait("scalar", self.pe, mm3)
        self.op("scalar", lambda e: e.activation(out=self.mcol[:, :], in_=self.psF[:, 0:1], func=AF.Copy))
        self.inc("scalar", lambda e: e.activation(out=self.rcol[:, :], in_=self.psD[:, 0:1], func=AF.Copy), self.act)
        self.wait("vector", self.act, self.act.n)
        for j in range(NCH):
            fn = lambda e, j=j: e.scalar_tensor_tensor(
                out=self.xn[:, j, 0:1], in0=xpc[:, j, :], scalar=self.mcol[:, 0:1],
                in1=self.rcol[:, 0:1], op0=ALU.subtract, op1=ALU.mult)
            if j == NCH - 1:
                self.inc("vector", fn, self.dve)
            else:
                self.op("vector", fn)

    def strip(self, s, src, prefix):
        buf = s % 2
        c0 = 1 + s * T
        qs = self.qs
        for g in self.xsg[buf]:
            self.waitg("sync", g)
        xin = src.rearrange("(g p) t -> p g t", p=128)[:, :, c0:c0 + T]
        xl = self.dma("sync", self.xs[:, :, buf, :], xin, self.qx[buf])
        self.wait("scalar", self.qx[buf], xl)
        stats_mm = []
        for j in range(NCH):
            if j >= 4:
                self.wait("scalar", self.pe, stats_mm[j - 4])
            self.op("scalar", lambda e, j=j, buf=buf: e.activation(
                out=self.xbr[:, j % 4, :], in_=self.xs[:, j, buf, :], func=AF.Copy))
            self.inc("scalar", lambda e, j=j, buf=buf: e.activation(
                out=self.xsqr[:, j % 4, :], in_=self.xs[:, j, buf, :], func=AF.Square), self.act)
            self.wait("tensor", self.act, self.act.n)
            if j == 0:
                self.waitg("tensor", self.gh_guard)
            stG, spG = (j == 0), (j == NCH - 1)
            self.op("tensor", lambda e, j=j, st=stG, sp=spG: e.matmul(
                self.psG[0:1, :], self.onesb[:, :], self.xbr[:, j % 4, :], start=st, stop=sp))
            self.inc("tensor", lambda e, j=j, st=stG, sp=spG: e.matmul(
                self.psH[0:1, :], self.onesb[:, :], self.xsqr[:, j % 4, :], start=st, stop=sp), self.pe)
            stats_mm.append(self.pe.n)
        stats_done = self.pe.n
        self.wait("vector", self.pe, stats_done)
        rw = self.rows
        self.op("vector", lambda e: e.tensor_single_scalar(out=rw[:, 0, :], in_=self.psG[0:1, :], scalar=1.0 / C, op=ALU.mult))
        self.op("vector", lambda e: e.tensor_single_scalar(out=rw[:, 1, :], in_=self.psH[0:1, :], scalar=1.0 / C, op=ALU.mult))
        self.op("vector", lambda e: e.tensor_tensor(out=rw[:, 2, :], in0=rw[:, 0, :], in1=rw[:, 0, :], op=ALU.mult))
        self.op("vector", lambda e: e.tensor_tensor(out=rw[:, 1, :], in0=rw[:, 1, :], in1=rw[:, 2, :], op=ALU.subtract))
        self.inc("vector", lambda e: e.tensor_single_scalar(out=rw[:, 1, :], in_=rw[:, 1, :], scalar=EPS, op=ALU.add), self.dve)
        self.wait("scalar", self.dve, self.dve.n)
        self.op("scalar", lambda e: e.activation(out=rw[:, 2, :], in_=rw[:, 1, :], func=AF.Ln))
        self.inc("scalar", lambda e: e.activation(out=rw[:, 3, :], in_=rw[:, 2, :], func=AF.Exp, scale=-0.5), self.act)
        self.wait("vector", self.act, self.act.n)
        self.inc("vector", lambda e: e.tensor_tensor(out=self.mrrow[:, :], in0=rw[:, 0, :], in1=rw[:, 3, :], op=ALU.mult), self.dve)
        self.wait("tensor", self.dve, self.dve.n)
        self.mmgroup(self.psG[:, :], [(self.onesf[:, :], rw[:, 3, :])])
        bc_done = self.mmgroup(self.psH[:, :], [(self.onesf[:, :], self.mrrow[:, :])])
        self.wait("scalar", self.pe, bc_done)
        if self.rb_guard is not None:
            self.waitg("scalar", self.rb_guard)
        self.op("scalar", lambda e: e.activation(out=self.Rb[:, :], in_=self.psG[:, :], func=AF.Copy))
        self.inc("scalar", lambda e: e.activation(out=self.MRb[:, :], in_=self.psH[:, :], func=AF.Copy), self.act)
        self.gh_guard = (self.act, self.act.n)
        self.wait("vector", self.act, self.act.n)
        if s > 0:
            self.op("vector", lambda e: e.tensor_copy(out=self.xn[:, :, 0:1], in_=self.xnlast[:, :, :]))
        self.waitg("vector", self.mix_guard)
        for j in range(NCH):
            self.op("vector", lambda e, j=j, buf=buf: e.tensor_tensor(
                out=self.xn[:, j, 1:T + 1], in0=self.xs[:, j, buf, :], in1=self.Rb[:, :], op=ALU.mult))
            self.op("vector", lambda e, j=j: e.tensor_tensor(
                out=self.xn[:, j, 1:T + 1], in0=self.xn[:, j, 1:T + 1], in1=self.MRb[:, :], op=ALU.subtract))
        self.inc("vector", lambda e: e.engine_nop(), self.dve)
        xn_done = self.dve.n
        self.rb_guard = (self.dve, xn_done)
        for j in range(NCH):
            self.op("vector", lambda e, j=j: e.tensor_tensor(
                out=self.dd[:, :], in0=self.xn[:, j, 1:T + 1], in1=self.xn[:, j, 0:T], op=ALU.subtract))
            self.op("vector", lambda e, j=j: e.scalar_tensor_tensor(
                out=self.xkb[:, j, :], in0=self.dd[:, :], scalar=self.tmk[:, j:j + 1],
                in1=self.xn[:, j, 0:T], op0=ALU.mult, op1=ALU.add))
            fnv = lambda e, j=j: e.scalar_tensor_tensor(
                out=self.xvb[:, j, :], in0=self.dd[:, :], scalar=self.tmv[:, j:j + 1],
                in1=self.xn[:, j, 0:T], op0=ALU.mult, op1=ALU.add)
            if prefix:
                self.inc("vector", fnv, self.dve)
            else:
                self.op("vector", fnv)
                self.inc("vector", lambda e, j=j: e.scalar_tensor_tensor(
                    out=self.xrb[:, j, :], in0=self.dd[:, :], scalar=self.tmr[:, j:j + 1],
                    in1=self.xn[:, j, 0:T], op0=ALU.mult, op1=ALU.add), self.dve)
        self.inc("vector", lambda e: e.tensor_copy(
            out=self.xnlast[:, :, :], in_=self.xn[:, :, T:T + 1]), self.dve)
        mixes_done = self.dve.n
        if DEBUG_XN and s == 0 and not prefix:
            self.wait("sync", self.dve, mixes_done)
            dbgwr = self.dma("sync", self.xnd.rearrange("(g p) t -> p g t", p=128),
                             self.xn[:, :, :], self.qs)
            self.wait("vector", self.qs, dbgwr)
        self.xnlast_src = (self.dve, mixes_done)
        self.wait("tensor", self.dve, mixes_done)
        if s == 0 and prefix:
            self.wait("tensor", self.qwc, self.w_loaded)
        rotps = [self.psA, self.psB, self.psC]
        dn_cnt = {}
        for dj in range(NCH):
            bank = self.gidx % 3
            self.waitg("tensor", self.rot3[bank])
            kmm = self.mmgroup(rotps[bank][:, :],
                               [(self.wkb[:, ci, dj * 128:(dj + 1) * 128], self.xkb[:, ci, :])
                                for ci in range(NCH)])
            self.wait("scalar", self.pe, kmm)
            if dj >= 2:
                self.wait("scalar", self.dve, dn_cnt[dj - 2])
            self.inc("scalar", lambda e, b=bank, dj=dj: e.activation(
                out=self.arot[:, dj % 2, :], in_=rotps[b][:, :], func=AF.Exp), self.act)
            a_ev = self.act.n
            self.rot3[bank] = (self.act, a_ev)
            self.gidx += 1
            if DEBUG_Y and s == 0 and not prefix:
                self.wait("sync", self.act, a_ev)
                kw_ = self.dma("sync", self.kdbg[dj * 128:(dj + 1) * 128, :],
                               self.arot[:, dj % 2, :], self.qs)
                self.wait("scalar", self.qs, kw_)
            bank = self.gidx % 3
            self.waitg("tensor", self.rot3[bank])
            vmm = self.mmgroup(rotps[bank][:, :],
                               [(self.wvb[:, ci, dj * 128:(dj + 1) * 128], self.xvb[:, ci, :])
                                for ci in range(NCH)])
            self.wait("vector", self.pe, vmm)
            self.wait("vector", self.act, a_ev)
            self.inc("vector", lambda e, b=bank, dj=dj: e.tensor_tensor(
                out=self.avrot[:, dj % 2, :], in0=rotps[b][:, :], in1=self.arot[:, dj % 2, :], op=ALU.mult), self.dve)
            self.rot3[bank] = (self.dve, self.dve.n)
            self.gidx += 1
            self.op("vector", lambda e, dj=dj: e.tensor_copy(out=self.N1[:, 0:1], in_=self.cN[:, dj:dj + 1]))
            self.op("vector", lambda e, dj=dj: e.tensor_copy(out=self.D1[:, 0:1], in_=self.cD[:, dj:dj + 1]))
            self.op("vector", lambda e, dj=dj: e.tensor_tensor_scan(
                out=self.N1[:, 1:T + 1], data0=self.alpha[:, dj:dj + 1].broadcast_to([128, T]),
                data1=self.avrot[:, dj % 2, :], initial=self.cN[:, dj:dj + 1], op0=ALU.mult, op1=ALU.add))
            self.op("vector", lambda e, dj=dj: e.tensor_tensor_scan(
                out=self.D1[:, 1:T + 1], data0=self.alpha[:, dj:dj + 1].broadcast_to([128, T]),
                data1=self.arot[:, dj % 2, :], initial=self.cD[:, dj:dj + 1], op0=ALU.mult, op1=ALU.add))
            self.op("vector", lambda e, dj=dj: e.tensor_copy(out=self.cN[:, dj:dj + 1], in_=self.N1[:, T:T + 1]))
            self.op("vector", lambda e, dj=dj: e.tensor_copy(out=self.cD[:, dj:dj + 1], in_=self.D1[:, T:T + 1]))
            if not prefix:
                self.op("vector", lambda e, dj=dj: e.scalar_tensor_tensor(
                    out=self.numt[:, :], in0=self.avrot[:, dj % 2, :], scalar=self.eu[:, dj:dj + 1],
                    in1=self.N1[:, 0:T], op0=ALU.mult, op1=ALU.add))
                self.op("vector", lambda e, dj=dj: e.scalar_tensor_tensor(
                    out=self.dent[:, :], in0=self.arot[:, dj % 2, :], scalar=self.eu[:, dj:dj + 1],
                    in1=self.D1[:, 0:T], op0=ALU.mult, op1=ALU.add))
                self.op("vector", lambda e: e.reciprocal(out=self.dent[:, :], in_=self.dent[:, :]))
                self.inc("vector", lambda e, dj=dj: e.tensor_tensor(
                    out=self.yt[:, dj, :], in0=self.numt[:, :], in1=self.dent[:, :], op=ALU.mult), self.dve)
            else:
                self.inc("vector", lambda e: e.engine_nop(), self.dve)
            dn_cnt[dj] = self.dve.n
        if DEBUG_Y and s == 0 and not prefix:
            self.wait("sync", self.dve, dn_cnt[NCH - 1])
            yw = self.dma("sync", self.ydbg.rearrange("(g p) t -> p g t", p=128),
                          self.yt[:, :, :], self.qs)
            self.wait("gpsimd", self.qs, yw)
        if prefix:
            self.xsg[buf] = [(self.dve, dn_cnt[NCH - 1])]
            return
        rwkv_cnt = {}
        for dj in range(NCH):
            bank = self.gidx % 3
            self.waitg("tensor", self.rot3[bank])
            rmm = self.mmgroup(rotps[bank][:, :],
                               [(self.wrb[:, ci, dj * 128:(dj + 1) * 128], self.xrb[:, ci, :])
                                for ci in range(NCH)])
            self.wait("scalar", self.pe, rmm)
            if dj >= 2:
                self.wait("scalar", self.dve, rwkv_cnt[dj - 2])
            self.inc("scalar", lambda e, b=bank, dj=dj: e.activation(
                out=self.sg[:, dj % 2, :], in_=rotps[b][:, :], func=AF.Sigmoid), self.act)
            self.rot3[bank] = (self.act, self.act.n)
            self.gidx += 1
            self.wait("vector", self.act, self.act.n)
            self.inc("vector", lambda e, dj=dj: e.tensor_tensor(
                out=self.rwkvb[:, dj, :], in0=self.sg[:, dj % 2, :], in1=self.yt[:, dj, :], op=ALU.mult), self.dve)
            rwkv_cnt[dj] = self.dve.n
        x1_evac_last = 0
        for dj in range(NCH):
            bank = self.gidx % 3
            self.waitg("tensor", self.rot3[bank])
            if dj == 0:
                self.wait("tensor", self.dve, rwkv_cnt[NCH - 1])
            womm = self.mmgroup(rotps[bank][:, :],
                                [(self.wob[:, ci, dj * 128:(dj + 1) * 128], self.rwkvb[:, ci, :])
                                 for ci in range(NCH)])
            self.wait("vector", self.pe, womm)
            slot = dj % 2
            if slot in self.x1wr_slot:
                self.wait("vector", self.qw[slot], self.x1wr_slot[slot])
            self.inc("vector", lambda e, b=bank, dj=dj, buf=buf: e.tensor_tensor(
                out=self.x1c[:, dj % 2, :], in0=rotps[b][:, :], in1=self.xs[:, dj, buf, :], op=ALU.add), self.dve)
            x1_evac_last = self.dve.n
            self.rot3[bank] = (self.dve, self.dve.n)
            self.gidx += 1
            self.wait("sync", self.dve, x1_evac_last)
            self.x1wr_slot[slot] = self.dma(
                "sync", self.x1d[dj * 128:(dj + 1) * 128, s * T:(s + 1) * T],
                self.x1c[:, dj % 2, :], self.qw[slot])
        self.xsg[buf] = [(self.dve, x1_evac_last)]
        self.mix_guard = (self.pe, self.pe.n)

    def tern_pass1(self, which, chunks):
        cols = self.colsA if which == 1 else self.colsB
        for src, col in chunks:
            slot = self.tern_slot
            self.tern_slot = (self.tern_slot + 1) % 3
            self.waitg("gpsimd", self.stage_guard[slot])
            ld = self.dma("gpsimd", self.stg[:, slot, :], src, self.qt[slot])
            self.wait("vector", self.qt[slot], ld)
            self.inc("vector", lambda e, slot=slot, cols=cols, col=col: e.tensor_reduce(
                out=cols[:, col:col + 1], in_=self.stg[:, slot, :], axis=AX.X,
                op=ALU.add, apply_absolute_value=True), self.dve)
            self.stage_guard[slot] = (self.dve, self.dve.n)

    def tern_scale(self, which, nelem):
        cols = self.colsA if which == 1 else self.colsB
        thr = self.thr1 if which == 1 else self.thr2
        nthr = self.nthr1 if which == 1 else self.nthr2
        self.inc("vector", lambda e, cols=cols: e.tensor_reduce(
            out=self.colred[:, :], in_=cols[:, :], axis=AX.X, op=ALU.add), self.dve)
        self.wait("tensor", self.dve, self.dve.n)
        self.wait("tensor", self.act, self.act.n)
        tot = self.mmgroup(self.psE[0:1, 0:1], [(self.colred[:, :], self.onescol[:, :])])
        self.wait("scalar", self.pe, tot)
        self.inc("scalar", lambda e, n=nelem: e.activation(
            out=self.trow[:, :], in_=self.psE[0:1, 0:1], func=AF.Copy, scale=1.0 / n), self.act)
        self.wait("vector", self.act, self.act.n)
        self.inc("vector", lambda e: e.tensor_single_scalar(
            out=self.trow[:, :], in_=self.trow[:, :], scalar=1e-8, op=ALU.max), self.dve)
        self.wait("tensor", self.dve, self.dve.n)
        bc = self.mmgroup(self.psE[:, 0:1], [(self.onesf[:, :], self.trow[:, :])])
        self.wait("scalar", self.pe, bc)
        self.op("scalar", lambda e, thr=thr: e.activation(out=thr[:, :], in_=self.psE[:, 0:1], func=AF.Copy, scale=0.5))
        self.inc("scalar", lambda e, nthr=nthr: e.activation(out=nthr[:, :], in_=self.psE[:, 0:1], func=AF.Copy, scale=-0.5), self.act)
        if which == 1:
            self.thr1_done = self.act.n
        else:
            self.thr2_done = self.act.n

    def tern_stream(self, which, chunks, on_pool=False):
        thr = self.thr1 if which == 1 else self.thr2
        nthr = self.nthr1 if which == 1 else self.nthr2
        thr_done = self.thr1_done if which == 1 else self.thr2_done
        done = []
        meng = "gpsimd" if on_pool else "vector"
        msem = self.pl if on_pool else self.dve
        for src, dst in chunks:
            slot = self.tern_slot
            self.tern_slot = (self.tern_slot + 1) % 3
            self.waitg("gpsimd", self.stage_guard[slot])
            ld = self.dma("gpsimd", self.stg[:, slot, :], src, self.qt[slot])
            self.wait(meng, self.qt[slot], ld)
            self.wait(meng, self.act, thr_done)
            self.inc(meng, lambda e, slot=slot, nthr=nthr: e.tensor_single_scalar(
                out=self.mn[:, slot, :], in_=self.stg[:, slot, :], scalar=nthr[:, 0:1], op=ALU.is_lt), msem)
            if on_pool:
                self.wait("vector", self.pl, self.pl.n)
            self.inc("vector", lambda e, slot=slot, thr=thr, dst=dst: e.scalar_tensor_tensor(
                out=dst, in0=self.stg[:, slot, :], scalar=thr[:, 0:1],
                in1=self.mn[:, slot, :], op0=ALU.is_gt, op1=ALU.subtract), self.dve)
            self.stage_guard[slot] = (self.dve, self.dve.n)
            done.append(self.dve.n)
        return done

    def phase_b(self, w1p2, w2p2):
        qs = self.qs
        for s in (self.pe, self.act, self.dve, self.pl, self.qg, self.qs,
                  self.qwc, self.qx[0], self.qx[1], self.qw[0], self.qw[1],
                  self.qt[0], self.qt[1], self.qt[2]):
            self.wait("sync", s, s.n)
        barrier = [(self.qw[0], self.qw[0].n), (self.qw[1], self.qw[1].n),
                   (self.dve, self.dve.n)]
        for e in ("scalar", "tensor", "vector", "gpsimd"):
            for g in barrier:
                self.waitg(e, g)
        w1done = self.tern_stream(1, w1p2)
        w2done = self.tern_stream(2, w2p2)
        if DEBUG_W:
            self.wait("sync", self.dve, self.dve.n)
            for ci in range(NCH):
                self.dma("sync", self.w1dbg[ci * 128:(ci + 1) * 128, :], self.w1tb[:, ci, :], self.qs)
            for fi in range(NFF):
                self.dma("sync", self.w2dbg[fi * 128:(fi + 1) * 128, :], self.w2tb[:, fi, :], self.qs)
            self.wait("tensor", self.qs, self.qs.n)
        gh_guard = None
        rot2 = [None, None]
        outbank = [None] * 4
        outwr_slot = {}
        x1_guard = {}
        hg_guard = None
        for u in range(NU):
            buf = u % 2
            if buf in x1_guard:
                self.waitg("sync", x1_guard[buf])
            x1in = self.x1d.rearrange("(g p) t -> p g t", p=128)[:, :, u * U:(u + 1) * U]
            xl = self.dma("sync", self.x1s[:, :, buf, :], x1in, self.qx[buf])
            self.wait("scalar", self.qx[buf], xl)
            stats_mm = []
            for j in range(NCH):
                if j >= 2:
                    self.wait("scalar", self.pe, stats_mm[j - 2])
                self.op("scalar", lambda e, j=j, buf=buf: e.activation(
                    out=self.xb2[:, j % 2, :], in_=self.x1s[:, j, buf, :], func=AF.Copy))
                self.inc("scalar", lambda e, j=j, buf=buf: e.activation(
                    out=self.xsq2[:, j % 2, :], in_=self.x1s[:, j, buf, :], func=AF.Square), self.act)
                self.wait("tensor", self.act, self.act.n)
                if j == 0 and gh_guard is not None:
                    self.waitg("tensor", gh_guard)
                stG, spG = (j == 0), (j == NCH - 1)
                self.op("tensor", lambda e, j=j, st=stG, sp=spG: e.matmul(
                    self.psG[0:1, 0:U], self.onesb[:, :], self.xb2[:, j % 2, :], start=st, stop=sp))
                self.inc("tensor", lambda e, j=j, st=stG, sp=spG: e.matmul(
                    self.psH[0:1, 0:U], self.onesb[:, :], self.xsq2[:, j % 2, :], start=st, stop=sp), self.pe)
                stats_mm.append(self.pe.n)
            self.wait("vector", self.pe, self.pe.n)
            rw = self.rows
            self.op("vector", lambda e: e.tensor_single_scalar(out=rw[:, 0, 0:U], in_=self.psG[0:1, 0:U], scalar=1.0 / C, op=ALU.mult))
            self.op("vector", lambda e: e.tensor_single_scalar(out=rw[:, 1, 0:U], in_=self.psH[0:1, 0:U], scalar=1.0 / C, op=ALU.mult))
            self.op("vector", lambda e: e.tensor_tensor(out=rw[:, 2, 0:U], in0=rw[:, 0, 0:U], in1=rw[:, 0, 0:U], op=ALU.mult))
            self.op("vector", lambda e: e.tensor_tensor(out=rw[:, 1, 0:U], in0=rw[:, 1, 0:U], in1=rw[:, 2, 0:U], op=ALU.subtract))
            self.inc("vector", lambda e: e.tensor_single_scalar(out=rw[:, 1, 0:U], in_=rw[:, 1, 0:U], scalar=EPS, op=ALU.add), self.dve)
            self.wait("scalar", self.dve, self.dve.n)
            self.op("scalar", lambda e: e.activation(out=rw[:, 2, 0:U], in_=rw[:, 1, 0:U], func=AF.Ln))
            self.inc("scalar", lambda e: e.activation(out=rw[:, 3, 0:U], in_=rw[:, 2, 0:U], func=AF.Exp, scale=-0.5), self.act)
            self.wait("vector", self.act, self.act.n)
            self.inc("vector", lambda e: e.tensor_tensor(out=self.mrrow[:, 0:U], in0=rw[:, 0, 0:U], in1=rw[:, 3, 0:U], op=ALU.mult), self.dve)
            self.wait("tensor", self.dve, self.dve.n)
            self.mmgroup(self.psG[:, 0:U], [(self.onesf[:, :], rw[:, 3, 0:U])])
            bc_done = self.mmgroup(self.psH[:, 0:U], [(self.onesf[:, :], self.mrrow[:, 0:U])])
            self.wait("scalar", self.pe, bc_done)
            if self.rb_guard is not None:
                self.waitg("scalar", self.rb_guard)
            self.op("scalar", lambda e: e.activation(out=self.Rb[:, 0:U], in_=self.psG[:, 0:U], func=AF.Copy))
            self.inc("scalar", lambda e: e.activation(out=self.MRb[:, 0:U], in_=self.psH[:, 0:U], func=AF.Copy), self.act)
            gh_guard = (self.act, self.act.n)
            self.wait("vector", self.act, self.act.n)
            if hg_guard is not None:
                self.waitg("vector", hg_guard)
            for j in range(NCH):
                self.op("vector", lambda e, j=j, buf=buf: e.tensor_tensor(
                    out=self.xn2f[:, :], in0=self.x1s[:, j, buf, :], in1=self.Rb[:, 0:U], op=ALU.mult))
                self.op("vector", lambda e, j=j: e.tensor_tensor(
                    out=self.xn2b[:, j, :], in0=self.xn2f[:, :], in1=self.MRb[:, 0:U], op=ALU.subtract))
            self.inc("vector", lambda e: e.engine_nop(), self.dve)
            xn2_done = self.dve.n
            self.rb_guard = (self.dve, xn2_done)
            self.wait("tensor", self.dve, xn2_done)
            rotps = [self.psA, self.psB]
            for fj in range(NFF):
                bank = fj % 2
                if u == 0:
                    self.wait("tensor", self.dve, w1done[(fj // 4) * 8 + 7])
                if rot2[bank] is not None:
                    self.waitg("tensor", rot2[bank])
                hmm = self.mmgroup(rotps[bank][:, 0:U],
                                   [(self.w1tb[:, ci, fj * 128:(fj + 1) * 128], self.xn2b[:, ci, :])
                                    for ci in range(NCH)])
                self.wait("scalar", self.pe, hmm)
                if fj == 0 and hg_guard is not None:
                    self.waitg("scalar", hg_guard)
                self.inc("scalar", lambda e, b=bank, fj=fj: e.activation(
                    out=self.hgb[:, fj, :], in_=rotps[b][:, 0:U], func=AF.Gelu,
                    bias=self.b1v[:, fj:fj + 1]), self.act)
                rot2[bank] = (self.act, self.act.n)
            gelu_all = self.act.n
            outps = [self.psC, self.psD, self.psE, self.psF]
            self.wait("tensor", self.act, gelu_all)
            for w in range(2):
                for q in range(4):
                    dj = 4 * w + q
                    if outbank[q] is not None:
                        self.waitg("tensor", outbank[q])
                    if u == 0 and w == 0 and q == 0:
                        pairs = []
                        for fi in range(NFF):
                            self.wait("tensor", self.dve, w2done[fi * 2 + 1])
                            st_, sp_ = (fi == 0), (fi == NFF - 1)
                            self.inc("tensor", lambda e, fi=fi, st_=st_, sp_=sp_, dj=dj, q=q:
                                     e.matmul(outps[q][:, 0:U], self.w2tb[:, fi, dj * 128:(dj + 1) * 128],
                                              self.hgb[:, fi, :], start=st_, stop=sp_), self.pe)
                        omm = self.pe.n
                    else:
                        omm = self.mmgroup(outps[q][:, 0:U],
                                           [(self.w2tb[:, fi, dj * 128:(dj + 1) * 128], self.hgb[:, fi, :])
                                            for fi in range(NFF)])
                    self.wait("vector", self.pe, omm)
                    slot = dj % 2
                    if slot in outwr_slot:
                        self.wait("vector", self.qw[slot], outwr_slot[slot])
                    self.inc("vector", lambda e, q=q, dj=dj, buf=buf: e.scalar_tensor_tensor(
                        out=self.outc[:, dj % 2, :], in0=outps[q][:, 0:U], scalar=self.b2v[:, dj:dj + 1],
                        in1=self.x1s[:, dj, buf, :], op0=ALU.add, op1=ALU.add), self.dve)
                    outbank[q] = (self.dve, self.dve.n)
                    self.wait("sync", self.dve, self.dve.n)
                    outwr_slot[slot] = self.dma(
                        "sync", self.outT[dj * 128:(dj + 1) * 128, u * U:(u + 1) * U],
                        self.outc[:, dj % 2, :], self.qw[slot])
            hg_guard = (self.pe, self.pe.n)
            x1_guard[buf] = (self.dve, self.dve.n)
        for s_ in (self.qs, self.qg, self.qwc, self.qx[0], self.qx[1],
                   self.qw[0], self.qw[1], self.qt[0], self.qt[1], self.qt[2]):
            self.wait("sync", s_, s_.n)

    def build(self):
        self.init_phase()
        w1p1 = [(self.w1[ci * 128:(ci + 1) * 128, fo * T:(fo + 1) * T], ci * (FF // T) + fo)
                for ci in range(NCH) for fo in range(FF // T)]
        w2p1 = [(self.w2[fi * 128:(fi + 1) * 128, do * T:(do + 1) * T], fi * (C // T) + do)
                for fi in range(NFF) for do in range(C // T)]
        w1p2 = [(self.w1[ci * 128:(ci + 1) * 128, fo * T:(fo + 1) * T],
                 self.w1tb[:, ci, fo * T:(fo + 1) * T])
                for fo in range(FF // T) for ci in range(NCH)]
        w2p2 = [(self.w2[fi * 128:(fi + 1) * 128, do * T:(do + 1) * T],
                 self.w2tb[:, fi, do * T:(do + 1) * T])
                for fi in range(NFF) for do in range(C // T)]

        self.tiny_stats(self.xpcP, wait_q=self.vec_loads)
        for s in range(NS):
            self.strip(s, self.xp, True)
            self.tern_pass1(1, w1p1[s * 8:(s + 1) * 8])
            self.tern_pass1(2, w2p1[s * 8:(s + 1) * 8])
        self.tern_pass1(1, w1p1[32:])
        self.tern_pass1(2, w2p1[32:])
        self.tern_scale(1, C * FF)
        self.tern_scale(2, C * FF)
        self.wait("vector", self.act, self.mk_done)
        self.op("vector", lambda e: e.tensor_single_scalar(
            out=self.cN[:, :], in_=self.cN[:, :], scalar=self.mkcol[:, 0:1], op=ALU.mult))
        self.inc("vector", lambda e: e.tensor_single_scalar(
            out=self.cD[:, :], in_=self.cD[:, :], scalar=self.mkcol[:, 0:1], op=ALU.mult), self.dve)
        self.mix_guard = (self.pe, self.pe.n)
        self.tiny_stats(self.xpcM)
        for s in range(NS):
            self.strip(s, self.xm, False)
        self.phase_b(w1p2, w2p2)

        nc = self.nc
        with nc.Block() as blk:
            for name, meth in (("sync", blk.sync), ("gpsimd", blk.gpsimd),
                               ("scalar", blk.scalar), ("vector", blk.vector),
                               ("tensor", blk.tensor)):
                thunks = self.prog[name]
                meth(lambda eng, thunks=thunks: [f(eng) for f in thunks])
        return nc


_NC_CACHE = {}


def _get_nc():
    if "nc" not in _NC_CACHE:
        _NC_CACHE["nc"] = Build().build()
    return _NC_CACHE["nc"]


def kernel(L0_attn_x, L0_attn_time_decay, L0_attn_time_first, L0_attn_time_mix_k,
           L0_attn_time_mix_v, L0_attn_time_mix_r, L0_attn_Wk, L0_attn_Wv,
           L0_attn_Wr, L0_attn_Wo, L0_ffn_W1, L0_ffn_b1, L0_ffn_W2, L0_ffn_b2):
    x = np.ascontiguousarray(np.asarray(L0_attn_x, dtype=np.float32))
    assert x.shape == (4, 4096, 1024)
    ac = np.ascontiguousarray

    def rv(v, n):
        return ac(np.asarray(v, np.float32).reshape(n, 128).T)

    shared = dict(
        wk=ac(np.asarray(L0_attn_Wk, np.float32).T),
        wv=ac(np.asarray(L0_attn_Wv, np.float32).T),
        wr=ac(np.asarray(L0_attn_Wr, np.float32).T),
        wo=ac(np.asarray(L0_attn_Wo, np.float32).T),
        w1=ac(np.asarray(L0_ffn_W1, np.float32).T),
        w2=ac(np.asarray(L0_ffn_W2, np.float32).T),
        tmkd=rv(L0_attn_time_mix_k, 8), tmvd=rv(L0_attn_time_mix_v, 8),
        tmrd=rv(L0_attn_time_mix_r, 8), tdd=rv(L0_attn_time_decay, 8),
        tfd=rv(L0_attn_time_first, 8), b1d=rv(L0_ffn_b1, 32), b2d=rv(L0_ffn_b2, 8))
    in_maps = []
    for c in range(8):
        b, h = c // 2, c % 2
        xmT = np.empty((1024, TLOC + 1), np.float32)
        xmT[:, 0] = 0.0 if h == 0 else x[b, TLOC - 1, :]
        xmT[:, 1:] = x[b, h * TLOC:(h + 1) * TLOC, :].T
        xpT = np.empty((1024, TLOC + 1), np.float32)
        xpT[:, 0] = 0.0
        xpT[:, 1:] = x[b, 0:TLOC, :].T
        m = dict(shared)
        m["xm"] = ac(xmT)
        m["xp"] = ac(xpT)
        m["msk"] = np.full((1, 1), float(h), np.float32)
        in_maps.append(m)

    nc = _get_nc()
    res = run_bass_kernel_spmd(nc, in_maps, core_ids=list(range(8)))
    out = np.empty((4, 4096, 1024), np.float32)
    for c in range(8):
        b, h = c // 2, c % 2
        out[b, h * TLOC:(h + 1) * TLOC, :] = np.asarray(res.results[c]["outT"]).T
    return out


# version 10
# speedup vs baseline: 1.7248x; 1.1145x over previous
import numpy as np
import concourse.bass as bass
import concourse.mybir as mybir
from concourse.bass_utils import run_bass_kernel_spmd

F32 = mybir.dt.float32
BF16 = mybir.dt.bfloat16
AF = mybir.ActivationFunctionType
ALU = mybir.AluOpType
AX = mybir.AxisListType

DEBUG_X1 = False
DEBUG_XN = False
DEBUG_Y = False
DEBUG_W = False
C = 1024
NCH = 8
FF = 4096
NFF = 32
T = 512
NS = 4
U = 256
NU = 8
TLOC = 2048
EPS = 1e-5


class Sem:
    def __init__(self, h):
        self.h = h
        self.n = 0


class Arena:
    def __init__(self, nc, size):
        self.nc = nc
        self.size = size
        self.ctx = nc.sbuf_tensor("arena", [128, size], mybir.dt.uint8)
        self.big = self.ctx.__enter__()
        self.base = nc.lookup_mloc(self.big).addr
        self.cur = 0

    def alloc(self, name, shape, dtype, at=None):
        per = int(np.prod(shape[1:])) * mybir.dt.size(dtype)
        off = self.cur if at is None else at
        off = (off + 63) & ~63
        assert off + per <= self.size, (name, off, per, self.size)
        h = self.nc.virtual_sbuf_tensor(name, shape, dtype)
        self.nc._tensor(h.name, h.shape, h.dtype, type="SB")
        m = self.nc.lookup_mloc(h)
        m.allocated = True
        m.addr = self.base + off
        if at is None:
            self.cur = off + per
        return h, off


class Build:
    ENGS = ("sync", "gpsimd", "scalar", "vector", "tensor")

    def __init__(self):
        nc = bass.Bass("TRN2", target_bir_lowering=False, debug=False)
        self.nc = nc
        self.prog = {e: [] for e in self.ENGS}
        d = nc.dram_tensor
        self.xm = d("xm", [C, TLOC + 1], F32, kind="ExternalInput").ap()
        self.xp = d("xp", [C, TLOC + 1], F32, kind="ExternalInput").ap()
        self.msk = d("msk", [1, 1], F32, kind="ExternalInput").ap()
        self.wk = d("wk", [C, C], F32, kind="ExternalInput").ap()
        self.wv = d("wv", [C, C], F32, kind="ExternalInput").ap()
        self.wr = d("wr", [C, C], F32, kind="ExternalInput").ap()
        self.wo = d("wo", [C, C], F32, kind="ExternalInput").ap()
        self.w1 = d("w1", [C, FF], F32, kind="ExternalInput").ap()
        self.w2 = d("w2", [FF, C], F32, kind="ExternalInput").ap()
        self.tmkd = d("tmkd", [128, NCH], F32, kind="ExternalInput").ap()
        self.tmvd = d("tmvd", [128, NCH], F32, kind="ExternalInput").ap()
        self.tmrd = d("tmrd", [128, NCH], F32, kind="ExternalInput").ap()
        self.tdd = d("tdd", [128, NCH], F32, kind="ExternalInput").ap()
        self.tfd = d("tfd", [128, NCH], F32, kind="ExternalInput").ap()
        self.b1d = d("b1d", [128, NFF], F32, kind="ExternalInput").ap()
        self.b2d = d("b2d", [128, NCH], F32, kind="ExternalInput").ap()
        self.outT = d("outT", [C, TLOC], F32, kind="ExternalOutput").ap()
        self.x1d = d("x1d", [C, TLOC], F32,
                     kind="ExternalOutput" if DEBUG_X1 else "Internal").ap()
        self.xnd = (d("xnd", [C, T + 1], F32, kind="ExternalOutput").ap()
                    if DEBUG_XN else None)
        self.ydbg = (d("ydbg", [C, T], F32, kind="ExternalOutput").ap()
                     if DEBUG_Y else None)
        self.kdbg = (d("kdbg", [C, T], F32, kind="ExternalOutput").ap()
                     if DEBUG_Y else None)
        self.rkdbg = (d("rkdbg", [C, T], F32, kind="ExternalOutput").ap()
                      if DEBUG_Y else None)
        self.sgdbg = (d("sgdbg", [C, T], F32, kind="ExternalOutput").ap()
                      if DEBUG_Y else None)
        self.w1dbg = (d("w1dbg", [C, FF], BF16, kind="ExternalOutput").ap()
                      if DEBUG_W else None)
        self.w2dbg = (d("w2dbg", [FF, C], BF16, kind="ExternalOutput").ap()
                      if DEBUG_W else None)

        self.qs = Sem(nc.alloc_semaphore("qs"))
        self.qg = Sem(nc.alloc_semaphore("qg"))
        self.qwc = Sem(nc.alloc_semaphore("qwc"))
        self.qx = [Sem(nc.alloc_semaphore("qx0")), Sem(nc.alloc_semaphore("qx1"))]
        self.qw = [Sem(nc.alloc_semaphore("qw0")), Sem(nc.alloc_semaphore("qw1"))]
        self.qt = [Sem(nc.alloc_semaphore("qt%d" % i)) for i in range(6)]
        self.pe = Sem(nc.alloc_semaphore("pe"))
        self.act = Sem(nc.alloc_semaphore("act"))
        self.dve = Sem(nc.alloc_semaphore("dve"))
        self.pl = Sem(nc.alloc_semaphore("pl"))

        self._alloc_sbuf()
        p = nc.alloc_psum_tensor
        self.psA = p("psA", [128, T], F32)
        self.psB = p("psB", [128, T], F32)
        self.psC = p("psC", [128, T], F32)
        self.psD = p("psD", [128, T], F32)
        self.psE = p("psE", [128, T], F32)
        self.psF = p("psF", [128, T], F32)
        self.psG = p("psG", [128, T], F32)
        self.psH = p("psH", [128, T], F32)

        self.rot3 = [None, None, None]
        self.gidx = 0
        self.stage_guard = [None] * 6
        self.tern_slot = 0
        self.xsg = {0: [], 1: []}
        self.x1wr_slot = {}
        self.gh_guard = None
        self.xn_guard = []
        self.mix_guard = None
        self.xnlast_src = None
        self.rb_guard = None

    def _alloc_sbuf(self):
        a = Arena(self.nc, 200 * 1024)
        self.ar = a

        def al(name, shape, dtype, at=None):
            h, off = a.alloc(name, shape, dtype, at)
            setattr(self, name, h)
            return off
        al("tmk", [128, NCH], F32); al("tmv", [128, NCH], F32)
        al("tmr", [128, NCH], F32); al("tdr", [128, NCH], F32)
        al("tfr", [128, NCH], F32); al("alpha", [128, NCH], F32)
        al("eu", [128, NCH], F32); al("b1v", [128, NFF], F32)
        al("b2v", [128, NCH], F32); al("onesb", [128, 1], BF16)
        al("onesf", [1, 128], F32); al("onescol", [128, 1], F32)
        al("msksb", [1, 1], F32); al("mkcol", [128, 1], F32)
        al("mcol", [128, 1], F32); al("rcol", [128, 1], F32)
        al("thr1", [128, 1], F32); al("nthr1", [128, 1], F32)
        al("thr2", [128, 1], F32); al("nthr2", [128, 1], F32)
        al("trow", [1, 1], F32); al("cN", [128, NCH], F32)
        al("cD", [128, NCH], F32)
        al("xpcP", [128, NCH, 1], F32); al("xpcM", [128, NCH, 1], F32)
        al("xpcB", [128, NCH, 1], BF16)
        al("colsA", [128, 64], F32); al("colsB", [128, 64], F32)
        al("colred", [128, 1], F32); al("tiny", [1, 8], F32)
        al("xnlast", [128, NCH, 1], BF16)
        al("rows", [1, 4, T], F32); al("mrrow", [1, T], F32)
        al("Rb", [128, T], F32); al("MRb", [128, T], F32)
        al("stg", [128, 6, T], F32)
        wbase = a.cur
        al("wkb", [128, NCH, C], BF16); al("wvb", [128, NCH, C], BF16)
        al("wrb", [128, NCH, C], BF16); al("wob", [128, NCH, C], BF16)
        al("w1tb", [128, NCH, FF], BF16, at=wbase)
        sbase = a.cur
        al("xs", [128, NCH, 2, T], F32)
        al("xn", [128, NCH, T + 1], BF16)
        xyoff = al("xkb", [128, NCH, T], BF16)
        al("xvb", [128, NCH, T], BF16)
        al("xrb", [128, NCH, T], BF16)
        al("yt", [128, NCH, T], BF16)
        al("arot", [128, 2, T], F32); al("avrot", [128, 2, T], F32)
        al("N1", [128, T + 1], F32); al("D1", [128, T + 1], F32)
        al("xbr", [128, 4, T], BF16); al("xsqr", [128, 4, T], BF16)
        al("dd", [128, T], BF16); al("numt", [128, T], F32)
        al("dent", [128, T], F32)
        al("sg", [128, 2, T], F32); al("x1c", [128, 2, T], F32)
        a_end = a.cur
        al("rwkvb", [128, NCH, T], BF16, at=xyoff)
        al("w2tb", [128, NFF, C], BF16, at=sbase)
        o = sbase + NFF * C * 2
        o = al("x1s", [128, NCH, 2, U], F32, at=o) + NCH * 2 * U * 4
        o = al("xn2b", [128, NCH, U], BF16, at=o) + NCH * U * 2
        o = al("hgb", [128, NFF, U], BF16, at=o) + NFF * U * 2
        o = al("xb2", [128, 2, U], BF16, at=o) + 2 * U * 2
        o = al("xsq2", [128, 2, U], BF16, at=o) + 2 * U * 2
        o = al("xn2f", [128, U], F32, at=o) + U * 4
        o = al("outc", [128, 2, U], F32, at=o) + 2 * U * 4
        assert max(a_end, o) <= 200 * 1024, (a_end, o)

    def emit(self, e, fn):
        self.prog[e].append(fn)

    def wait(self, e, s, val=None):
        v = s.n if val is None else val
        h = s.h
        if v > 0:
            self.emit(e, lambda eng, h=h, v=v: eng.wait_ge(h, v))

    def waitg(self, e, g):
        if g:
            self.wait(e, g[0], g[1])

    def inc(self, e, fn, s, k=1):
        s.n += k
        h = s.h
        self.emit(e, lambda eng, fn=fn, h=h, k=k: fn(eng).then_inc(h, k))
        return s.n

    def op(self, e, fn):
        self.emit(e, lambda eng, fn=fn: fn(eng))

    def dma(self, e, out, in_, s, slow=False):
        if slow:
            return self.inc(e, lambda eng, o=out, i=in_: eng.dma_start(
                out=o, in_=i, allow_slow_non_contiguous=True), s, 16)
        return self.inc(e, lambda eng, o=out, i=in_: eng.dma_start(out=o, in_=i), s, 16)

    def mmgroup(self, ps_ap, pairs):
        n = len(pairs)
        for i, (lhsT, rhs) in enumerate(pairs):
            st, sp = (i == 0), (i == n - 1)
            if sp:
                self.inc("tensor", lambda eng, o=ps_ap, l=lhsT, r=rhs, st=st, sp=sp:
                         eng.matmul(o, l, r, start=st, stop=sp), self.pe)
            else:
                self.op("tensor", lambda eng, o=ps_ap, l=lhsT, r=rhs, st=st, sp=sp:
                        eng.matmul(o, l, r, start=st, stop=sp))
        return self.pe.n

    def init_phase(self):
        qs, qg = self.qs, self.qg
        for dst, src in ((self.tmk, self.tmkd), (self.tmv, self.tmvd),
                         (self.tmr, self.tmrd), (self.tdr, self.tdd),
                         (self.tfr, self.tfd), (self.b1v, self.b1d),
                         (self.b2v, self.b2d), (self.msksb, self.msk)):
            self.dma("sync", dst[:, :], src, qs)
        self.dma("sync", self.xpcP[:, :, :],
                 self.xp[:, 0:1].rearrange("(g p) t -> p g t", p=128), qs, slow=True)
        self.dma("sync", self.xpcM[:, :, :],
                 self.xm[:, 0:1].rearrange("(g p) t -> p g t", p=128), qs, slow=True)
        self.vec_loads = qs.n
        for wdst, wsrc in ((self.wkb, self.wk), (self.wvb, self.wv),
                           (self.wrb, self.wr), (self.wob, self.wo)):
            for ci in range(NCH):
                self.dma("gpsimd", wdst[:, ci, :], wsrc[ci * 128:(ci + 1) * 128, :], self.qwc)
        self.w_loaded = self.qwc.n
        self.op("vector", lambda e: e.memset(self.onesb[:, :], 1.0))
        self.op("vector", lambda e: e.memset(self.onesf[:, :], 1.0))
        self.op("vector", lambda e: e.memset(self.onescol[:, :], 1.0))
        self.op("vector", lambda e: e.memset(self.cN[:, :], 0.0))
        self.inc("vector", lambda e: e.memset(self.cD[:, :], 0.0), self.dve)
        self.memset_done = self.dve.n
        self.wait("scalar", qs, self.vec_loads)
        self.op("scalar", lambda e: e.activation(out=self.alpha[:, :], in_=self.tdr[:, :], func=AF.Exp))
        self.op("scalar", lambda e: e.activation(out=self.alpha[:, :], in_=self.alpha[:, :], func=AF.Exp, scale=-1.0))
        self.inc("scalar", lambda e: e.activation(out=self.eu[:, :], in_=self.tfr[:, :], func=AF.Exp), self.act)
        self.wait("tensor", self.dve, self.memset_done)
        self.wait("tensor", qs, self.vec_loads)
        mk_mm = self.mmgroup(self.psE[:, 0:1], [(self.onesf[:, :], self.msksb[:, :])])
        self.wait("scalar", self.pe, mk_mm)
        self.inc("scalar", lambda e: e.activation(out=self.mkcol[:, :], in_=self.psE[:, 0:1], func=AF.Copy), self.act)
        self.mk_done = self.act.n

    def tiny_stats(self, xpc, wait_q=None):
        if wait_q is not None:
            self.wait("scalar", self.qs, wait_q)
        self.inc("scalar", lambda e: e.activation(out=self.xpcB[:, :, :], in_=xpc[:, :, :], func=AF.Copy), self.act)
        self.wait("tensor", self.act, self.act.n)
        mmx = self.mmgroup(self.psF[0:1, 0:1],
                           [(self.onesb[:, :], self.xpcB[:, j, :]) for j in range(NCH)])
        self.wait("scalar", self.pe, mmx)
        self.inc("scalar", lambda e: e.activation(out=self.xpcB[:, :, :], in_=xpc[:, :, :], func=AF.Square), self.act)
        self.wait("tensor", self.act, self.act.n)
        self.mmgroup(self.psD[0:1, 0:1],
                     [(self.onesb[:, :], self.xpcB[:, j, :]) for j in range(NCH)])
        self.wait("vector", self.pe, self.pe.n)
        ti = self.tiny
        self.op("vector", lambda e: e.tensor_single_scalar(out=ti[:, 0:1], in_=self.psF[0:1, 0:1], scalar=1.0 / C, op=ALU.mult))
        self.op("vector", lambda e: e.tensor_single_scalar(out=ti[:, 1:2], in_=self.psD[0:1, 0:1], scalar=1.0 / C, op=ALU.mult))
        self.op("vector", lambda e: e.tensor_tensor(out=ti[:, 2:3], in0=ti[:, 0:1], in1=ti[:, 0:1], op=ALU.mult))
        self.op("vector", lambda e: e.tensor_tensor(out=ti[:, 1:2], in0=ti[:, 1:2], in1=ti[:, 2:3], op=ALU.subtract))
        self.inc("vector", lambda e: e.tensor_single_scalar(out=ti[:, 1:2], in_=ti[:, 1:2], scalar=EPS, op=ALU.add), self.dve)
        self.wait("scalar", self.dve, self.dve.n)
        self.op("scalar", lambda e: e.activation(out=ti[:, 2:3], in_=ti[:, 1:2], func=AF.Ln))
        self.inc("scalar", lambda e: e.activation(out=ti[:, 3:4], in_=ti[:, 2:3], func=AF.Exp, scale=-0.5), self.act)
        self.wait("tensor", self.act, self.act.n)
        self.mmgroup(self.psF[:, 0:1], [(self.onesf[:, :], ti[:, 0:1])])
        mm3 = self.mmgroup(self.psD[:, 0:1], [(self.onesf[:, :], ti[:, 3:4])])
        self.wait("scalar", self.pe, mm3)
        self.op("scalar", lambda e: e.activation(out=self.mcol[:, :], in_=self.psF[:, 0:1], func=AF.Copy))
        self.inc("scalar", lambda e: e.activation(out=self.rcol[:, :], in_=self.psD[:, 0:1], func=AF.Copy), self.act)
        self.wait("vector", self.act, self.act.n)
        for j in range(NCH):
            fn = lambda e, j=j: e.scalar_tensor_tensor(
                out=self.xn[:, j, 0:1], in0=xpc[:, j, :], scalar=self.mcol[:, 0:1],
                in1=self.rcol[:, 0:1], op0=ALU.subtract, op1=ALU.mult)
            if j == NCH - 1:
                self.inc("vector", fn, self.dve)
            else:
                self.op("vector", fn)

    def strip(self, s, src, prefix):
        buf = s % 2
        c0 = 1 + s * T
        qs = self.qs
        for g in self.xsg[buf]:
            self.waitg("sync", g)
        xin = src.rearrange("(g p) t -> p g t", p=128)[:, :, c0:c0 + T]
        xl = self.dma("sync", self.xs[:, :, buf, :], xin, self.qx[buf])
        self.wait("scalar", self.qx[buf], xl)
        stats_mm = []
        for j in range(NCH):
            if j >= 4:
                self.wait("scalar", self.pe, stats_mm[j - 4])
            self.op("scalar", lambda e, j=j, buf=buf: e.activation(
                out=self.xbr[:, j % 4, :], in_=self.xs[:, j, buf, :], func=AF.Copy))
            self.inc("scalar", lambda e, j=j, buf=buf: e.activation(
                out=self.xsqr[:, j % 4, :], in_=self.xs[:, j, buf, :], func=AF.Square), self.act)
            self.wait("tensor", self.act, self.act.n)
            if j == 0:
                self.waitg("tensor", self.gh_guard)
            stG, spG = (j == 0), (j == NCH - 1)
            self.op("tensor", lambda e, j=j, st=stG, sp=spG: e.matmul(
                self.psG[0:1, :], self.onesb[:, :], self.xbr[:, j % 4, :], start=st, stop=sp))
            self.inc("tensor", lambda e, j=j, st=stG, sp=spG: e.matmul(
                self.psH[0:1, :], self.onesb[:, :], self.xsqr[:, j % 4, :], start=st, stop=sp), self.pe)
            stats_mm.append(self.pe.n)
        stats_done = self.pe.n
        self.wait("vector", self.pe, stats_done)
        rw = self.rows
        self.op("vector", lambda e: e.tensor_single_scalar(out=rw[:, 0, :], in_=self.psG[0:1, :], scalar=1.0 / C, op=ALU.mult))
        self.op("vector", lambda e: e.tensor_single_scalar(out=rw[:, 1, :], in_=self.psH[0:1, :], scalar=1.0 / C, op=ALU.mult))
        self.op("vector", lambda e: e.tensor_tensor(out=rw[:, 2, :], in0=rw[:, 0, :], in1=rw[:, 0, :], op=ALU.mult))
        self.op("vector", lambda e: e.tensor_tensor(out=rw[:, 1, :], in0=rw[:, 1, :], in1=rw[:, 2, :], op=ALU.subtract))
        self.inc("vector", lambda e: e.tensor_single_scalar(out=rw[:, 1, :], in_=rw[:, 1, :], scalar=EPS, op=ALU.add), self.dve)
        self.wait("scalar", self.dve, self.dve.n)
        self.op("scalar", lambda e: e.activation(out=rw[:, 2, :], in_=rw[:, 1, :], func=AF.Ln))
        self.inc("scalar", lambda e: e.activation(out=rw[:, 3, :], in_=rw[:, 2, :], func=AF.Exp, scale=-0.5), self.act)
        self.wait("vector", self.act, self.act.n)
        self.inc("vector", lambda e: e.tensor_tensor(out=self.mrrow[:, :], in0=rw[:, 0, :], in1=rw[:, 3, :], op=ALU.mult), self.dve)
        self.wait("tensor", self.dve, self.dve.n)
        self.mmgroup(self.psG[:, :], [(self.onesf[:, :], rw[:, 3, :])])
        bc_done = self.mmgroup(self.psH[:, :], [(self.onesf[:, :], self.mrrow[:, :])])
        self.wait("scalar", self.pe, bc_done)
        if self.rb_guard is not None:
            self.waitg("scalar", self.rb_guard)
        self.op("scalar", lambda e: e.activation(out=self.Rb[:, :], in_=self.psG[:, :], func=AF.Copy))
        self.inc("scalar", lambda e: e.activation(out=self.MRb[:, :], in_=self.psH[:, :], func=AF.Copy), self.act)
        self.gh_guard = (self.act, self.act.n)
        self.wait("vector", self.act, self.act.n)
        if s > 0:
            self.op("vector", lambda e: e.tensor_copy(out=self.xn[:, :, 0:1], in_=self.xnlast[:, :, :]))
        self.waitg("vector", self.mix_guard)
        for j in range(NCH):
            self.op("vector", lambda e, j=j, buf=buf: e.tensor_tensor(
                out=self.xn[:, j, 1:T + 1], in0=self.xs[:, j, buf, :], in1=self.Rb[:, :], op=ALU.mult))
            self.op("vector", lambda e, j=j: e.tensor_tensor(
                out=self.xn[:, j, 1:T + 1], in0=self.xn[:, j, 1:T + 1], in1=self.MRb[:, :], op=ALU.subtract))
        self.inc("vector", lambda e: e.engine_nop(), self.dve)
        xn_done = self.dve.n
        self.rb_guard = (self.dve, xn_done)
        for j in range(NCH):
            self.op("vector", lambda e, j=j: e.tensor_tensor(
                out=self.dd[:, :], in0=self.xn[:, j, 1:T + 1], in1=self.xn[:, j, 0:T], op=ALU.subtract))
            self.op("vector", lambda e, j=j: e.scalar_tensor_tensor(
                out=self.xkb[:, j, :], in0=self.dd[:, :], scalar=self.tmk[:, j:j + 1],
                in1=self.xn[:, j, 0:T], op0=ALU.mult, op1=ALU.add))
            fnv = lambda e, j=j: e.scalar_tensor_tensor(
                out=self.xvb[:, j, :], in0=self.dd[:, :], scalar=self.tmv[:, j:j + 1],
                in1=self.xn[:, j, 0:T], op0=ALU.mult, op1=ALU.add)
            if prefix:
                self.inc("vector", fnv, self.dve)
            else:
                self.op("vector", fnv)
                self.inc("vector", lambda e, j=j: e.scalar_tensor_tensor(
                    out=self.xrb[:, j, :], in0=self.dd[:, :], scalar=self.tmr[:, j:j + 1],
                    in1=self.xn[:, j, 0:T], op0=ALU.mult, op1=ALU.add), self.dve)
        self.inc("vector", lambda e: e.tensor_copy(
            out=self.xnlast[:, :, :], in_=self.xn[:, :, T:T + 1]), self.dve)
        mixes_done = self.dve.n
        if DEBUG_XN and s == 0 and not prefix:
            self.wait("sync", self.dve, mixes_done)
            dbgwr = self.dma("sync", self.xnd.rearrange("(g p) t -> p g t", p=128),
                             self.xn[:, :, :], self.qs)
            self.wait("vector", self.qs, dbgwr)
        self.xnlast_src = (self.dve, mixes_done)
        self.wait("tensor", self.dve, mixes_done)
        if s == 0 and prefix:
            self.wait("tensor", self.qwc, self.w_loaded)
        rotps = [self.psA, self.psB, self.psC]
        dn_cnt = {}
        for dj in range(NCH):
            bank = self.gidx % 3
            self.waitg("tensor", self.rot3[bank])
            kmm = self.mmgroup(rotps[bank][:, :],
                               [(self.wkb[:, ci, dj * 128:(dj + 1) * 128], self.xkb[:, ci, :])
                                for ci in range(NCH)])
            self.wait("scalar", self.pe, kmm)
            if dj >= 2:
                self.wait("scalar", self.dve, dn_cnt[dj - 2])
            self.inc("scalar", lambda e, b=bank, dj=dj: e.activation(
                out=self.arot[:, dj % 2, :], in_=rotps[b][:, :], func=AF.Exp), self.act)
            a_ev = self.act.n
            self.rot3[bank] = (self.act, a_ev)
            self.gidx += 1
            if DEBUG_Y and s == 0 and not prefix:
                self.wait("sync", self.act, a_ev)
                kw_ = self.dma("sync", self.kdbg[dj * 128:(dj + 1) * 128, :],
                               self.arot[:, dj % 2, :], self.qs)
                self.wait("scalar", self.qs, kw_)
            bank = self.gidx % 3
            self.waitg("tensor", self.rot3[bank])
            vmm = self.mmgroup(rotps[bank][:, :],
                               [(self.wvb[:, ci, dj * 128:(dj + 1) * 128], self.xvb[:, ci, :])
                                for ci in range(NCH)])
            self.wait("vector", self.pe, vmm)
            self.wait("vector", self.act, a_ev)
            self.inc("vector", lambda e, b=bank, dj=dj: e.tensor_tensor(
                out=self.avrot[:, dj % 2, :], in0=rotps[b][:, :], in1=self.arot[:, dj % 2, :], op=ALU.mult), self.dve)
            self.rot3[bank] = (self.dve, self.dve.n)
            self.gidx += 1
            self.op("vector", lambda e, dj=dj: e.tensor_copy(out=self.N1[:, 0:1], in_=self.cN[:, dj:dj + 1]))
            self.op("vector", lambda e, dj=dj: e.tensor_copy(out=self.D1[:, 0:1], in_=self.cD[:, dj:dj + 1]))
            self.op("vector", lambda e, dj=dj: e.tensor_tensor_scan(
                out=self.N1[:, 1:T + 1], data0=self.alpha[:, dj:dj + 1].broadcast_to([128, T]),
                data1=self.avrot[:, dj % 2, :], initial=self.cN[:, dj:dj + 1], op0=ALU.mult, op1=ALU.add))
            self.op("vector", lambda e, dj=dj: e.tensor_tensor_scan(
                out=self.D1[:, 1:T + 1], data0=self.alpha[:, dj:dj + 1].broadcast_to([128, T]),
                data1=self.arot[:, dj % 2, :], initial=self.cD[:, dj:dj + 1], op0=ALU.mult, op1=ALU.add))
            self.op("vector", lambda e, dj=dj: e.tensor_copy(out=self.cN[:, dj:dj + 1], in_=self.N1[:, T:T + 1]))
            self.op("vector", lambda e, dj=dj: e.tensor_copy(out=self.cD[:, dj:dj + 1], in_=self.D1[:, T:T + 1]))
            if not prefix:
                self.op("vector", lambda e, dj=dj: e.scalar_tensor_tensor(
                    out=self.numt[:, :], in0=self.avrot[:, dj % 2, :], scalar=self.eu[:, dj:dj + 1],
                    in1=self.N1[:, 0:T], op0=ALU.mult, op1=ALU.add))
                self.op("vector", lambda e, dj=dj: e.scalar_tensor_tensor(
                    out=self.dent[:, :], in0=self.arot[:, dj % 2, :], scalar=self.eu[:, dj:dj + 1],
                    in1=self.D1[:, 0:T], op0=ALU.mult, op1=ALU.add))
                self.op("vector", lambda e: e.reciprocal(out=self.dent[:, :], in_=self.dent[:, :]))
                self.inc("vector", lambda e, dj=dj: e.tensor_tensor(
                    out=self.yt[:, dj, :], in0=self.numt[:, :], in1=self.dent[:, :], op=ALU.mult), self.dve)
            else:
                self.inc("vector", lambda e: e.engine_nop(), self.dve)
            dn_cnt[dj] = self.dve.n
        if DEBUG_Y and s == 0 and not prefix:
            self.wait("sync", self.dve, dn_cnt[NCH - 1])
            yw = self.dma("sync", self.ydbg.rearrange("(g p) t -> p g t", p=128),
                          self.yt[:, :, :], self.qs)
            self.wait("gpsimd", self.qs, yw)
        if prefix:
            self.xsg[buf] = [(self.dve, dn_cnt[NCH - 1])]
            return
        rwkv_cnt = {}
        for dj in range(NCH):
            bank = self.gidx % 3
            self.waitg("tensor", self.rot3[bank])
            rmm = self.mmgroup(rotps[bank][:, :],
                               [(self.wrb[:, ci, dj * 128:(dj + 1) * 128], self.xrb[:, ci, :])
                                for ci in range(NCH)])
            self.wait("scalar", self.pe, rmm)
            if dj >= 2:
                self.wait("scalar", self.dve, rwkv_cnt[dj - 2])
            self.inc("scalar", lambda e, b=bank, dj=dj: e.activation(
                out=self.sg[:, dj % 2, :], in_=rotps[b][:, :], func=AF.Sigmoid), self.act)
            self.rot3[bank] = (self.act, self.act.n)
            self.gidx += 1
            self.wait("vector", self.act, self.act.n)
            self.inc("vector", lambda e, dj=dj: e.tensor_tensor(
                out=self.rwkvb[:, dj, :], in0=self.sg[:, dj % 2, :], in1=self.yt[:, dj, :], op=ALU.mult), self.dve)
            rwkv_cnt[dj] = self.dve.n
        x1_evac_last = 0
        for dj in range(NCH):
            bank = self.gidx % 3
            self.waitg("tensor", self.rot3[bank])
            if dj == 0:
                self.wait("tensor", self.dve, rwkv_cnt[NCH - 1])
            womm = self.mmgroup(rotps[bank][:, :],
                                [(self.wob[:, ci, dj * 128:(dj + 1) * 128], self.rwkvb[:, ci, :])
                                 for ci in range(NCH)])
            self.wait("vector", self.pe, womm)
            slot = dj % 2
            if slot in self.x1wr_slot:
                self.wait("vector", self.qw[slot], self.x1wr_slot[slot])
            self.inc("vector", lambda e, b=bank, dj=dj, buf=buf: e.tensor_tensor(
                out=self.x1c[:, dj % 2, :], in0=rotps[b][:, :], in1=self.xs[:, dj, buf, :], op=ALU.add), self.dve)
            x1_evac_last = self.dve.n
            self.rot3[bank] = (self.dve, self.dve.n)
            self.gidx += 1
            self.wait("sync", self.dve, x1_evac_last)
            self.x1wr_slot[slot] = self.dma(
                "sync", self.x1d[dj * 128:(dj + 1) * 128, s * T:(s + 1) * T],
                self.x1c[:, dj % 2, :], self.qw[slot])
        self.xsg[buf] = [(self.dve, x1_evac_last)]
        self.mix_guard = (self.pe, self.pe.n)

    def tern_pass1(self, which, chunks):
        cols = self.colsA if which == 1 else self.colsB
        for src, col in chunks:
            slot = self.tern_slot
            self.tern_slot = (self.tern_slot + 1) % 6
            self.waitg("gpsimd", self.stage_guard[slot])
            ld = self.dma("gpsimd", self.stg[:, slot, :], src, self.qt[slot])
            self.wait("vector", self.qt[slot], ld)
            self.inc("vector", lambda e, slot=slot, cols=cols, col=col: e.tensor_reduce(
                out=cols[:, col:col + 1], in_=self.stg[:, slot, :], axis=AX.X,
                op=ALU.add, apply_absolute_value=True), self.dve)
            self.stage_guard[slot] = (self.dve, self.dve.n)

    def tern_scale(self, which, nelem):
        cols = self.colsA if which == 1 else self.colsB
        thr = self.thr1 if which == 1 else self.thr2
        nthr = self.nthr1 if which == 1 else self.nthr2
        self.inc("vector", lambda e, cols=cols: e.tensor_reduce(
            out=self.colred[:, :], in_=cols[:, :], axis=AX.X, op=ALU.add), self.dve)
        self.wait("tensor", self.dve, self.dve.n)
        self.wait("tensor", self.act, self.act.n)
        tot = self.mmgroup(self.psE[0:1, 0:1], [(self.colred[:, :], self.onescol[:, :])])
        self.wait("scalar", self.pe, tot)
        self.inc("scalar", lambda e, n=nelem: e.activation(
            out=self.trow[:, :], in_=self.psE[0:1, 0:1], func=AF.Copy, scale=1.0 / n), self.act)
        self.wait("vector", self.act, self.act.n)
        self.inc("vector", lambda e: e.tensor_single_scalar(
            out=self.trow[:, :], in_=self.trow[:, :], scalar=1e-8, op=ALU.max), self.dve)
        self.wait("tensor", self.dve, self.dve.n)
        bc = self.mmgroup(self.psE[:, 0:1], [(self.onesf[:, :], self.trow[:, :])])
        self.wait("scalar", self.pe, bc)
        self.op("scalar", lambda e, thr=thr: e.activation(out=thr[:, :], in_=self.psE[:, 0:1], func=AF.Copy, scale=0.5))
        self.inc("scalar", lambda e, nthr=nthr: e.activation(out=nthr[:, :], in_=self.psE[:, 0:1], func=AF.Copy, scale=-0.5), self.act)
        if which == 1:
            self.thr1_done = self.act.n
        else:
            self.thr2_done = self.act.n

    def tern_stream(self, which, chunks, on_pool=False):
        thr = self.thr1 if which == 1 else self.thr2
        nthr = self.nthr1 if which == 1 else self.nthr2
        thr_done = self.thr1_done if which == 1 else self.thr2_done
        done = []
        meng = "gpsimd" if on_pool else "vector"
        msem = self.pl if on_pool else self.dve
        for src, dst in chunks:
            slot = self.tern_slot
            self.tern_slot = (self.tern_slot + 1) % 6
            self.waitg("gpsimd", self.stage_guard[slot])
            ld = self.dma("gpsimd", self.stg[:, slot, :], src, self.qt[slot])
            self.wait("vector", self.qt[slot], ld)
            self.wait("vector", self.act, thr_done)
            self.op("vector", lambda e, slot=slot, nthr=nthr, dst=dst: e.tensor_single_scalar(
                out=dst, in_=self.stg[:, slot, :], scalar=nthr[:, 0:1], op=ALU.is_lt))
            self.inc("vector", lambda e, slot=slot, thr=thr, dst=dst: e.scalar_tensor_tensor(
                out=dst, in0=self.stg[:, slot, :], scalar=thr[:, 0:1],
                in1=dst, op0=ALU.is_gt, op1=ALU.subtract), self.dve)
            self.stage_guard[slot] = (self.dve, self.dve.n)
            done.append(self.dve.n)
        return done

    def phase_b(self, w1p2, w2p2):
        qs = self.qs
        for s in (self.pe, self.act, self.dve, self.pl, self.qg, self.qs,
                  self.qwc, self.qx[0], self.qx[1], self.qw[0], self.qw[1],
                  self.qt[0], self.qt[1], self.qt[2]):
            self.wait("sync", s, s.n)
        barrier = [(self.qw[0], self.qw[0].n), (self.qw[1], self.qw[1].n),
                   (self.dve, self.dve.n)]
        for e in ("scalar", "tensor", "vector", "gpsimd"):
            for g in barrier:
                self.waitg(e, g)
        w1done = self.tern_stream(1, w1p2)
        w2done = self.tern_stream(2, w2p2)
        if DEBUG_W:
            self.wait("sync", self.dve, self.dve.n)
            for ci in range(NCH):
                self.dma("sync", self.w1dbg[ci * 128:(ci + 1) * 128, :], self.w1tb[:, ci, :], self.qs)
            for fi in range(NFF):
                self.dma("sync", self.w2dbg[fi * 128:(fi + 1) * 128, :], self.w2tb[:, fi, :], self.qs)
            self.wait("tensor", self.qs, self.qs.n)
        gh_guard = None
        rot2 = [None, None]
        outbank = [None] * 4
        outwr_slot = {}
        x1_guard = {}
        hg_guard = None
        for u in range(NU):
            buf = u % 2
            if buf in x1_guard:
                self.waitg("sync", x1_guard[buf])
            x1in = self.x1d.rearrange("(g p) t -> p g t", p=128)[:, :, u * U:(u + 1) * U]
            xl = self.dma("sync", self.x1s[:, :, buf, :], x1in, self.qx[buf])
            self.wait("scalar", self.qx[buf], xl)
            stats_mm = []
            for j in range(NCH):
                if j >= 2:
                    self.wait("scalar", self.pe, stats_mm[j - 2])
                self.op("scalar", lambda e, j=j, buf=buf: e.activation(
                    out=self.xb2[:, j % 2, :], in_=self.x1s[:, j, buf, :], func=AF.Copy))
                self.inc("scalar", lambda e, j=j, buf=buf: e.activation(
                    out=self.xsq2[:, j % 2, :], in_=self.x1s[:, j, buf, :], func=AF.Square), self.act)
                self.wait("tensor", self.act, self.act.n)
                if j == 0 and gh_guard is not None:
                    self.waitg("tensor", gh_guard)
                stG, spG = (j == 0), (j == NCH - 1)
                self.op("tensor", lambda e, j=j, st=stG, sp=spG: e.matmul(
                    self.psG[0:1, 0:U], self.onesb[:, :], self.xb2[:, j % 2, :], start=st, stop=sp))
                self.inc("tensor", lambda e, j=j, st=stG, sp=spG: e.matmul(
                    self.psH[0:1, 0:U], self.onesb[:, :], self.xsq2[:, j % 2, :], start=st, stop=sp), self.pe)
                stats_mm.append(self.pe.n)
            self.wait("vector", self.pe, self.pe.n)
            rw = self.rows
            self.op("vector", lambda e: e.tensor_single_scalar(out=rw[:, 0, 0:U], in_=self.psG[0:1, 0:U], scalar=1.0 / C, op=ALU.mult))
            self.op("vector", lambda e: e.tensor_single_scalar(out=rw[:, 1, 0:U], in_=self.psH[0:1, 0:U], scalar=1.0 / C, op=ALU.mult))
            self.op("vector", lambda e: e.tensor_tensor(out=rw[:, 2, 0:U], in0=rw[:, 0, 0:U], in1=rw[:, 0, 0:U], op=ALU.mult))
            self.op("vector", lambda e: e.tensor_tensor(out=rw[:, 1, 0:U], in0=rw[:, 1, 0:U], in1=rw[:, 2, 0:U], op=ALU.subtract))
            self.inc("vector", lambda e: e.tensor_single_scalar(out=rw[:, 1, 0:U], in_=rw[:, 1, 0:U], scalar=EPS, op=ALU.add), self.dve)
            self.wait("scalar", self.dve, self.dve.n)
            self.op("scalar", lambda e: e.activation(out=rw[:, 2, 0:U], in_=rw[:, 1, 0:U], func=AF.Ln))
            self.inc("scalar", lambda e: e.activation(out=rw[:, 3, 0:U], in_=rw[:, 2, 0:U], func=AF.Exp, scale=-0.5), self.act)
            self.wait("vector", self.act, self.act.n)
            self.inc("vector", lambda e: e.tensor_tensor(out=self.mrrow[:, 0:U], in0=rw[:, 0, 0:U], in1=rw[:, 3, 0:U], op=ALU.mult), self.dve)
            self.wait("tensor", self.dve, self.dve.n)
            self.mmgroup(self.psG[:, 0:U], [(self.onesf[:, :], rw[:, 3, 0:U])])
            bc_done = self.mmgroup(self.psH[:, 0:U], [(self.onesf[:, :], self.mrrow[:, 0:U])])
            self.wait("scalar", self.pe, bc_done)
            if self.rb_guard is not None:
                self.waitg("scalar", self.rb_guard)
            self.op("scalar", lambda e: e.activation(out=self.Rb[:, 0:U], in_=self.psG[:, 0:U], func=AF.Copy))
            self.inc("scalar", lambda e: e.activation(out=self.MRb[:, 0:U], in_=self.psH[:, 0:U], func=AF.Copy), self.act)
            gh_guard = (self.act, self.act.n)
            self.wait("vector", self.act, self.act.n)
            if hg_guard is not None:
                self.waitg("vector", hg_guard)
            for j in range(NCH):
                self.op("vector", lambda e, j=j, buf=buf: e.tensor_tensor(
                    out=self.xn2f[:, :], in0=self.x1s[:, j, buf, :], in1=self.Rb[:, 0:U], op=ALU.mult))
                self.op("vector", lambda e, j=j: e.tensor_tensor(
                    out=self.xn2b[:, j, :], in0=self.xn2f[:, :], in1=self.MRb[:, 0:U], op=ALU.subtract))
            self.inc("vector", lambda e: e.engine_nop(), self.dve)
            xn2_done = self.dve.n
            self.rb_guard = (self.dve, xn2_done)
            self.wait("tensor", self.dve, xn2_done)
            rotps = [self.psA, self.psB]
            for fj in range(NFF):
                bank = fj % 2
                if u == 0:
                    self.wait("tensor", self.dve, w1done[(fj // 4) * 8 + 7])
                if rot2[bank] is not None:
                    self.waitg("tensor", rot2[bank])
                hmm = self.mmgroup(rotps[bank][:, 0:U],
                                   [(self.w1tb[:, ci, fj * 128:(fj + 1) * 128], self.xn2b[:, ci, :])
                                    for ci in range(NCH)])
                self.wait("scalar", self.pe, hmm)
                if fj == 0 and hg_guard is not None:
                    self.waitg("scalar", hg_guard)
                self.inc("scalar", lambda e, b=bank, fj=fj: e.activation(
                    out=self.hgb[:, fj, :], in_=rotps[b][:, 0:U], func=AF.Gelu,
                    bias=self.b1v[:, fj:fj + 1]), self.act)
                rot2[bank] = (self.act, self.act.n)
            gelu_all = self.act.n
            outps = [self.psC, self.psD, self.psE, self.psF]
            self.wait("tensor", self.act, gelu_all)
            for w in range(2):
                for q in range(4):
                    dj = 4 * w + q
                    if outbank[q] is not None:
                        self.waitg("tensor", outbank[q])
                    if u == 0 and w == 0 and q == 0:
                        pairs = []
                        for fi in range(NFF):
                            self.wait("tensor", self.dve, w2done[fi * 2 + 1])
                            st_, sp_ = (fi == 0), (fi == NFF - 1)
                            self.inc("tensor", lambda e, fi=fi, st_=st_, sp_=sp_, dj=dj, q=q:
                                     e.matmul(outps[q][:, 0:U], self.w2tb[:, fi, dj * 128:(dj + 1) * 128],
                                              self.hgb[:, fi, :], start=st_, stop=sp_), self.pe)
                        omm = self.pe.n
                    else:
                        omm = self.mmgroup(outps[q][:, 0:U],
                                           [(self.w2tb[:, fi, dj * 128:(dj + 1) * 128], self.hgb[:, fi, :])
                                            for fi in range(NFF)])
                    self.wait("vector", self.pe, omm)
                    slot = dj % 2
                    if slot in outwr_slot:
                        self.wait("vector", self.qw[slot], outwr_slot[slot])
                    self.inc("vector", lambda e, q=q, dj=dj, buf=buf: e.scalar_tensor_tensor(
                        out=self.outc[:, dj % 2, :], in0=outps[q][:, 0:U], scalar=self.b2v[:, dj:dj + 1],
                        in1=self.x1s[:, dj, buf, :], op0=ALU.add, op1=ALU.add), self.dve)
                    outbank[q] = (self.dve, self.dve.n)
                    self.wait("sync", self.dve, self.dve.n)
                    outwr_slot[slot] = self.dma(
                        "sync", self.outT[dj * 128:(dj + 1) * 128, u * U:(u + 1) * U],
                        self.outc[:, dj % 2, :], self.qw[slot])
            hg_guard = (self.pe, self.pe.n)
            x1_guard[buf] = (self.dve, self.dve.n)
        for s_ in (self.qs, self.qg, self.qwc, self.qx[0], self.qx[1],
                   self.qw[0], self.qw[1], self.qt[0], self.qt[1], self.qt[2]):
            self.wait("sync", s_, s_.n)

    def build(self):
        self.init_phase()
        w1p1 = [(self.w1[ci * 128:(ci + 1) * 128, fo * T:(fo + 1) * T], ci * (FF // T) + fo)
                for ci in range(NCH) for fo in range(FF // T)]
        w2p1 = [(self.w2[fi * 128:(fi + 1) * 128, do * T:(do + 1) * T], fi * (C // T) + do)
                for fi in range(NFF) for do in range(C // T)]
        w1p2 = [(self.w1[ci * 128:(ci + 1) * 128, fo * T:(fo + 1) * T],
                 self.w1tb[:, ci, fo * T:(fo + 1) * T])
                for fo in range(FF // T) for ci in range(NCH)]
        w2p2 = [(self.w2[fi * 128:(fi + 1) * 128, do * T:(do + 1) * T],
                 self.w2tb[:, fi, do * T:(do + 1) * T])
                for fi in range(NFF) for do in range(C // T)]

        self.tiny_stats(self.xpcP, wait_q=self.vec_loads)
        for s in range(NS):
            self.strip(s, self.xp, True)
            self.tern_pass1(1, w1p1[s * 16:(s + 1) * 16])
            self.tern_pass1(2, w2p1[s * 16:(s + 1) * 16])
        self.tern_scale(1, C * FF)
        self.tern_scale(2, C * FF)
        self.wait("vector", self.act, self.mk_done)
        self.op("vector", lambda e: e.tensor_single_scalar(
            out=self.cN[:, :], in_=self.cN[:, :], scalar=self.mkcol[:, 0:1], op=ALU.mult))
        self.inc("vector", lambda e: e.tensor_single_scalar(
            out=self.cD[:, :], in_=self.cD[:, :], scalar=self.mkcol[:, 0:1], op=ALU.mult), self.dve)
        self.mix_guard = (self.pe, self.pe.n)
        self.tiny_stats(self.xpcM)
        for s in range(NS):
            self.strip(s, self.xm, False)
        self.phase_b(w1p2, w2p2)

        nc = self.nc
        with nc.Block() as blk:
            for name, meth in (("sync", blk.sync), ("gpsimd", blk.gpsimd),
                               ("scalar", blk.scalar), ("vector", blk.vector),
                               ("tensor", blk.tensor)):
                thunks = self.prog[name]
                meth(lambda eng, thunks=thunks: [f(eng) for f in thunks])
        return nc


_NC_CACHE = {}


def _get_nc():
    if "nc" not in _NC_CACHE:
        _NC_CACHE["nc"] = Build().build()
    return _NC_CACHE["nc"]


def kernel(L0_attn_x, L0_attn_time_decay, L0_attn_time_first, L0_attn_time_mix_k,
           L0_attn_time_mix_v, L0_attn_time_mix_r, L0_attn_Wk, L0_attn_Wv,
           L0_attn_Wr, L0_attn_Wo, L0_ffn_W1, L0_ffn_b1, L0_ffn_W2, L0_ffn_b2):
    x = np.ascontiguousarray(np.asarray(L0_attn_x, dtype=np.float32))
    assert x.shape == (4, 4096, 1024)
    ac = np.ascontiguousarray

    def rv(v, n):
        return ac(np.asarray(v, np.float32).reshape(n, 128).T)

    shared = dict(
        wk=ac(np.asarray(L0_attn_Wk, np.float32).T),
        wv=ac(np.asarray(L0_attn_Wv, np.float32).T),
        wr=ac(np.asarray(L0_attn_Wr, np.float32).T),
        wo=ac(np.asarray(L0_attn_Wo, np.float32).T),
        w1=ac(np.asarray(L0_ffn_W1, np.float32).T),
        w2=ac(np.asarray(L0_ffn_W2, np.float32).T),
        tmkd=rv(L0_attn_time_mix_k, 8), tmvd=rv(L0_attn_time_mix_v, 8),
        tmrd=rv(L0_attn_time_mix_r, 8), tdd=rv(L0_attn_time_decay, 8),
        tfd=rv(L0_attn_time_first, 8), b1d=rv(L0_ffn_b1, 32), b2d=rv(L0_ffn_b2, 8))
    in_maps = []
    for c in range(8):
        b, h = c // 2, c % 2
        xmT = np.empty((1024, TLOC + 1), np.float32)
        xmT[:, 0] = 0.0 if h == 0 else x[b, TLOC - 1, :]
        xmT[:, 1:] = x[b, h * TLOC:(h + 1) * TLOC, :].T
        xpT = np.empty((1024, TLOC + 1), np.float32)
        xpT[:, 0] = 0.0
        xpT[:, 1:] = x[b, 0:TLOC, :].T
        m = dict(shared)
        m["xm"] = ac(xmT)
        m["xp"] = ac(xpT)
        m["msk"] = np.full((1, 1), float(h), np.float32)
        in_maps.append(m)

    nc = _get_nc()
    res = run_bass_kernel_spmd(nc, in_maps, core_ids=list(range(8)))
    out = np.empty((4, 4096, 1024), np.float32)
    for c in range(8):
        b, h = c // 2, c % 2
        out[b, h * TLOC:(h + 1) * TLOC, :] = np.asarray(res.results[c]["outT"]).T
    return out


# version 15
# speedup vs baseline: 1.7780x; 1.0309x over previous
import numpy as np
import concourse.bass as bass
import concourse.mybir as mybir
from concourse.bass_utils import run_bass_kernel_spmd
from concourse.library_overlay import lower_extended_insts

F32 = mybir.dt.float32
BF16 = mybir.dt.bfloat16
AF = mybir.ActivationFunctionType
ALU = mybir.AluOpType
AX = mybir.AxisListType

DEBUG_X1 = False
DEBUG_XN = False
DEBUG_Y = False
DEBUG_W = False
C = 1024
NCH = 8
FF = 4096
NFF = 32
T = 512
NS = 4
U = 256
NU = 8
TLOC = 2048
EPS = 1e-5


class Sem:
    def __init__(self, h):
        self.h = h
        self.n = 0


class Arena:
    def __init__(self, nc, size):
        self.nc = nc
        self.size = size
        self.ctx = nc.sbuf_tensor("arena", [128, size], mybir.dt.uint8)
        self.big = self.ctx.__enter__()
        self.base = nc.lookup_mloc(self.big).addr
        self.cur = 0

    def alloc(self, name, shape, dtype, at=None):
        per = int(np.prod(shape[1:])) * mybir.dt.size(dtype)
        off = self.cur if at is None else at
        off = (off + 63) & ~63
        assert off + per <= self.size, (name, off, per, self.size)
        h = self.nc.virtual_sbuf_tensor(name, shape, dtype)
        self.nc._tensor(h.name, h.shape, h.dtype, type="SB")
        m = self.nc.lookup_mloc(h)
        m.allocated = True
        m.addr = self.base + off
        if at is None:
            self.cur = off + per
        return h, off


class Build:
    ENGS = ("sync", "gpsimd", "scalar", "vector", "tensor")

    def __init__(self):
        nc = bass.Bass("TRN2", target_bir_lowering=False, debug=False)
        self.nc = nc
        self.prog = {e: [] for e in self.ENGS}
        d = nc.dram_tensor
        self.xm = d("xm", [C, TLOC + 1], F32, kind="ExternalInput").ap()
        self.xp = d("xp", [C, TLOC + 1], F32, kind="ExternalInput").ap()
        self.msk = d("msk", [1, 1], F32, kind="ExternalInput").ap()
        self.wk = d("wk", [C, C], F32, kind="ExternalInput").ap()
        self.wv = d("wv", [C, C], F32, kind="ExternalInput").ap()
        self.wr = d("wr", [C, C], F32, kind="ExternalInput").ap()
        self.wo = d("wo", [C, C], F32, kind="ExternalInput").ap()
        self.w1 = d("w1", [C, FF], F32, kind="ExternalInput").ap()
        self.w2 = d("w2", [FF, C], F32, kind="ExternalInput").ap()
        self.tmkd = d("tmkd", [128, NCH], F32, kind="ExternalInput").ap()
        self.tmvd = d("tmvd", [128, NCH], F32, kind="ExternalInput").ap()
        self.tmrd = d("tmrd", [128, NCH], F32, kind="ExternalInput").ap()
        self.tdd = d("tdd", [128, NCH], F32, kind="ExternalInput").ap()
        self.tfd = d("tfd", [128, NCH], F32, kind="ExternalInput").ap()
        self.b1d = d("b1d", [128, NFF], F32, kind="ExternalInput").ap()
        self.b2d = d("b2d", [128, NCH], F32, kind="ExternalInput").ap()
        self.outT = d("outT", [C, TLOC], F32, kind="ExternalOutput").ap()
        self.x1d = d("x1d", [C, TLOC], F32,
                     kind="ExternalOutput" if DEBUG_X1 else "Internal").ap()
        self.xnd = (d("xnd", [C, T + 1], F32, kind="ExternalOutput").ap()
                    if DEBUG_XN else None)
        self.ydbg = (d("ydbg", [C, T], F32, kind="ExternalOutput").ap()
                     if DEBUG_Y else None)
        self.kdbg = (d("kdbg", [C, T], F32, kind="ExternalOutput").ap()
                     if DEBUG_Y else None)
        self.rkdbg = (d("rkdbg", [C, T], F32, kind="ExternalOutput").ap()
                      if DEBUG_Y else None)
        self.sgdbg = (d("sgdbg", [C, T], F32, kind="ExternalOutput").ap()
                      if DEBUG_Y else None)
        self.w1dbg = (d("w1dbg", [C, FF], BF16, kind="ExternalOutput").ap()
                      if DEBUG_W else None)
        self.w2dbg = (d("w2dbg", [FF, C], BF16, kind="ExternalOutput").ap()
                      if DEBUG_W else None)

        self.qs = Sem(nc.alloc_semaphore("qs"))
        self.qg = Sem(nc.alloc_semaphore("qg"))
        self.qwc = Sem(nc.alloc_semaphore("qwc"))
        self.qx = [Sem(nc.alloc_semaphore("qx0")), Sem(nc.alloc_semaphore("qx1"))]
        self.qw = [Sem(nc.alloc_semaphore("qw0")), Sem(nc.alloc_semaphore("qw1"))]
        self.qt = [Sem(nc.alloc_semaphore("qt%d" % i)) for i in range(6)]
        self.pe = Sem(nc.alloc_semaphore("pe"))
        self.act = Sem(nc.alloc_semaphore("act"))
        self.dve = Sem(nc.alloc_semaphore("dve"))
        self.pl = Sem(nc.alloc_semaphore("pl"))

        self._alloc_sbuf()
        p = nc.alloc_psum_tensor
        self.psA = p("psA", [128, T], F32)
        self.psB = p("psB", [128, T], F32)
        self.psC = p("psC", [128, T], F32)
        self.psD = p("psD", [128, T], F32)
        self.psE = p("psE", [128, T], F32)
        self.psF = p("psF", [128, T], F32)
        self.psG = p("psG", [128, T], F32)
        self.psH = p("psH", [128, T], F32)

        self.rot3 = [None, None, None]
        self.gidx = 0
        self.stage_guard = [None] * 6
        self.tern_slot = 0
        self.xsg = {0: [], 1: []}
        self.x1wr_slot = {}
        self.gh_guard = None
        self.xn_guard = []
        self.mix_guard = None
        self.xnlast_src = None
        self.rb_guard = None

    def _alloc_sbuf(self):
        a = Arena(self.nc, 207 * 1024)
        self.ar = a

        def al(name, shape, dtype, at=None):
            h, off = a.alloc(name, shape, dtype, at)
            setattr(self, name, h)
            return off
        al("tmk", [128, NCH], F32); al("tmv", [128, NCH], F32)
        al("tmr", [128, NCH], F32); al("tdr", [128, NCH], F32)
        al("tfr", [128, NCH], F32); al("alpha", [128, NCH], F32)
        al("eu", [128, NCH], F32); al("b1v", [128, NFF], F32)
        al("b2v", [128, NCH], F32); al("onesb", [128, 1], BF16)
        al("onesf", [1, 128], F32); al("onescol", [128, 1], F32)
        al("msksb", [1, 1], F32); al("mkcol", [128, 1], F32)
        al("mcol", [128, 1], F32); al("rcol", [128, 1], F32)
        al("thr1", [128, 1], F32); al("nthr1", [128, 1], F32)
        al("thr2", [128, 1], F32); al("nthr2", [128, 1], F32)
        al("trow", [1, 1], F32); al("cN", [128, NCH], F32)
        al("cD", [128, NCH], F32)
        al("xpcP", [128, NCH, 1], F32); al("xpcM", [128, NCH, 1], F32)
        al("xpcB", [128, NCH, 1], BF16)
        al("colsA", [128, 64], F32); al("colsB", [128, 64], F32)
        al("ttmp", [128, T], BF16); al("rscr", [128, T], F32)
        al("colred", [128, 1], F32); al("tiny", [1, 8], F32)
        al("xnlast", [128, NCH, 1], BF16)
        al("rows", [1, 4, T], F32); al("mrrow", [1, T], F32)
        al("Rb", [128, T], F32); al("MRb", [128, T], F32)
        al("stg", [128, 6, T], F32)
        wbase = a.cur
        al("wkb", [128, NCH, C], BF16); al("wvb", [128, NCH, C], BF16)
        al("wrb", [128, NCH, C], BF16); al("wob", [128, NCH, C], BF16)
        al("w1tb", [128, NCH, FF], BF16, at=wbase)
        sbase = a.cur
        al("xs", [128, NCH, 2, T], F32)
        al("xn", [128, NCH, T + 1], BF16)
        xyoff = al("xkb", [128, NCH, T], BF16)
        al("xvb", [128, NCH, T], BF16)
        al("xrb", [128, NCH, T], BF16)
        al("yt", [128, NCH, T], BF16)
        al("arot", [128, 2, T], F32); al("avrot", [128, 2, T], F32)
        al("N1", [128, T + 1], F32); al("D1", [128, T + 1], F32)
        al("xbr", [128, 4, T], BF16); al("xsqr", [128, 4, T], BF16)
        al("dd", [128, T], BF16); al("numt", [128, T], F32)
        al("dent", [128, T], F32)
        al("sg", [128, 2, T], F32); al("x1c", [128, 2, T], F32)
        a_end = a.cur
        al("rwkvb", [128, NCH, T], BF16, at=xyoff)
        al("w2tb", [128, NFF, C], BF16, at=sbase)
        o = sbase + NFF * C * 2
        o = al("x1s", [128, NCH, 2, U], F32, at=o) + NCH * 2 * U * 4
        o = al("xn2b", [128, NCH, U], BF16, at=o) + NCH * U * 2
        o = al("hgb", [128, NFF, U], BF16, at=o) + NFF * U * 2
        o = al("xb2", [128, 2, U], BF16, at=o) + 2 * U * 2
        o = al("xsq2", [128, 2, U], BF16, at=o) + 2 * U * 2
        o = al("xn2f", [128, U], F32, at=o) + U * 4
        o = al("outc", [128, 2, U], F32, at=o) + 2 * U * 4
        assert max(a_end, o) <= 207 * 1024, (a_end, o)

    def emit(self, e, fn):
        self.prog[e].append(fn)

    def wait(self, e, s, val=None):
        v = s.n if val is None else val
        h = s.h
        if v > 0:
            self.emit(e, lambda eng, h=h, v=v: eng.wait_ge(h, v))

    def waitg(self, e, g):
        if g:
            self.wait(e, g[0], g[1])

    def inc(self, e, fn, s, k=1):
        s.n += k
        h = s.h
        self.emit(e, lambda eng, fn=fn, h=h, k=k: fn(eng).then_inc(h, k))
        return s.n

    def op(self, e, fn):
        self.emit(e, lambda eng, fn=fn: fn(eng))

    def dma(self, e, out, in_, s, slow=False):
        if slow:
            return self.inc(e, lambda eng, o=out, i=in_: eng.dma_start(
                out=o, in_=i, allow_slow_non_contiguous=True), s, 16)
        return self.inc(e, lambda eng, o=out, i=in_: eng.dma_start(out=o, in_=i), s, 16)

    def mmgroup(self, ps_ap, pairs):
        n = len(pairs)
        for i, (lhsT, rhs) in enumerate(pairs):
            st, sp = (i == 0), (i == n - 1)
            if sp:
                self.inc("tensor", lambda eng, o=ps_ap, l=lhsT, r=rhs, st=st, sp=sp:
                         eng.matmul(o, l, r, start=st, stop=sp), self.pe)
            else:
                self.op("tensor", lambda eng, o=ps_ap, l=lhsT, r=rhs, st=st, sp=sp:
                        eng.matmul(o, l, r, start=st, stop=sp))
        return self.pe.n

    def init_phase(self):
        qs, qg = self.qs, self.qg
        for dst, src in ((self.tmk, self.tmkd), (self.tmv, self.tmvd),
                         (self.tmr, self.tmrd), (self.tdr, self.tdd),
                         (self.tfr, self.tfd), (self.b1v, self.b1d),
                         (self.b2v, self.b2d), (self.msksb, self.msk)):
            self.dma("sync", dst[:, :], src, qs)
        self.dma("sync", self.xpcP[:, :, :],
                 self.xp[:, 0:1].rearrange("(g p) t -> p g t", p=128), qs, slow=True)
        self.dma("sync", self.xpcM[:, :, :],
                 self.xm[:, 0:1].rearrange("(g p) t -> p g t", p=128), qs, slow=True)
        self.vec_loads = qs.n
        for wdst, wsrc in ((self.wkb, self.wk), (self.wvb, self.wv),
                           (self.wrb, self.wr), (self.wob, self.wo)):
            for ci in range(NCH):
                self.dma("gpsimd", wdst[:, ci, :], wsrc[ci * 128:(ci + 1) * 128, :], self.qwc)
        self.w_loaded = self.qwc.n
        self.op("vector", lambda e: e.memset(self.onesb[:, :], 1.0))
        self.op("vector", lambda e: e.memset(self.onesf[:, :], 1.0))
        self.op("vector", lambda e: e.memset(self.onescol[:, :], 1.0))
        self.op("vector", lambda e: e.memset(self.cN[:, :], 0.0))
        self.inc("vector", lambda e: e.memset(self.cD[:, :], 0.0), self.dve)
        self.memset_done = self.dve.n
        self.wait("scalar", qs, self.vec_loads)
        self.op("scalar", lambda e: e.activation(out=self.alpha[:, :], in_=self.tdr[:, :], func=AF.Exp))
        self.op("scalar", lambda e: e.activation(out=self.alpha[:, :], in_=self.alpha[:, :], func=AF.Exp, scale=-1.0))
        self.inc("scalar", lambda e: e.activation(out=self.eu[:, :], in_=self.tfr[:, :], func=AF.Exp), self.act)
        self.wait("tensor", self.dve, self.memset_done)
        self.wait("tensor", qs, self.vec_loads)
        mk_mm = self.mmgroup(self.psE[:, 0:1], [(self.onesf[:, :], self.msksb[:, :])])
        self.wait("scalar", self.pe, mk_mm)
        self.inc("scalar", lambda e: e.activation(out=self.mkcol[:, :], in_=self.psE[:, 0:1], func=AF.Copy), self.act)
        self.mk_done = self.act.n

    def tiny_stats(self, xpc, wait_q=None):
        if wait_q is not None:
            self.wait("scalar", self.qs, wait_q)
        self.inc("scalar", lambda e: e.activation(out=self.xpcB[:, :, :], in_=xpc[:, :, :], func=AF.Copy), self.act)
        self.wait("tensor", self.act, self.act.n)
        mmx = self.mmgroup(self.psF[0:1, 0:1],
                           [(self.onesb[:, :], self.xpcB[:, j, :]) for j in range(NCH)])
        self.wait("scalar", self.pe, mmx)
        self.inc("scalar", lambda e: e.activation(out=self.xpcB[:, :, :], in_=xpc[:, :, :], func=AF.Square), self.act)
        self.wait("tensor", self.act, self.act.n)
        self.mmgroup(self.psD[0:1, 0:1],
                     [(self.onesb[:, :], self.xpcB[:, j, :]) for j in range(NCH)])
        self.wait("vector", self.pe, self.pe.n)
        ti = self.tiny
        self.op("vector", lambda e: e.tensor_single_scalar(out=ti[:, 0:1], in_=self.psF[0:1, 0:1], scalar=1.0 / C, op=ALU.mult))
        self.op("vector", lambda e: e.tensor_single_scalar(out=ti[:, 1:2], in_=self.psD[0:1, 0:1], scalar=1.0 / C, op=ALU.mult))
        self.op("vector", lambda e: e.tensor_tensor(out=ti[:, 2:3], in0=ti[:, 0:1], in1=ti[:, 0:1], op=ALU.mult))
        self.op("vector", lambda e: e.tensor_tensor(out=ti[:, 1:2], in0=ti[:, 1:2], in1=ti[:, 2:3], op=ALU.subtract))
        self.inc("vector", lambda e: e.tensor_single_scalar(out=ti[:, 1:2], in_=ti[:, 1:2], scalar=EPS, op=ALU.add), self.dve)
        self.wait("scalar", self.dve, self.dve.n)
        self.op("scalar", lambda e: e.activation(out=ti[:, 2:3], in_=ti[:, 1:2], func=AF.Ln))
        self.inc("scalar", lambda e: e.activation(out=ti[:, 3:4], in_=ti[:, 2:3], func=AF.Exp, scale=-0.5), self.act)
        self.wait("tensor", self.act, self.act.n)
        self.mmgroup(self.psF[:, 0:1], [(self.onesf[:, :], ti[:, 0:1])])
        mm3 = self.mmgroup(self.psD[:, 0:1], [(self.onesf[:, :], ti[:, 3:4])])
        self.wait("scalar", self.pe, mm3)
        self.op("scalar", lambda e: e.activation(out=self.mcol[:, :], in_=self.psF[:, 0:1], func=AF.Copy))
        self.inc("scalar", lambda e: e.activation(out=self.rcol[:, :], in_=self.psD[:, 0:1], func=AF.Copy), self.act)
        self.wait("vector", self.act, self.act.n)
        for j in range(NCH):
            fn = lambda e, j=j: e.scalar_tensor_tensor(
                out=self.xn[:, j, 0:1], in0=xpc[:, j, :], scalar=self.mcol[:, 0:1],
                in1=self.rcol[:, 0:1], op0=ALU.subtract, op1=ALU.mult)
            if j == NCH - 1:
                self.inc("vector", fn, self.dve)
            else:
                self.op("vector", fn)

    def strip(self, s, src, prefix):
        buf = s % 2
        c0 = 1 + s * T
        qs = self.qs
        for g in self.xsg[buf]:
            self.waitg("sync", g)
        xin = src.rearrange("(g p) t -> p g t", p=128)[:, :, c0:c0 + T]
        xl = self.dma("sync", self.xs[:, :, buf, :], xin, self.qx[buf])
        self.wait("scalar", self.qx[buf], xl)
        stats_mm = []
        for j in range(NCH):
            if j >= 4:
                self.wait("scalar", self.pe, stats_mm[j - 4])
            self.op("scalar", lambda e, j=j, buf=buf: e.activation(
                out=self.xbr[:, j % 4, :], in_=self.xs[:, j, buf, :], func=AF.Copy))
            self.inc("scalar", lambda e, j=j, buf=buf: e.activation(
                out=self.xsqr[:, j % 4, :], in_=self.xs[:, j, buf, :], func=AF.Square), self.act)
            self.wait("tensor", self.act, self.act.n)
            if j == 0:
                self.waitg("tensor", self.gh_guard)
            stG, spG = (j == 0), (j == NCH - 1)
            self.op("tensor", lambda e, j=j, st=stG, sp=spG: e.matmul(
                self.psG[0:1, :], self.onesb[:, :], self.xbr[:, j % 4, :], start=st, stop=sp))
            self.inc("tensor", lambda e, j=j, st=stG, sp=spG: e.matmul(
                self.psH[0:1, :], self.onesb[:, :], self.xsqr[:, j % 4, :], start=st, stop=sp), self.pe)
            stats_mm.append(self.pe.n)
        stats_done = self.pe.n
        self.wait("vector", self.pe, stats_done)
        rw = self.rows
        self.op("vector", lambda e: e.tensor_single_scalar(out=rw[:, 0, :], in_=self.psG[0:1, :], scalar=1.0 / C, op=ALU.mult))
        self.op("vector", lambda e: e.tensor_single_scalar(out=rw[:, 1, :], in_=self.psH[0:1, :], scalar=1.0 / C, op=ALU.mult))
        self.op("vector", lambda e: e.tensor_tensor(out=rw[:, 2, :], in0=rw[:, 0, :], in1=rw[:, 0, :], op=ALU.mult))
        self.op("vector", lambda e: e.tensor_tensor(out=rw[:, 1, :], in0=rw[:, 1, :], in1=rw[:, 2, :], op=ALU.subtract))
        self.inc("vector", lambda e: e.tensor_single_scalar(out=rw[:, 1, :], in_=rw[:, 1, :], scalar=EPS, op=ALU.add), self.dve)
        self.wait("scalar", self.dve, self.dve.n)
        self.op("scalar", lambda e: e.activation(out=rw[:, 2, :], in_=rw[:, 1, :], func=AF.Ln))
        self.inc("scalar", lambda e: e.activation(out=rw[:, 3, :], in_=rw[:, 2, :], func=AF.Exp, scale=-0.5), self.act)
        self.wait("vector", self.act, self.act.n)
        self.inc("vector", lambda e: e.tensor_tensor(out=self.mrrow[:, :], in0=rw[:, 0, :], in1=rw[:, 3, :], op=ALU.mult), self.dve)
        self.wait("tensor", self.dve, self.dve.n)
        self.mmgroup(self.psG[:, :], [(self.onesf[:, :], rw[:, 3, :])])
        bc_done = self.mmgroup(self.psH[:, :], [(self.onesf[:, :], self.mrrow[:, :])])
        self.wait("scalar", self.pe, bc_done)
        if self.rb_guard is not None:
            self.waitg("scalar", self.rb_guard)
        self.op("scalar", lambda e: e.activation(out=self.Rb[:, :], in_=self.psG[:, :], func=AF.Copy))
        self.inc("scalar", lambda e: e.activation(out=self.MRb[:, :], in_=self.psH[:, :], func=AF.Copy), self.act)
        self.gh_guard = (self.act, self.act.n)
        self.wait("vector", self.act, self.act.n)
        if s > 0:
            self.op("vector", lambda e: e.tensor_copy(out=self.xn[:, :, 0:1], in_=self.xnlast[:, :, :]))
        self.waitg("vector", self.mix_guard)
        for j in range(NCH):
            self.op("vector", lambda e, j=j, buf=buf: e.tensor_tensor(
                out=self.xn[:, j, 1:T + 1], in0=self.xs[:, j, buf, :], in1=self.Rb[:, :], op=ALU.mult))
            self.op("vector", lambda e, j=j: e.tensor_tensor(
                out=self.xn[:, j, 1:T + 1], in0=self.xn[:, j, 1:T + 1], in1=self.MRb[:, :], op=ALU.subtract))
        self.inc("vector", lambda e: e.engine_nop(), self.dve)
        xn_done = self.dve.n
        self.rb_guard = (self.dve, xn_done)
        for j in range(NCH):
            self.op("vector", lambda e, j=j: e.tensor_tensor(
                out=self.dd[:, :], in0=self.xn[:, j, 1:T + 1], in1=self.xn[:, j, 0:T], op=ALU.subtract))
            self.op("vector", lambda e, j=j: e.scalar_tensor_tensor(
                out=self.xkb[:, j, :], in0=self.dd[:, :], scalar=self.tmk[:, j:j + 1],
                in1=self.xn[:, j, 0:T], op0=ALU.mult, op1=ALU.add))
            fnv = lambda e, j=j: e.scalar_tensor_tensor(
                out=self.xvb[:, j, :], in0=self.dd[:, :], scalar=self.tmv[:, j:j + 1],
                in1=self.xn[:, j, 0:T], op0=ALU.mult, op1=ALU.add)
            if prefix:
                self.inc("vector", fnv, self.dve)
            else:
                self.op("vector", fnv)
                self.inc("vector", lambda e, j=j: e.scalar_tensor_tensor(
                    out=self.xrb[:, j, :], in0=self.dd[:, :], scalar=self.tmr[:, j:j + 1],
                    in1=self.xn[:, j, 0:T], op0=ALU.mult, op1=ALU.add), self.dve)
        self.inc("vector", lambda e: e.tensor_copy(
            out=self.xnlast[:, :, :], in_=self.xn[:, :, T:T + 1]), self.dve)
        mixes_done = self.dve.n
        if DEBUG_XN and s == 0 and not prefix:
            self.wait("sync", self.dve, mixes_done)
            dbgwr = self.dma("sync", self.xnd.rearrange("(g p) t -> p g t", p=128),
                             self.xn[:, :, :], self.qs)
            self.wait("vector", self.qs, dbgwr)
        self.xnlast_src = (self.dve, mixes_done)
        self.wait("tensor", self.dve, mixes_done)
        if s == 0 and prefix:
            self.wait("tensor", self.qwc, self.w_loaded)
        rotps = [self.psA, self.psB, self.psC]
        dn_cnt = {}
        for dj in range(NCH):
            bank = self.gidx % 3
            self.waitg("tensor", self.rot3[bank])
            kmm = self.mmgroup(rotps[bank][:, :],
                               [(self.wkb[:, ci, dj * 128:(dj + 1) * 128], self.xkb[:, ci, :])
                                for ci in range(NCH)])
            self.wait("scalar", self.pe, kmm)
            if dj >= 2:
                self.wait("scalar", self.dve, dn_cnt[dj - 2])
            self.inc("scalar", lambda e, b=bank, dj=dj: e.activation(
                out=self.arot[:, dj % 2, :], in_=rotps[b][:, :], func=AF.Exp), self.act)
            a_ev = self.act.n
            self.rot3[bank] = (self.act, a_ev)
            self.gidx += 1
            if DEBUG_Y and s == 0 and not prefix:
                self.wait("sync", self.act, a_ev)
                kw_ = self.dma("sync", self.kdbg[dj * 128:(dj + 1) * 128, :],
                               self.arot[:, dj % 2, :], self.qs)
                self.wait("scalar", self.qs, kw_)
            bank = self.gidx % 3
            self.waitg("tensor", self.rot3[bank])
            vmm = self.mmgroup(rotps[bank][:, :],
                               [(self.wvb[:, ci, dj * 128:(dj + 1) * 128], self.xvb[:, ci, :])
                                for ci in range(NCH)])
            self.wait("vector", self.pe, vmm)
            self.wait("vector", self.act, a_ev)
            self.inc("vector", lambda e, b=bank, dj=dj: e.tensor_tensor(
                out=self.avrot[:, dj % 2, :], in0=rotps[b][:, :], in1=self.arot[:, dj % 2, :], op=ALU.mult), self.dve)
            self.rot3[bank] = (self.dve, self.dve.n)
            self.gidx += 1
            self.op("vector", lambda e, dj=dj: e.tensor_copy(out=self.N1[:, 0:1], in_=self.cN[:, dj:dj + 1]))
            self.op("vector", lambda e, dj=dj: e.tensor_copy(out=self.D1[:, 0:1], in_=self.cD[:, dj:dj + 1]))
            self.op("vector", lambda e, dj=dj: e.tensor_tensor_scan(
                out=self.N1[:, 1:T + 1], data0=self.alpha[:, dj:dj + 1].broadcast_to([128, T]),
                data1=self.avrot[:, dj % 2, :], initial=self.cN[:, dj:dj + 1], op0=ALU.mult, op1=ALU.add))
            self.op("vector", lambda e, dj=dj: e.tensor_tensor_scan(
                out=self.D1[:, 1:T + 1], data0=self.alpha[:, dj:dj + 1].broadcast_to([128, T]),
                data1=self.arot[:, dj % 2, :], initial=self.cD[:, dj:dj + 1], op0=ALU.mult, op1=ALU.add))
            self.op("vector", lambda e, dj=dj: e.tensor_copy(out=self.cN[:, dj:dj + 1], in_=self.N1[:, T:T + 1]))
            self.op("vector", lambda e, dj=dj: e.tensor_copy(out=self.cD[:, dj:dj + 1], in_=self.D1[:, T:T + 1]))
            if not prefix:
                self.op("vector", lambda e, dj=dj: e.scalar_tensor_tensor(
                    out=self.numt[:, :], in0=self.avrot[:, dj % 2, :], scalar=self.eu[:, dj:dj + 1],
                    in1=self.N1[:, 0:T], op0=ALU.mult, op1=ALU.add))
                self.op("vector", lambda e, dj=dj: e.scalar_tensor_tensor(
                    out=self.dent[:, :], in0=self.arot[:, dj % 2, :], scalar=self.eu[:, dj:dj + 1],
                    in1=self.D1[:, 0:T], op0=ALU.mult, op1=ALU.add))
                self.op("vector", lambda e: e.reciprocal_approx_accurate(
                    out=self.dent[:, :], in_=self.dent[:, :], scratch=self.rscr[:, :]))
                self.inc("vector", lambda e, dj=dj: e.tensor_tensor(
                    out=self.yt[:, dj, :], in0=self.numt[:, :], in1=self.dent[:, :], op=ALU.mult), self.dve)
            else:
                self.inc("vector", lambda e: e.engine_nop(), self.dve)
            dn_cnt[dj] = self.dve.n
        if DEBUG_Y and s == 0 and not prefix:
            self.wait("sync", self.dve, dn_cnt[NCH - 1])
            yw = self.dma("sync", self.ydbg.rearrange("(g p) t -> p g t", p=128),
                          self.yt[:, :, :], self.qs)
            self.wait("gpsimd", self.qs, yw)
        if prefix:
            self.xsg[buf] = [(self.dve, dn_cnt[NCH - 1])]
            return
        rwkv_cnt = {}
        for dj in range(NCH):
            bank = self.gidx % 3
            self.waitg("tensor", self.rot3[bank])
            rmm = self.mmgroup(rotps[bank][:, :],
                               [(self.wrb[:, ci, dj * 128:(dj + 1) * 128], self.xrb[:, ci, :])
                                for ci in range(NCH)])
            self.wait("scalar", self.pe, rmm)
            if dj >= 2:
                self.wait("scalar", self.dve, rwkv_cnt[dj - 2])
            self.inc("scalar", lambda e, b=bank, dj=dj: e.activation(
                out=self.sg[:, dj % 2, :], in_=rotps[b][:, :], func=AF.Sigmoid), self.act)
            self.rot3[bank] = (self.act, self.act.n)
            self.gidx += 1
            self.wait("vector", self.act, self.act.n)
            self.inc("vector", lambda e, dj=dj: e.tensor_tensor(
                out=self.rwkvb[:, dj, :], in0=self.sg[:, dj % 2, :], in1=self.yt[:, dj, :], op=ALU.mult), self.dve)
            rwkv_cnt[dj] = self.dve.n
        x1_evac_last = 0
        for dj in range(NCH):
            bank = self.gidx % 3
            self.waitg("tensor", self.rot3[bank])
            if dj == 0:
                self.wait("tensor", self.dve, rwkv_cnt[NCH - 1])
            womm = self.mmgroup(rotps[bank][:, :],
                                [(self.wob[:, ci, dj * 128:(dj + 1) * 128], self.rwkvb[:, ci, :])
                                 for ci in range(NCH)])
            self.wait("vector", self.pe, womm)
            slot = dj % 2
            if slot in self.x1wr_slot:
                self.wait("vector", self.qw[slot], self.x1wr_slot[slot])
            self.inc("vector", lambda e, b=bank, dj=dj, buf=buf: e.tensor_tensor(
                out=self.x1c[:, dj % 2, :], in0=rotps[b][:, :], in1=self.xs[:, dj, buf, :], op=ALU.add), self.dve)
            x1_evac_last = self.dve.n
            self.rot3[bank] = (self.dve, self.dve.n)
            self.gidx += 1
            self.wait("sync", self.dve, x1_evac_last)
            self.x1wr_slot[slot] = self.dma(
                "sync", self.x1d[dj * 128:(dj + 1) * 128, s * T:(s + 1) * T],
                self.x1c[:, dj % 2, :], self.qw[slot])
        self.xsg[buf] = [(self.dve, x1_evac_last)]
        self.mix_guard = (self.pe, self.pe.n)

    def tern_pass1(self, which, chunks):
        cols = self.colsA if which == 1 else self.colsB
        for src, col in chunks:
            slot = self.tern_slot
            self.tern_slot = (self.tern_slot + 1) % 6
            self.waitg("gpsimd", self.stage_guard[slot])
            ld = self.dma("gpsimd", self.stg[:, slot, :], src, self.qt[slot])
            self.wait("scalar", self.qt[slot], ld)
            self.inc("scalar", lambda e, slot=slot, cols=cols, col=col: e.activation(
                out=self.ttmp[:, :], in_=self.stg[:, slot, :], func=AF.Abs,
                accum_out=cols[:, col:col + 1]), self.act)
            self.stage_guard[slot] = (self.act, self.act.n)

    def tern_scale(self, which, nelem):
        cols = self.colsA if which == 1 else self.colsB
        thr = self.thr1 if which == 1 else self.thr2
        nthr = self.nthr1 if which == 1 else self.nthr2
        self.wait("vector", self.act, self.act.n)
        self.inc("vector", lambda e, cols=cols: e.tensor_reduce(
            out=self.colred[:, :], in_=cols[:, :], axis=AX.X, op=ALU.add), self.dve)
        self.wait("tensor", self.dve, self.dve.n)
        self.wait("tensor", self.act, self.act.n)
        tot = self.mmgroup(self.psE[0:1, 0:1], [(self.colred[:, :], self.onescol[:, :])])
        self.wait("scalar", self.pe, tot)
        self.inc("scalar", lambda e, n=nelem: e.activation(
            out=self.trow[:, :], in_=self.psE[0:1, 0:1], func=AF.Copy, scale=1.0 / n), self.act)
        self.wait("vector", self.act, self.act.n)
        self.inc("vector", lambda e: e.tensor_single_scalar(
            out=self.trow[:, :], in_=self.trow[:, :], scalar=1e-8, op=ALU.max), self.dve)
        self.wait("tensor", self.dve, self.dve.n)
        bc = self.mmgroup(self.psE[:, 0:1], [(self.onesf[:, :], self.trow[:, :])])
        self.wait("scalar", self.pe, bc)
        self.op("scalar", lambda e, thr=thr: e.activation(out=thr[:, :], in_=self.psE[:, 0:1], func=AF.Copy, scale=0.5))
        self.inc("scalar", lambda e, nthr=nthr: e.activation(out=nthr[:, :], in_=self.psE[:, 0:1], func=AF.Copy, scale=-0.5), self.act)
        if which == 1:
            self.thr1_done = self.act.n
        else:
            self.thr2_done = self.act.n

    def tern_pass2(self, w1p2, w2p2, NV1=64):
        w1done = [None] * len(w1p2)
        w2done = [None] * len(w2p2)
        dv = ([(1, i, s, d) for i, (s, d) in enumerate(w1p2[:NV1])]
              + [(2, i, s, d) for i, (s, d) in enumerate(w2p2[:NV1])])
        pl = ([(1, NV1 + i, s, d) for i, (s, d) in enumerate(w1p2[NV1:])]
              + [(2, NV1 + i, s, d) for i, (s, d) in enumerate(w2p2[NV1:])])
        thr = {1: self.thr1, 2: self.thr2}
        nthr = {1: self.nthr1, 2: self.nthr2}
        thrd = {1: self.thr1_done, 2: self.thr2_done}
        DS = (0, 1, 2, 3, 4, 5)
        PS = (4, 5)

        def emit_dv(k):
            w, idx, src, dst = dv[k]
            slot = DS[k % 6]
            self.waitg("gpsimd", self.stage_guard[slot])
            ld = self.dma("gpsimd", self.stg[:, slot, :], src, self.qt[slot])
            self.wait("vector", self.qt[slot], ld)
            self.wait("vector", self.act, thrd[w])
            self.op("vector", lambda e, slot=slot, n=nthr[w], dst=dst: e.tensor_single_scalar(
                out=dst, in_=self.stg[:, slot, :], scalar=n[:, 0:1], op=ALU.is_lt))
            self.inc("vector", lambda e, slot=slot, t=thr[w], dst=dst: e.scalar_tensor_tensor(
                out=dst, in0=self.stg[:, slot, :], scalar=t[:, 0:1],
                in1=dst, op0=ALU.is_gt, op1=ALU.subtract), self.dve)
            self.stage_guard[slot] = (self.dve, self.dve.n)
            (w1done if w == 1 else w2done)[idx] = (self.dve, self.dve.n)

        for di in range(len(dv)):
            emit_dv(di)
        assert not pl
        return w1done, w2done

    def phase_b(self, w1p2, w2p2):
        qs = self.qs
        for s in (self.pe, self.act, self.dve, self.pl, self.qg, self.qs,
                  self.qwc, self.qx[0], self.qx[1], self.qw[0], self.qw[1],
                  self.qt[0], self.qt[1], self.qt[2]):
            self.wait("sync", s, s.n)
        barrier = [(self.qw[0], self.qw[0].n), (self.qw[1], self.qw[1].n),
                   (self.dve, self.dve.n)]
        for e in ("scalar", "tensor", "vector", "gpsimd"):
            for g in barrier:
                self.waitg(e, g)
        w1done, w2done = self.tern_pass2(w1p2, w2p2)
        if DEBUG_W:
            self.wait("sync", self.dve, self.dve.n)
            for ci in range(NCH):
                self.dma("sync", self.w1dbg[ci * 128:(ci + 1) * 128, :], self.w1tb[:, ci, :], self.qs)
            for fi in range(NFF):
                self.dma("sync", self.w2dbg[fi * 128:(fi + 1) * 128, :], self.w2tb[:, fi, :], self.qs)
            self.wait("tensor", self.qs, self.qs.n)
        gh_guard = None
        rot2 = [None, None]
        outbank = [None] * 4
        outwr_slot = {}
        x1_guard = {}
        hg_guard = None
        for u in range(NU):
            buf = u % 2
            if buf in x1_guard:
                self.waitg("sync", x1_guard[buf])
            x1in = self.x1d.rearrange("(g p) t -> p g t", p=128)[:, :, u * U:(u + 1) * U]
            xl = self.dma("sync", self.x1s[:, :, buf, :], x1in, self.qx[buf])
            self.wait("scalar", self.qx[buf], xl)
            stats_mm = []
            for j in range(NCH):
                if j >= 2:
                    self.wait("scalar", self.pe, stats_mm[j - 2])
                self.op("scalar", lambda e, j=j, buf=buf: e.activation(
                    out=self.xb2[:, j % 2, :], in_=self.x1s[:, j, buf, :], func=AF.Copy))
                self.inc("scalar", lambda e, j=j, buf=buf: e.activation(
                    out=self.xsq2[:, j % 2, :], in_=self.x1s[:, j, buf, :], func=AF.Square), self.act)
                self.wait("tensor", self.act, self.act.n)
                if j == 0 and gh_guard is not None:
                    self.waitg("tensor", gh_guard)
                stG, spG = (j == 0), (j == NCH - 1)
                self.op("tensor", lambda e, j=j, st=stG, sp=spG: e.matmul(
                    self.psG[0:1, 0:U], self.onesb[:, :], self.xb2[:, j % 2, :], start=st, stop=sp))
                self.inc("tensor", lambda e, j=j, st=stG, sp=spG: e.matmul(
                    self.psH[0:1, 0:U], self.onesb[:, :], self.xsq2[:, j % 2, :], start=st, stop=sp), self.pe)
                stats_mm.append(self.pe.n)
            self.wait("vector", self.pe, self.pe.n)
            rw = self.rows
            self.op("vector", lambda e: e.tensor_single_scalar(out=rw[:, 0, 0:U], in_=self.psG[0:1, 0:U], scalar=1.0 / C, op=ALU.mult))
            self.op("vector", lambda e: e.tensor_single_scalar(out=rw[:, 1, 0:U], in_=self.psH[0:1, 0:U], scalar=1.0 / C, op=ALU.mult))
            self.op("vector", lambda e: e.tensor_tensor(out=rw[:, 2, 0:U], in0=rw[:, 0, 0:U], in1=rw[:, 0, 0:U], op=ALU.mult))
            self.op("vector", lambda e: e.tensor_tensor(out=rw[:, 1, 0:U], in0=rw[:, 1, 0:U], in1=rw[:, 2, 0:U], op=ALU.subtract))
            self.inc("vector", lambda e: e.tensor_single_scalar(out=rw[:, 1, 0:U], in_=rw[:, 1, 0:U], scalar=EPS, op=ALU.add), self.dve)
            self.wait("scalar", self.dve, self.dve.n)
            self.op("scalar", lambda e: e.activation(out=rw[:, 2, 0:U], in_=rw[:, 1, 0:U], func=AF.Ln))
            self.inc("scalar", lambda e: e.activation(out=rw[:, 3, 0:U], in_=rw[:, 2, 0:U], func=AF.Exp, scale=-0.5), self.act)
            self.wait("vector", self.act, self.act.n)
            self.inc("vector", lambda e: e.tensor_tensor(out=self.mrrow[:, 0:U], in0=rw[:, 0, 0:U], in1=rw[:, 3, 0:U], op=ALU.mult), self.dve)
            self.wait("tensor", self.dve, self.dve.n)
            self.mmgroup(self.psG[:, 0:U], [(self.onesf[:, :], rw[:, 3, 0:U])])
            bc_done = self.mmgroup(self.psH[:, 0:U], [(self.onesf[:, :], self.mrrow[:, 0:U])])
            self.wait("scalar", self.pe, bc_done)
            if self.rb_guard is not None:
                self.waitg("scalar", self.rb_guard)
            self.op("scalar", lambda e: e.activation(out=self.Rb[:, 0:U], in_=self.psG[:, 0:U], func=AF.Copy))
            self.inc("scalar", lambda e: e.activation(out=self.MRb[:, 0:U], in_=self.psH[:, 0:U], func=AF.Copy), self.act)
            gh_guard = (self.act, self.act.n)
            self.wait("vector", self.act, self.act.n)
            if hg_guard is not None:
                self.waitg("vector", hg_guard)
            for j in range(NCH):
                self.op("vector", lambda e, j=j, buf=buf: e.tensor_tensor(
                    out=self.xn2f[:, :], in0=self.x1s[:, j, buf, :], in1=self.Rb[:, 0:U], op=ALU.mult))
                self.op("vector", lambda e, j=j: e.tensor_tensor(
                    out=self.xn2b[:, j, :], in0=self.xn2f[:, :], in1=self.MRb[:, 0:U], op=ALU.subtract))
            self.inc("vector", lambda e: e.engine_nop(), self.dve)
            xn2_done = self.dve.n
            self.rb_guard = (self.dve, xn2_done)
            self.wait("tensor", self.dve, xn2_done)
            rotps = [self.psA, self.psB]
            for fj in range(NFF):
                bank = fj % 2
                if u == 0:
                    self.waitg("tensor", w1done[(fj // 4) * 8 + 7])
                if rot2[bank] is not None:
                    self.waitg("tensor", rot2[bank])
                hmm = self.mmgroup(rotps[bank][:, 0:U],
                                   [(self.w1tb[:, ci, fj * 128:(fj + 1) * 128], self.xn2b[:, ci, :])
                                    for ci in range(NCH)])
                self.wait("scalar", self.pe, hmm)
                if fj == 0 and hg_guard is not None:
                    self.waitg("scalar", hg_guard)
                self.inc("scalar", lambda e, b=bank, fj=fj: e.activation(
                    out=self.hgb[:, fj, :], in_=rotps[b][:, 0:U], func=AF.Gelu,
                    bias=self.b1v[:, fj:fj + 1]), self.act)
                rot2[bank] = (self.act, self.act.n)
            gelu_all = self.act.n
            outps = [self.psC, self.psD, self.psE, self.psF]
            self.wait("tensor", self.act, gelu_all)
            for w in range(2):
                for q in range(4):
                    dj = 4 * w + q
                    if outbank[q] is not None:
                        self.waitg("tensor", outbank[q])
                    if u == 0 and w == 0 and q == 0:
                        pairs = []
                        for fi in range(NFF):
                            self.waitg("tensor", w2done[fi * 2 + 1])
                            st_, sp_ = (fi == 0), (fi == NFF - 1)
                            self.inc("tensor", lambda e, fi=fi, st_=st_, sp_=sp_, dj=dj, q=q:
                                     e.matmul(outps[q][:, 0:U], self.w2tb[:, fi, dj * 128:(dj + 1) * 128],
                                              self.hgb[:, fi, :], start=st_, stop=sp_), self.pe)
                        omm = self.pe.n
                    else:
                        omm = self.mmgroup(outps[q][:, 0:U],
                                           [(self.w2tb[:, fi, dj * 128:(dj + 1) * 128], self.hgb[:, fi, :])
                                            for fi in range(NFF)])
                    self.wait("vector", self.pe, omm)
                    slot = dj % 2
                    if slot in outwr_slot:
                        self.wait("vector", self.qw[slot], outwr_slot[slot])
                    self.inc("vector", lambda e, q=q, dj=dj, buf=buf: e.scalar_tensor_tensor(
                        out=self.outc[:, dj % 2, :], in0=outps[q][:, 0:U], scalar=self.b2v[:, dj:dj + 1],
                        in1=self.x1s[:, dj, buf, :], op0=ALU.add, op1=ALU.add), self.dve)
                    outbank[q] = (self.dve, self.dve.n)
                    self.wait("sync", self.dve, self.dve.n)
                    outwr_slot[slot] = self.dma(
                        "sync", self.outT[dj * 128:(dj + 1) * 128, u * U:(u + 1) * U],
                        self.outc[:, dj % 2, :], self.qw[slot])
            hg_guard = (self.pe, self.pe.n)
            x1_guard[buf] = (self.dve, self.dve.n)
        for s_ in (self.qs, self.qg, self.qwc, self.qx[0], self.qx[1],
                   self.qw[0], self.qw[1]) + tuple(self.qt):
            self.wait("sync", s_, s_.n)

    def build(self):
        self.init_phase()
        w1p1 = [(self.w1[ci * 128:(ci + 1) * 128, fo * T:(fo + 1) * T], ci * (FF // T) + fo)
                for ci in range(NCH) for fo in range(FF // T)]
        w2p1 = [(self.w2[fi * 128:(fi + 1) * 128, do * T:(do + 1) * T], fi * (C // T) + do)
                for fi in range(NFF) for do in range(C // T)]
        w1p2 = [(self.w1[ci * 128:(ci + 1) * 128, fo * T:(fo + 1) * T],
                 self.w1tb[:, ci, fo * T:(fo + 1) * T])
                for fo in range(FF // T) for ci in range(NCH)]
        w2p2 = [(self.w2[fi * 128:(fi + 1) * 128, do * T:(do + 1) * T],
                 self.w2tb[:, fi, do * T:(do + 1) * T])
                for fi in range(NFF) for do in range(C // T)]

        self.tiny_stats(self.xpcP, wait_q=self.vec_loads)
        for s in range(NS):
            self.strip(s, self.xp, True)
            self.tern_pass1(1, w1p1[s * 16:(s + 1) * 16])
            self.tern_pass1(2, w2p1[s * 16:(s + 1) * 16])
        self.tern_scale(1, C * FF)
        self.tern_scale(2, C * FF)
        self.wait("vector", self.act, self.mk_done)
        self.op("vector", lambda e: e.tensor_single_scalar(
            out=self.cN[:, :], in_=self.cN[:, :], scalar=self.mkcol[:, 0:1], op=ALU.mult))
        self.inc("vector", lambda e: e.tensor_single_scalar(
            out=self.cD[:, :], in_=self.cD[:, :], scalar=self.mkcol[:, 0:1], op=ALU.mult), self.dve)
        self.mix_guard = (self.pe, self.pe.n)
        self.tiny_stats(self.xpcM)
        for s in range(NS):
            self.strip(s, self.xm, False)
        self.phase_b(w1p2, w2p2)

        nc = self.nc
        with nc.Block() as blk:
            for name, meth in (("sync", blk.sync), ("gpsimd", blk.gpsimd),
                               ("scalar", blk.scalar), ("vector", blk.vector),
                               ("tensor", blk.tensor)):
                thunks = self.prog[name]
                meth(lambda eng, thunks=thunks: [f(eng) for f in thunks])
        return nc


_NC_CACHE = {}


def _get_nc():
    if "nc" not in _NC_CACHE:
        nc = Build().build()
        lower_extended_insts(nc)
        _NC_CACHE["nc"] = nc
    return _NC_CACHE["nc"]


def kernel(L0_attn_x, L0_attn_time_decay, L0_attn_time_first, L0_attn_time_mix_k,
           L0_attn_time_mix_v, L0_attn_time_mix_r, L0_attn_Wk, L0_attn_Wv,
           L0_attn_Wr, L0_attn_Wo, L0_ffn_W1, L0_ffn_b1, L0_ffn_W2, L0_ffn_b2):
    x = np.ascontiguousarray(np.asarray(L0_attn_x, dtype=np.float32))
    assert x.shape == (4, 4096, 1024)
    ac = np.ascontiguousarray

    def rv(v, n):
        return ac(np.asarray(v, np.float32).reshape(n, 128).T)

    shared = dict(
        wk=ac(np.asarray(L0_attn_Wk, np.float32).T),
        wv=ac(np.asarray(L0_attn_Wv, np.float32).T),
        wr=ac(np.asarray(L0_attn_Wr, np.float32).T),
        wo=ac(np.asarray(L0_attn_Wo, np.float32).T),
        w1=ac(np.asarray(L0_ffn_W1, np.float32).T),
        w2=ac(np.asarray(L0_ffn_W2, np.float32).T),
        tmkd=rv(L0_attn_time_mix_k, 8), tmvd=rv(L0_attn_time_mix_v, 8),
        tmrd=rv(L0_attn_time_mix_r, 8), tdd=rv(L0_attn_time_decay, 8),
        tfd=rv(L0_attn_time_first, 8), b1d=rv(L0_ffn_b1, 32), b2d=rv(L0_ffn_b2, 8))
    in_maps = []
    for c in range(8):
        b, h = c // 2, c % 2
        xmT = np.empty((1024, TLOC + 1), np.float32)
        xmT[:, 0] = 0.0 if h == 0 else x[b, TLOC - 1, :]
        xmT[:, 1:] = x[b, h * TLOC:(h + 1) * TLOC, :].T
        xpT = np.empty((1024, TLOC + 1), np.float32)
        xpT[:, 0] = 0.0
        xpT[:, 1:] = x[b, 0:TLOC, :].T
        m = dict(shared)
        m["xm"] = ac(xmT)
        m["xp"] = ac(xpT)
        m["msk"] = np.full((1, 1), float(h), np.float32)
        in_maps.append(m)

    nc = _get_nc()
    res = run_bass_kernel_spmd(nc, in_maps, core_ids=list(range(8)))
    out = np.empty((4, 4096, 1024), np.float32)
    for c in range(8):
        b, h = c // 2, c % 2
        out[b, h * TLOC:(h + 1) * TLOC, :] = np.asarray(res.results[c]["outT"]).T
    return out
